# version 08
# baseline (speedup 1.0000x reference)
import contextlib
import numpy as np
import concourse.bass as bass
import concourse.mybir as mybir
from concourse.bass_utils import run_bass_kernel_spmd

F32 = mybir.dt.float32
F32R = mybir.dt.float32r
BF16 = mybir.dt.bfloat16
AF = mybir.ActivationFunctionType
ALU = mybir.AluOpType
AX = mybir.AxisListType

NCORES = 8
D = 1024
T = 16384
TC = T // NCORES
KC = D // 128
RG = [list(range(NCORES))]

NS_SCHED = [
    (50.392372131347656, -20565.65625, 2108726.5),
    (4.029094219207764, -10.593995094299316, 7.42457914352417),
    (4.029093265533447, -10.593989372253418, 7.424572944641113),
    (4.029092788696289, -10.593984603881836, 7.424567222595215),
    (4.0290937423706055, -10.593992233276367, 7.424575328826904),
    (4.029094219207764, -10.593997955322266, 7.424582004547119),
    (4.029094696044922, -10.594000816345215, 7.424585819244385),
    (4.029094696044922, -10.593998908996582, 7.4245829582214355),
    (4.029094219207764, -10.59399700164795, 7.424581050872803),
    (4.029094696044922, -10.593999862670898, 7.42458438873291),
    (3.9041237831115723, -9.692113876342773, 6.666903018951416),
    (3.1152637004852295, -5.149963855743408, 3.0012638568878174),
    (2.136584997177124, -1.7934309244155884, 0.6653010249137878),
    (1.5, -0.5, 0.0),
]
LN_EPS = 1e-5
DBG_L1FFN = False
DBG_NT1 = False
DBG_L1ATTN = False
DBG_L0ATTN = False
DBG_NS = False
DBG_NS_IT = 2
WCACHE1 = True
WCACHE3 = True
PP_ON = True
PP_LIMIT = None
PP_SMALL3 = False



class LB:

    def __init__(self):
        self.nc = bass.Bass("TRN2", target_bir_lowering=False)
        self.stack = contextlib.ExitStack()
        self.blk = None
        self._sems = {}
        self.sem_prefix = ""
        self._scoped = []
        self._scoped_ps = []
        self._zones = []

    def _q(self, name):
        if name.startswith("@"):
            return name[1:]
        return self.sem_prefix + name

    def qual(self, name):
        return "@" + self._q(name)

    def push_zone(self):
        self._zones.append((len(self._scoped), len(self._scoped_ps)))

    def pop_zone(self):
        base, base_ps = self._zones.pop()
        while len(self._scoped) > base:
            nm, cm = self._scoped.pop()
            cm.__exit__(None, None, None)
        while len(self._scoped_ps) > base_ps:
            nm, cm = self._scoped_ps.pop()
            cm.__exit__(None, None, None)

    def inp(self, name, shape, dt=F32):
        return self.nc.dram_tensor(name, shape, dt, kind="ExternalInput").ap()

    def out(self, name, shape, dt=F32):
        return self.nc.dram_tensor(name, shape, dt, kind="ExternalOutput").ap()

    def dram(self, name, shape, dt=F32, addr_space="Local"):
        return self.nc.dram_tensor(self.sem_prefix + name, shape, dt,
                                   kind="Internal",
                                   addr_space=addr_space).ap()

    def sb(self, name, shape, dt):
        name = self.sem_prefix + name
        if self._zones:
            return self.sb_scoped_raw(name, shape, dt)
        return self.stack.enter_context(self.nc.sbuf_tensor(name, shape, dt))

    def sb_scoped_raw(self, name, shape, dt):
        cm = self.nc.sbuf_tensor(name, shape, dt)
        h = cm.__enter__()
        self._scoped.append((name, cm))
        return h

    def sb_scoped(self, name, shape, dt):
        return self.sb_scoped_raw(self.sem_prefix + name, shape, dt)

    def free_sb(self, name):
        name = self.sem_prefix + name
        nm, cm = self._scoped.pop()
        assert nm == name, f"free_sb out of order: {nm} != {name}"
        cm.__exit__(None, None, None)

    def ps(self, name, shape, dt=F32):
        name = self.sem_prefix + name
        if self._zones:
            cm = self.nc.psum_tensor(name, shape, dt)
            h = cm.__enter__()
            self._scoped_ps.append((name, cm))
            return h
        return self.stack.enter_context(self.nc.psum_tensor(name, shape, dt))

    def start_block(self, sem_names, allow_lp=False):
        for s in sem_names:
            self.sem(s)
        if allow_lp:
            self.stack.enter_context(self.nc.allow_low_precision(
                "bf16 intermediate storage; validated against fp32 reference "
                "within the measured tolerance"))
        self.blk = self.stack.enter_context(self.nc.Block())

    def sem(self, name):
        name = self._q(name)
        if name not in self._sems:
            h = self.stack.enter_context(self.nc.semaphore(name))
            self._sems[name] = [h, 0]
        return self._sems[name][0]

    def bump(self, name, n=1):
        h = self.sem(name)
        self._sems[self._q(name)][1] += n
        return h, n

    def mark(self, name):
        self.sem(name)
        return self._sems[self._q(name)][1]

    def wait(self, eng, name, target):
        if target > 0:
            eng.wait_ge(self.sem(name), target)

    def ss(self, eng, sem_name, ins):
        ins.then_inc(*self.bump(sem_name))
        eng.wait_ge(self.sem(sem_name), self.mark(sem_name))

    def t(self, fn):
        self.blk.tensor(fn)

    def v(self, fn):
        self.blk.vector(fn)

    def s(self, fn):
        self.blk.scalar(fn)

    def g(self, fn):
        self.blk.gpsimd(fn)

    def y(self, fn):
        self.blk.sync(fn)

    def done(self):
        while self._scoped:
            nm, cm = self._scoped.pop()
            cm.__exit__(None, None, None)
        while getattr(self, "_scoped_ps", None):
            nm, cm = self._scoped_ps.pop()
            cm.__exit__(None, None, None)
        self.stack.close()
        return self.nc


class Banks:

    def __init__(self, lb, tiles):
        self.lb = lb
        self.tiles = tiles
        self.next = 0
        self.guard = [None] * len(tiles)

    def acquire(self, teng):
        i = self.next
        self.next = (self.next + 1) % len(self.tiles)
        if self.guard[i] is not None:
            self.lb.wait(teng, *self.guard[i])
            self.guard[i] = None
        return i, self.tiles[i]

    def set_guard(self, i, sem_name, target):
        self.guard[i] = (sem_name, target)


class WBuf:

    def __init__(self, lb, tiles, name):
        self.lb = lb
        self.tiles = tiles
        self.n = len(tiles)
        self.name = name
        self.n_loads = 0
        self.loads_per_slot = [0] * self.n
        self.free_mark = [None] * self.n
        self.load_target = []

    def _ssem(self, s):
        return f"{self.name}_s{s}"

    def sems(self):
        return [self._ssem(i) for i in range(self.n)]

    def emit_load(self, g, hbm_ap, kchunks=None):
        i = self.n_loads
        s = i % self.n
        if self.free_mark[s] is not None:
            for dep in self.free_mark[s]:
                self.lb.wait(g, *dep)
            self.free_mark[s] = None
        dst = self.tiles[s][:] if kchunks is None else \
            self.tiles[s][:, 0:kchunks, :]
        g.dma_start(dst, hbm_ap).then_inc(
            *self.lb.bump(self._ssem(s), 16))
        self.loads_per_slot[s] += 1
        self.load_target.append((self._ssem(s), 16 * self.loads_per_slot[s]))
        self.n_loads += 1

    def consume(self, teng, i):
        self.lb.wait(teng, *self.load_target[i])
        return self.tiles[i % self.n]

    def release_at(self, i, sem_name, target):
        s = i % self.n
        if self.free_mark[s] is None:
            self.free_mark[s] = []
        self.free_mark[s].append((sem_name, target))


def ceil_div(a, b):
    return (a + b - 1) // b


def get_inp(lb, name, shape, dt=F32):
    cache = getattr(lb, "_inp_cache", None)
    if cache is None:
        cache = lb._inp_cache = {}
    if name not in cache:
        cache[name] = lb.inp(name, shape, dt)
    return cache[name]


def rearr_pcn(hbm_2d, col0, ncols):
    return hbm_2d[:, col0:col0 + ncols].rearrange("(c p) n -> p c n", p=128)


def rearr_pct(hbm_2d, t0, nt):
    return hbm_2d[:, t0:t0 + nt].rearrange("(c p) t -> p c t", p=128)



def emit_ln(lb, banks, aux, z_sb, out_sb, z_dep, nt=512, nchunks=KC,
            out_bf=None, out_bf_off=0, out_extra_deps=(), z_off=0):
    zsq, statrow = aux["zsq"], aux["statrow"]
    onescol, onesrow = aux["onescol"], aux["onesrow"]
    m = {}

    def s_squares(s):
        lb.wait(s, *z_dep)
        for c in range(nchunks):
            ins = s.activation(zsq[:, c, 0:nt], z_sb[:, c, z_off:z_off + nt], AF.Square)
        ins.then_inc(*lb.bump("sq"))
        m["sq"] = lb.mark("sq")
    lb.s(s_squares)

    holders = {}

    def t_sums(t):
        lb.wait(t, *z_dep)
        i1, b1 = banks.acquire(t)
        for c in range(nchunks):
            ins = t.matmul(b1[0:1, 0:nt], onescol[:], z_sb[:, c, z_off:z_off + nt],
                           start=(c == 0), stop=(c == nchunks - 1))
        ins.then_inc(*lb.bump("mm"))
        m["sums"] = lb.mark("mm")
        lb.wait(t, "sq", m["sq"])
        i2, b2 = banks.acquire(t)
        for c in range(nchunks):
            ins = t.matmul(b2[0:1, 0:nt], onescol[:], zsq[:, c, 0:nt],
                           start=(c == 0), stop=(c == nchunks - 1))
        ins.then_inc(*lb.bump("mm"))
        m["sumsq"] = lb.mark("mm")
        holders.update(i1=i1, b1=b1, i2=i2, b2=b2)
    lb.t(t_sums)

    def v_rows(v):
        lb.wait(v, "mm", m["sums"])
        v.tensor_scalar_mul(statrow[0:1, 0:nt], holders["b1"][0:1, 0:nt],
                            1.0 / D)
        lb.wait(v, "mm", m["sumsq"])
        ins = v.tensor_scalar_mul(statrow[0:1, nt:2 * nt],
                                  holders["b2"][0:1, 0:nt], 1.0 / D)
        ins.then_inc(*lb.bump("evv"))
        m["rows"] = lb.mark("evv")
    lb.v(v_rows)
    banks.set_guard(holders["i1"], "evv", m["rows"])
    banks.set_guard(holders["i2"], "evv", m["rows"])

    def s_m2(s):
        lb.wait(s, "evv", m["rows"])
        ins = s.activation(statrow[0:1, 2 * nt:3 * nt], statrow[0:1, 0:nt],
                           AF.Square)
        ins.then_inc(*lb.bump("sq"))
        m["m2"] = lb.mark("sq")
    lb.s(s_m2)

    def v_var(v):
        lb.wait(v, "sq", m["m2"])
        lb.ss(v, "evv", v.tensor_sub(statrow[0:1, nt:2 * nt],
                                     statrow[0:1, nt:2 * nt],
                                     statrow[0:1, 2 * nt:3 * nt]))
        ins = v.tensor_scalar_add(statrow[0:1, nt:2 * nt],
                                  statrow[0:1, nt:2 * nt], LN_EPS)
        ins.then_inc(*lb.bump("evv"))
        m["var"] = lb.mark("evv")
    lb.v(v_var)

    def s_sd(s):
        lb.wait(s, "evv", m["var"])
        ins = s.activation(statrow[0:1, nt:2 * nt], statrow[0:1, nt:2 * nt],
                           AF.Sqrt)
        ins.then_inc(*lb.bump("sq"))
        m["sd"] = lb.mark("sq")
    lb.s(s_sd)

    def v_rstd(v):
        lb.wait(v, "sq", m["sd"])
        lb.ss(v, "evv", v.reciprocal(statrow[0:1, nt:2 * nt],
                                     statrow[0:1, nt:2 * nt]))
        ins = v.tensor_mul(statrow[0:1, 0:nt], statrow[0:1, 0:nt],
                           statrow[0:1, nt:2 * nt])
        ins.then_inc(*lb.bump("evv"))
        m["mrb"] = lb.mark("evv")
    lb.v(v_rstd)

    def t_bcast(t):
        lb.wait(t, "evv", m["mrb"])
        i3, b3 = banks.acquire(t)
        t.matmul(b3[:, 0:nt], onesrow[:], statrow[0:1, nt:2 * nt],
                 start=True, stop=True).then_inc(*lb.bump("mm"))
        m["rb"] = lb.mark("mm")
        i4, b4 = banks.acquire(t)
        t.matmul(b4[:, 0:nt], onesrow[:], statrow[0:1, 0:nt],
                 start=True, stop=True).then_inc(*lb.bump("mm"))
        m["mb"] = lb.mark("mm")
        holders.update(i3=i3, b3=b3, i4=i4, b4=b4)
    lb.t(t_bcast)

    def v_final(v):
        lb.wait(v, "mm", m["mb"])
        for dep in out_extra_deps:
            lb.wait(v, *dep)
        for c in range(nchunks):
            lb.ss(v, "evv",
                  v.tensor_mul(out_sb[:, c, 0:nt],
                               z_sb[:, c, z_off:z_off + nt],
                               holders["b3"][:, 0:nt]))
            ins = v.tensor_sub(out_sb[:, c, 0:nt], out_sb[:, c, 0:nt],
                               holders["b4"][:, 0:nt])
            if out_bf is not None:
                lb.ss(v, "evv", ins)
                ins = v.tensor_copy(
                    out_bf[:, c, out_bf_off:out_bf_off + nt],
                    out_sb[:, c, 0:nt])
        ins.then_inc(*lb.bump("evv"))
        m["out"] = lb.mark("evv")
    lb.v(v_final)
    banks.set_guard(holders["i3"], "evv", m["out"])
    banks.set_guard(holders["i4"], "evv", m["out"])
    return ("evv", m["out"])



def build_l0ffn():
    lb = LB()
    xin_h = lb.inp("xinT", [D, TC])
    ocol_h = lb.inp("onescol", [128, 1])
    orow_h = lb.inp("onesrow", [1, 128])
    xout_h = lb.out("xoutT", [D, TC])
    lb.start_block(["mm", "evv", "evs", "sq", "xld", "cld", "rld",
                    "st", "wk_s0", "wk_s1", "wk_s2", "wk_s3", "shared"])
    onescol = lb.sb("sh_onescol", [128, 1], F32)
    onesrow = lb.sb("sh_onesrow", [1, 128], F32)

    def y_shc(sy):
        sy.dma_start(onescol[:, :], ocol_h[:]).then_inc(*lb.bump("@shared", 16))
        sy.dma_start(onesrow[:, :], orow_h[:]).then_inc(*lb.bump("@shared", 16))
    lb.y(y_shc)
    io = {"xin": xin_h, "xout": xout_h, "onescol": onescol,
          "onesrow": onesrow, "shc_dep": ("@shared", lb.mark("@shared")),
          "stream_dep": None}
    dep = emit_l0ffn(lb, io)

    def y_final(sy):
        lb.wait(sy, *dep)
    lb.y(y_final)
    return lb.done()


def emit_l0ffn(lb, io):
    xin_h = io["xin"]
    xout_h = io["xout"]
    onescol = io["onescol"]
    onesrow = io["onesrow"]
    w1T_h = get_inp(lb, "w1T", [D, 4096])
    w2T_h = get_inp(lb, "w2T", [D, 4096])
    h1T_h = get_inp(lb, "h1T", [4096, 4096])
    h2T_h = get_inp(lb, "h2T", [4096, 4096])
    w3T_h = get_inp(lb, "w3T", [4096, D])
    s1_h = lb.inp("s1", [128, 32])
    s2_h = lb.inp("s2", [128, 32])
    s3_h = lb.inp("s3", [128, 32])
    s4_h = lb.inp("s4", [128, 32])
    s5_h = lb.inp("s5", [128, 8])

    TT = 512
    NT = TC // TT
    NSLOT = 4

    xbf = lb.sb("xbf", [128, KC, TT], BF16)
    A = lb.sb("A", [128, 32, TT], BF16)
    B = lb.sb("Bf", [128, 32, TT], BF16)
    wk = [lb.sb(f"wk{i}", [128, 32, 128], BF16) for i in range(NSLOT)]
    tmp1 = lb.sb("tmp1", [128, 2, TT], F32)
    resid = lb.sb("resid", [128, KC, TT], F32)
    ytile = lb.sb("ytile", [128, KC, TT], F32)
    yout = lb.sb("yout", [128, KC, TT], F32)
    s1s = lb.sb("s1s", [128, 32], F32)
    s2s = lb.sb("s2s", [128, 32], F32)
    s3s = lb.sb("s3s", [128, 32], F32)
    s4s = lb.sb("s4s", [128, 32], F32)
    s5s = lb.sb("s5s", [128, 8], F32)
    zsq = lb.sb("zsq", [128, KC, TT], F32)
    statrow = lb.sb("statrow", [1, 3 * TT], F32)

    pb = [lb.ps(f"pb{i}", [128, 512]) for i in range(8)]

    banks = Banks(lb, pb)
    wbuf = WBuf(lb, wk, "wk")
    aux = {"onescol": onescol, "onesrow": onesrow, "zsq": zsq,
           "statrow": statrow}

    def y_const(sy):
        lb.wait(sy, *io["shc_dep"])
        for dst, src in [(s1s, s1_h), (s2s, s2_h), (s3s, s3_h), (s4s, s4_h),
                         (s5s, s5_h)]:
            sy.dma_start(dst[:], src[:]).then_inc(*lb.bump("cld", 16))
    lb.y(y_const)
    CLD = lb.mark("cld")

    wcd = io.get("wc1")
    unique = []
    for j in range(32):
        unique.append((rearr_pcn(w1T_h, 128 * j, 128), KC))
        unique.append((rearr_pcn(w2T_h, 128 * j, 128), KC))
    for j in range(32):
        unique.append((rearr_pcn(h1T_h, 128 * j, 128), 32))
    for j in range(32):
        unique.append((rearr_pcn(h2T_h, 128 * j, 128), 32))
    for j in range(8):
        unique.append((rearr_pcn(w3T_h, 128 * j, 128), 32))
    WPT = 136

    def cache_ap(idx, kch):
        if kch == KC:
            return wcd[idx][:, 0:1024].rearrange("p (c m) -> p c m", m=128)
        return wcd[idx][:].rearrange("p (c m) -> p c m", m=128)

    if wcd is None:
        plan = unique * NT
    else:
        plan = list(unique)
        for t in range(1, NT):
            for idx, (_, kch) in enumerate(unique):
                plan.append((cache_ap(idx, kch), kch))
    pumped = [0]
    store_total = {}

    def pump(n):
        n = min(n, len(plan) - pumped[0])
        if n <= 0:
            return

        def g_loads(g):
            for _ in range(n):
                idx = pumped[0]
                ap, kch = plan[idx]
                if wcd is not None and idx == WPT:
                    for s2 in range(NSLOT):
                        lb.wait(g, f"wkc{s2}", store_total[s2])
                wbuf.emit_load(g, ap, kch)
                if wcd is not None and idx < WPT:
                    s2 = (wbuf.n_loads - 1) % NSLOT
                    tgt = wbuf.load_target[-1]
                    cap = cache_ap(idx, kch)

                    def y_st(sy, s2=s2, tgt=tgt, cap=cap, kch=kch):
                        lb.wait(sy, *tgt)
                        sy.dma_start(cap, wk[s2][:, 0:kch, :]
                                     ).then_inc(*lb.bump(f"wkc{s2}", 16))
                    lb.y(y_st)
                    store_total[s2] = lb.mark(f"wkc{s2}")
                    wbuf.release_at(wbuf.n_loads - 1, f"wkc{s2}",
                                    store_total[s2])
                pumped[0] += 1
        lb.g(g_loads)

    def pump_x(t):
        def g_x(g):
            if t > 0:
                lb.wait(g, "mm", xfree_mark[t - 1])
            elif io["stream_dep"] is not None:
                lb.wait(g, *io["stream_dep"])
            g.dma_start(xbf[:], rearr_pct(xin_h, TT * t, TT)
                        ).then_inc(*lb.bump("xld", 16))
        lb.g(g_x)

    xfree_mark = {}
    rfree_mark = {}
    pump_x(0)
    pump(NSLOT)

    for t in range(NT):
        wi0 = t * WPT
        cons_mark = {}

        for j in range(32):
            holders = {}

            def t_s1(te, t=t, j=j, holders=holders):
                if j == 0:
                    lb.wait(te, "cld", CLD)
                    lb.wait(te, "xld", 16 * (t + 1))
                w1t = wbuf.consume(te, wi0 + 2 * j)
                bi, bt = banks.acquire(te)
                for c in range(KC):
                    ins = te.matmul(bt[:], w1t[:, c, :], xbf[:, c, :],
                                    start=(c == 0), stop=(c == KC - 1))
                ins.then_inc(*lb.bump("mm"))
                holders["m1"] = lb.mark("mm")
                wbuf.release_at(wi0 + 2 * j, "mm", holders["m1"])
                holders["b1"] = (bi, bt)
                w2t = wbuf.consume(te, wi0 + 2 * j + 1)
                bi2, bt2 = banks.acquire(te)
                for c in range(KC):
                    ins = te.matmul(bt2[:], w2t[:, c, :], xbf[:, c, :],
                                    start=(c == 0), stop=(c == KC - 1))
                ins.then_inc(*lb.bump("mm"))
                holders["m2"] = lb.mark("mm")
                wbuf.release_at(wi0 + 2 * j + 1, "mm", holders["m2"])
                if j == 31:
                    xfree_mark[t] = holders["m2"]
                holders["b2"] = (bi2, bt2)
            lb.t(t_s1)

            def s_silu(se, j=j, holders=holders):
                if j == 0:
                    lb.wait(se, "cld", CLD)
                lb.wait(se, "mm", holders["m1"])
                if j >= 2:
                    lb.wait(se, "evv", cons_mark[j - 2])
                ins = se.activation(tmp1[:, j % 2, :], holders["b1"][1][:],
                                    AF.Silu, scale=s1s[:, j:j + 1])
                ins.then_inc(*lb.bump("evs"))
                holders["silu"] = lb.mark("evs")
                banks.set_guard(holders["b1"][0], "evs", holders["silu"])
            lb.s(s_silu)

            def v_h(ve, j=j, holders=holders):
                if j == 0:
                    lb.wait(ve, "cld", CLD)
                lb.wait(ve, "mm", holders["m2"])
                lb.wait(ve, "evs", holders["silu"])
                ins = ve.scalar_tensor_tensor(
                    A[:, j, :], holders["b2"][1][:], s2s[:, j:j + 1],
                    tmp1[:, j % 2, :], ALU.mult, ALU.mult)
                ins.then_inc(*lb.bump("evv"))
                cons_mark[j] = lb.mark("evv")
                banks.set_guard(holders["b2"][0], "evv", cons_mark[j])
            lb.v(v_h)
            pump(2)
            if t < NT - 1 and j == 0:
                pass
        h_done = ("evv", lb.mark("evv"))

        def hidden_stage(src, dst, scales, src_dep, wbase):
            for j in range(32):
                holders = {}

                def t_h(te, j=j, holders=holders, src=src, src_dep=src_dep,
                        wbase=wbase):
                    if j == 0:
                        lb.wait(te, *src_dep)
                    ht = wbuf.consume(te, wbase + j)
                    bi, bt = banks.acquire(te)
                    for c in range(32):
                        ins = te.matmul(bt[:], ht[:, c, :], src[:, c, :],
                                        start=(c == 0), stop=(c == 31))
                    ins.then_inc(*lb.bump("mm"))
                    holders["m"] = lb.mark("mm")
                    wbuf.release_at(wbase + j, "mm", holders["m"])
                    holders["b"] = (bi, bt)
                lb.t(t_h)

                def s_g(se, j=j, holders=holders, dst=dst, scales=scales):
                    lb.wait(se, "mm", holders["m"])
                    ins = se.activation(dst[:, j, :], holders["b"][1][:],
                                        AF.Gelu, scale=scales[:, j:j + 1])
                    ins.then_inc(*lb.bump("evs"))
                    banks.set_guard(holders["b"][0], "evs", lb.mark("evs"))
                lb.s(s_g)
                pump(1)
            return ("evs", lb.mark("evs"))

        g1_done = hidden_stage(A, B, s3s, h_done, wi0 + 64)
        g2_done = hidden_stage(B, A, s4s, g1_done, wi0 + 96)

        def y_resid(sy, t=t):
            if t > 0:
                lb.wait(sy, "evv", rfree_mark[t - 1])
            elif io["stream_dep"] is not None:
                lb.wait(sy, *io["stream_dep"])
            sy.dma_start(resid[:], rearr_pct(xin_h, TT * t, TT)
                         ).then_inc(*lb.bump("rld", 16))
        lb.y(y_resid)
        rld_mark = lb.mark("rld")

        for j in range(KC):
            holders = {}

            def t_w3(te, t=t, j=j, holders=holders, g2_done=g2_done):
                if j == 0:
                    lb.wait(te, *g2_done)
                wt = wbuf.consume(te, wi0 + 128 + j)
                bi, bt = banks.acquire(te)
                for c in range(32):
                    ins = te.matmul(bt[:], wt[:, c, :], A[:, c, :],
                                    start=(c == 0), stop=(c == 31))
                ins.then_inc(*lb.bump("mm"))
                holders["m"] = lb.mark("mm")
                wbuf.release_at(wi0 + 128 + j, "mm", holders["m"])
                holders["b"] = (bi, bt)
            lb.t(t_w3)

            def v_y(ve, j=j, holders=holders, rld_mark=rld_mark):
                if j == 0:
                    lb.wait(ve, "rld", rld_mark)
                lb.wait(ve, "mm", holders["m"])
                ins = ve.scalar_tensor_tensor(
                    ytile[:, j, :], holders["b"][1][:], s5s[:, j:j + 1],
                    resid[:, j, :], ALU.mult, ALU.add)
                ins.then_inc(*lb.bump("evv"))
                banks.set_guard(holders["b"][0], "evv", lb.mark("evv"))
            lb.v(v_y)
            pump(1)
        rfree_mark[t] = lb.mark("evv")
        z_dep = ("evv", lb.mark("evv"))

        ln_dep = emit_ln(lb, banks, aux, ytile, yout, z_dep, nt=TT,
                         out_extra_deps=[("st", 16 * t)] if t > 0 else ())

        def y_store(sy, t=t, ln_dep=ln_dep):
            lb.wait(sy, *ln_dep)
            sy.dma_start(rearr_pct(xout_h, TT * t, TT), yout[:]
                         ).then_inc(*lb.bump("st", 16))
        lb.y(y_store)
        if t == 0:
            pump_x(1)
        elif t < NT - 1:
            pump_x(t + 1)

    return (lb.qual("st"), lb.mark("st"))


def host_l0ffn_inputs(inputs, xT_slices):
    w1T = np.ascontiguousarray(inputs["L0_ffn_w1_w"].T).astype(np.float32)
    w2T = np.ascontiguousarray(inputs["L0_ffn_w2_w"].T).astype(np.float32)
    h1T = np.ascontiguousarray(inputs["L0_ffn_h1_w"].T).astype(np.float32)
    h2T = np.ascontiguousarray(inputs["L0_ffn_h2_w"].T).astype(np.float32)
    w3T = np.ascontiguousarray(inputs["L0_ffn_w3_w"].T).astype(np.float32)
    s1 = np.ascontiguousarray(inputs["L0_ffn_w1_s"].reshape(32, 128).T)
    s2 = np.ascontiguousarray(inputs["L0_ffn_w2_s"].reshape(32, 128).T)
    s3 = np.ascontiguousarray(inputs["L0_ffn_h1_s"].reshape(32, 128).T)
    s4 = np.ascontiguousarray(inputs["L0_ffn_h2_s"].reshape(32, 128).T)
    s5 = np.ascontiguousarray(inputs["L0_ffn_w3_s"].reshape(8, 128).T)
    common = dict(w1T=w1T, w2T=w2T, h1T=h1T, h2T=h2T, w3T=w3T,
                  s1=s1, s2=s2, s3=s3, s4=s4, s5=s5,
                  onescol=np.ones((128, 1), np.float32),
                  onesrow=np.ones((1, 128), np.float32))
    if xT_slices is None:
        return [dict(common) for c in range(NCORES)]
    return [dict(common, xinT=np.ascontiguousarray(xT_slices[c]))
            for c in range(NCORES)]




def build_l1ffn():
    lb = LB()
    xin_h = lb.inp("xinT", [D, TC])
    ocol_h = lb.inp("onescol", [128, 1])
    orow_h = lb.inp("onesrow", [1, 128])
    id128_h = lb.inp("ident128", [128, 128])
    xout_h = lb.out("xoutT", [D, TC])
    lb.start_block(["shared"])
    onescol = lb.sb("sh_onescol", [128, 1], F32)
    onesrow = lb.sb("sh_onesrow", [1, 128], F32)
    id128 = lb.sb("sh_id128", [128, 128], F32)

    def y_shc(sy):
        for dst, srcc in [(onescol[:, :], ocol_h[:]),
                          (onesrow[:, :], orow_h[:]),
                          (id128[:, :], id128_h[:])]:
            sy.dma_start(dst, srcc).then_inc(*lb.bump("@shared", 16))
    lb.y(y_shc)
    io = {"xin": xin_h, "xout": xout_h, "onescol": onescol,
          "onesrow": onesrow, "id128": id128,
          "shc_dep": ("@shared", lb.mark("@shared")), "stream_dep": None}
    dep = emit_l1ffn(lb, io)

    def y_final(sy):
        lb.wait(sy, *dep)
    lb.y(y_final)
    return lb.done()


def emit_l1ffn(lb, io):
    xin_h = io["xin"]
    xout_h = io["xout"]
    onescol = io["onescol"]
    onesrow = io["onesrow"]
    id128 = io["id128"]
    ws1_h = get_inp(lb, "ws1", [D, D])
    ws2_h = get_inp(lb, "ws2", [D, D])
    we1_h = get_inp(lb, "We1", [31, D, D])
    we2_h = get_inp(lb, "We2", [31, D, D])
    wr_h = lb.inp("Wr", [D, 31])
    brbc_h = lb.inp("br_bc", [128, 31])
    bs1_h = lb.inp("bs1pc", [128, 8])
    bs2_h = lb.inp("bs2pc", [128, 8])
    be1_h = lb.inp("be1pc", [128, 31 * 8])
    be2_h = lb.inp("be2r", [31, D])
    if DBG_L1FFN:
        dbg_gates_h = lb.out("dbg_gates", [128, 8, 31])
        dbg_gtmp_h = lb.out("dbg_gtmp", [31, 1024])
        dbg_rinit_h = lb.out("dbg_rinit", [D, 1024])
        dbg_shared_h = lb.out("dbg_shared", [D, 1024])
        dbg_e0_h = lb.out("dbg_e0", [D, 1024])
        dbg_h0_h = lb.out("dbg_h0", [D, 1024])
        dbg_gbc_h = lb.out("dbg_gbc", [128, 1024])
        dbg_grow_h = lb.out("dbg_grow", [1, 1024])

    TT = 1024
    NT = 1 if DBG_NT1 else TC // TT
    NF = TT // 512
    NSLOT = 4

    xbf = lb.sb("xbf", [128, KC, TT], BF16)
    xf32 = lb.sb("xf32", [128, KC, TT], F32)
    routed = lb.sb("routed", [128, KC, TT], F32)
    hbuf = lb.sb("hbuf", [128, KC, TT], BF16)
    s1buf = lb.sb("s1buf", [128, KC, TT], BF16)
    wk = [lb.sb(f"wk{i}", [128, KC, 512], BF16) for i in range(NSLOT)]
    wr_sb = lb.sb("wr_sb", [128, KC, 31], F32)
    brbc = lb.sb("brbc", [128, 31], F32)
    bs1s = lb.sb("bs1s", [128, 8], F32)
    bs2s = lb.sb("bs2s", [128, 8], F32)
    be1s = lb.sb("be1s", [128, 31 * 8], F32)
    be2s = lb.sb("be2s", [31, D], F32)
    growrow = lb.sb("growrow", [1, TT], F32)
    gaff = lb.sb("gaff", [128, KC, 31], F32)
    grow = lb.sb("grow", [128, 6], F32)
    gates = lb.sb("gates", [128, KC, 31], F32)
    gtmp = lb.sb("gtmp", [31, TT], F32)
    gbc = lb.sb("gbc", [128, TT], F32)
    htmp = lb.sb("htmp", [128, 4, TT], F32)
    gscr = lb.sb("gscr", [128, 31], F32)
    yout = lb.sb("yout2", [128, KC, 512], F32)
    statrow = lb.sb("statrow2", [1, 3 * 512], F32)

    pb = [lb.ps(f"pb{i}", [128, 512]) for i in range(8)]

    banks = Banks(lb, pb)
    wbuf = WBuf(lb, wk, "wk")
    zsq_view = htmp[:].rearrange("p a (b t) -> p (a b) t", t=512)
    aux = {"onescol": onescol, "onesrow": onesrow, "zsq": zsq_view,
           "statrow": statrow}

    def y_const(sy):
        lb.wait(sy, *io["shc_dep"])
        for dst, src in [(wr_sb, wr_h.rearrange("(c p) e -> p c e", p=128)),
                         (brbc, brbc_h[:]), (bs1s, bs1_h[:]),
                         (bs2s, bs2_h[:]), (be1s, be1_h[:]),
                         (be2s, be2_h[:])]:
            sy.dma_start(dst[:, :] if hasattr(dst, "shape") and len(dst.shape) == 2
                         else dst[:], src).then_inc(*lb.bump("cld", 16))
    lb.y(y_const)
    CLD = lb.mark("cld")

    wcd = io.get("wc3")
    unique = [rearr_pcn(ws1_h, 0, 512), rearr_pcn(ws1_h, 512, 512),
              rearr_pcn(ws2_h, 0, 512), rearr_pcn(ws2_h, 512, 512)]
    for e in range(31):
        unique.append(rearr_pcn(we1_h[e], 0, 512))
        unique.append(rearr_pcn(we1_h[e], 512, 512))
        unique.append(rearr_pcn(we2_h[e], 0, 512))
        unique.append(rearr_pcn(we2_h[e], 512, 512))
    WPT = 4 + 31 * 4
    if wcd is None:
        plan = unique * NT
    else:
        plan = list(unique)
        for t in range(1, NT):
            for k in range(WPT):
                plan.append(wcd[k][:].rearrange("p (c m) -> p c m", m=512))
    pumped = [0]
    store_total = {}

    def pump(n):
        n = min(n, len(plan) - pumped[0])
        if n <= 0:
            return

        def g_loads(g):
            for _ in range(n):
                idx = pumped[0]
                if wcd is not None and idx == WPT:
                    for s2 in range(NSLOT):
                        lb.wait(g, f"wkc{s2}", store_total[s2])
                wbuf.emit_load(g, plan[idx], KC)
                if wcd is not None and idx < WPT:
                    s2 = (wbuf.n_loads - 1) % NSLOT
                    tgt = wbuf.load_target[-1]
                    cap = wcd[idx][:].rearrange("p (c m) -> p c m", m=512)

                    def y_st(sy, s2=s2, tgt=tgt, cap=cap):
                        lb.wait(sy, *tgt)
                        sy.dma_start(cap, wk[s2][:, 0:KC, :]
                                     ).then_inc(*lb.bump(f"wkc{s2}", 16))
                    lb.y(y_st)
                    store_total[s2] = lb.mark(f"wkc{s2}")
                    wbuf.release_at(wbuf.n_loads - 1, f"wkc{s2}",
                                    store_total[s2])
                pumped[0] += 1
        lb.g(g_loads)

    xfree_mark = {}

    def pump_x(t):
        def g_x(g):
            if t > 0:
                lb.wait(g, "mm", xfree_mark[t - 1])
            elif io["stream_dep"] is not None:
                lb.wait(g, *io["stream_dep"])
            g.dma_start(xbf[:], rearr_pct(xin_h, TT * t, TT)
                        ).then_inc(*lb.bump("xld", 16))
        lb.g(g_x)

        def y_x(sy):
            if t > 0:
                lb.wait(sy, "evv", xf32_free[t - 1])
            elif io["stream_dep"] is not None:
                lb.wait(sy, *io["stream_dep"])
            sy.dma_start(xf32[:], rearr_pct(xin_h, TT * t, TT)
                         ).then_inc(*lb.bump("xfld", 16))
        lb.y(y_x)

    xf32_free = {}
    pump_x(0)
    pump(NSLOT)

    for t in range(NT):
        wi0 = t * WPT

        tr_marks = []
        for sti in range(KC):
            holders = {}

            def t_router(te, sti=sti, t=t, holders=holders):
                if sti == 0:
                    lb.wait(te, "cld", CLD)
                    lb.wait(te, "xfld", 16 * (t + 1))
                bi, bt = banks.acquire(te)
                for c in range(KC):
                    ins = te.matmul(bt[:, 0:31],
                                    xf32[:, c, 128 * sti:128 * (sti + 1)],
                                    wr_sb[:, c, :],
                                    start=(c == 0), stop=(c == KC - 1))
                ins.then_inc(*lb.bump("mm"))
                holders["m"] = lb.mark("mm")
                holders["b"] = (bi, bt)
            lb.t(t_router)

            def v_softmax_gates(v, sti=sti, holders=holders):
                if sti == 0:
                    lb.wait(v, "cld", CLD)
                lb.wait(v, "mm", holders["m"])
                a = gaff[:, sti, :]
                lb.ss(v, "evv", v.tensor_add(a, holders["b"][1][:, 0:31],
                                             brbc[:]))
                lb.ss(v, "evv", v.reduce_max(grow[:, 3:4], a, axis=AX.X))
                ins = v.tensor_scalar_mul(grow[:, 3:4], grow[:, 3:4], -1.0)
                ins.then_inc(*lb.bump("evv"))
                holders["aff0"] = lb.mark("evv")
                banks.set_guard(holders["b"][0], "evv", holders["aff0"])
            lb.v(v_softmax_gates)

            def s_exp(se, sti=sti, holders=holders):
                lb.wait(se, "evv", holders["aff0"])
                ins = se.activation(gaff[:, sti, :], gaff[:, sti, :], AF.Exp,
                                    bias=grow[:, 3:4])
                ins.then_inc(*lb.bump("evs"))
                holders["exp"] = lb.mark("evs")
            lb.s(s_exp)

            def v_soft2(v, sti=sti, holders=holders):
                lb.wait(v, "evs", holders["exp"])
                a = gaff[:, sti, :]
                g = gates[:, sti, :]

                def step(ins):
                    ins.then_inc(*lb.bump("evv"))
                    lb.wait(v, "evv", lb.mark("evv"))
                step(v.reduce_sum(grow[:, 4:5], a, axis=AX.X))
                step(v.reciprocal(grow[:, 5:6], grow[:, 4:5]))
                step(v.tensor_scalar_mul(a, a, grow[:, 5:6]))
                step(v.reduce_max(grow[:, 0:1], a, axis=AX.X))
                step(v.tensor_single_scalar(g, a, grow[:, 0:1], ALU.is_equal))
                step(v.scalar_tensor_tensor(g, g, -1e30, a, ALU.mult, ALU.add))
                step(v.reduce_max(grow[:, 1:2], g, axis=AX.X))
                step(v.tensor_single_scalar(gscr[:], g, grow[:, 1:2],
                                            ALU.is_equal))
                step(v.scalar_tensor_tensor(g, gscr[:], -1e30, g,
                                            ALU.mult, ALU.add))
                step(v.reduce_max(grow[:, 2:3], g, axis=AX.X))
                step(v.tensor_single_scalar(g, a, grow[:, 2:3], ALU.is_ge))
                ins = v.tensor_mul(g, g, a)
                ins.then_inc(*lb.bump("evv"))
                holders["gates"] = lb.mark("evv")
            lb.v(v_soft2)

            def t_tr(te, sti=sti, holders=holders):
                lb.wait(te, "evv", holders["gates"])
                bi, bt = banks.acquire(te)
                te.transpose(bt[0:31, 0:128], gates[:, sti, :], id128[:]
                             ).then_inc(*lb.bump("mm"))
                holders["mtr"] = lb.mark("mm")
                holders["btr"] = (bi, bt)
            lb.t(t_tr)

            def v_trev(v, sti=sti, holders=holders):
                lb.wait(v, "mm", holders["mtr"])
                ins = v.tensor_copy(gtmp[0:31, 128 * sti:128 * (sti + 1)],
                                    holders["btr"][1][0:31, 0:128])
                ins.then_inc(*lb.bump("evv"))
                banks.set_guard(holders["btr"][0], "evv", lb.mark("evv"))
            lb.v(v_trev)
            tr_marks.append(lb.mark("evv"))
        gates_done = ("evv", lb.mark("evv"))
        if DBG_L1FFN and t == 0:
            def y_dbg1(sy):
                lb.wait(sy, *gates_done)
                sy.dma_start(dbg_gates_h[:], gates[:]).then_inc(*lb.bump("st", 16))
                sy.dma_start(dbg_gtmp_h[:], gtmp[:]).then_inc(*lb.bump("st", 16))
            lb.y(y_dbg1)

        for j in range(KC):
            holders = {}

            def t_binit(te, j=j, holders=holders):
                if j == 0:
                    lb.wait(te, *gates_done)
                for f in range(NF):
                    bi, bt = banks.acquire(te)
                    ins = te.matmul(bt[:], be2s[:, 128 * j:128 * (j + 1)],
                                    gtmp[:, 512 * f:512 * (f + 1)],
                                    start=True, stop=True)
                    ins.then_inc(*lb.bump("mm"))
                    holders[f] = (bi, bt, lb.mark("mm"))
            lb.t(t_binit)

            def s_binit(se, j=j, holders=holders):
                for f in range(NF):
                    bi, bt, mk = holders[f]
                    lb.wait(se, "mm", mk)
                    ins = se.add(routed[:, j, 512 * f:512 * (f + 1)],
                                 bt[:], bs2s[:, j:j + 1])
                    ins.then_inc(*lb.bump("evs"))
                    banks.set_guard(bi, "evs", lb.mark("evs"))
            lb.s(s_binit)
        rinit_done = ("evs", lb.mark("evs"))
        if DBG_L1FFN and t == 0:
            def y_dbg2(sy):
                lb.wait(sy, *rinit_done)
                sy.dma_start(rearr_pct(dbg_rinit_h, 0, TT), routed[:]
                             ).then_inc(*lb.bump("st", 16))
            lb.y(y_dbg2)

        for half in range(2):
            holders = {}

            def t_s1(te, half=half, holders=holders):
                if half == 0:
                    lb.wait(te, "xld", 16 * (t + 1))
                w = wbuf.consume(te, wi0 + half)
                for j4 in range(4):
                    j = half * 4 + j4
                    for f in range(NF):
                        bi, bt = banks.acquire(te)
                        for c in range(KC):
                            ins = te.matmul(
                                bt[:], w[:, c, 128 * j4:128 * (j4 + 1)],
                                xbf[:, c, 512 * f:512 * (f + 1)],
                                start=(c == 0), stop=(c == KC - 1))
                        ins.then_inc(*lb.bump("mm"))
                        holders[(j4, f)] = (bi, bt, lb.mark("mm"))
                wbuf.release_at(wi0 + half, "mm", lb.mark("mm"))
            lb.t(t_s1)

            def s_s1(se, half=half, holders=holders):
                for j4 in range(4):
                    j = half * 4 + j4
                    for f in range(NF):
                        bi, bt, mk = holders[(j4, f)]
                        lb.wait(se, "mm", mk)
                        ins = se.activation(
                            s1buf[:, j, 512 * f:512 * (f + 1)], bt[:],
                            AF.Gelu, bias=bs1s[:, j:j + 1])
                        ins.then_inc(*lb.bump("evs"))
                        banks.set_guard(bi, "evs", lb.mark("evs"))
            lb.s(s_s1)
        s1_done = ("evs", lb.mark("evs"))

        for half in range(2):
            holders = {}

            def t_s2(te, half=half, holders=holders):
                if half == 0:
                    lb.wait(te, *s1_done)
                w = wbuf.consume(te, wi0 + 2 + half)
                for j4 in range(4):
                    j = half * 4 + j4
                    for f in range(NF):
                        bi, bt = banks.acquire(te)
                        for c in range(KC):
                            ins = te.matmul(
                                bt[:], w[:, c, 128 * j4:128 * (j4 + 1)],
                                s1buf[:, c, 512 * f:512 * (f + 1)],
                                start=(c == 0), stop=(c == KC - 1))
                        ins.then_inc(*lb.bump("mm"))
                        holders[(j4, f)] = (bi, bt, lb.mark("mm"))
                wbuf.release_at(wi0 + 2 + half, "mm", lb.mark("mm"))
            lb.t(t_s2)

            def v_s2(v, half=half, holders=holders):
                if DBG_L1FFN:
                    lb.wait(v, "st", lb.mark("st"))
                lb.wait(v, *rinit_done)
                for j4 in range(4):
                    j = half * 4 + j4
                    for f in range(NF):
                        bi, bt, mk = holders[(j4, f)]
                        lb.wait(v, "mm", mk)
                        sl = routed[:, j, 512 * f:512 * (f + 1)]
                        ins = v.tensor_add(sl, sl, bt[:])
                        ins.then_inc(*lb.bump("evv"))
                        banks.set_guard(bi, "evv", lb.mark("evv"))
            lb.v(v_s2)
        pump(4)

        if DBG_L1FFN and t == 0:
            def y_dbg3(sy):
                lb.wait(sy, "evv", lb.mark("evv"))
                sy.dma_start(rearr_pct(dbg_shared_h, 0, TT), routed[:]
                             ).then_inc(*lb.bump("st", 16))
            lb.y(y_dbg3)
        carry = {"prev_hg": None}
        for e in range(31):
            holders = {"prev_hg": carry["prev_hg"]}

            if DBG_L1FFN and t == 0 and e == 1:
                def y_dbg5(sy):
                    lb.wait(sy, "evv", lb.mark("evv"))
                    sy.dma_start(dbg_gbc_h[:], gbc[:]).then_inc(*lb.bump("st", 16))
                    sy.dma_start(dbg_grow_h[:], growrow[:]).then_inc(*lb.bump("st", 16))
                lb.y(y_dbg5)

            def t_grow(te, e=e, holders=holders):
                if DBG_L1FFN:
                    lb.wait(te, "st", lb.mark("st"))
                for f in range(NF):
                    bi, bt = banks.acquire(te)
                    ins = te.matmul(bt[0:1, :], id128[0:31, e:e + 1],
                                    gtmp[:, 512 * f:512 * (f + 1)],
                                    start=True, stop=True)
                    ins.then_inc(*lb.bump("mm"))
                    holders[("gr", f)] = (bi, bt, lb.mark("mm"))
            lb.t(t_grow)

            def v_grow(v, e=e, holders=holders):
                for f in range(NF):
                    bi, bt, mk = holders[("gr", f)]
                    lb.wait(v, "mm", mk)
                    ins = v.tensor_copy(growrow[0:1, 512 * f:512 * (f + 1)],
                                        bt[0:1, :])
                    ins.then_inc(*lb.bump("evv"))
                    banks.set_guard(bi, "evv", lb.mark("evv"))
                holders["grow"] = lb.mark("evv")
            lb.v(v_grow)

            def t_gbc(te, e=e, holders=holders):
                lb.wait(te, "evv", holders["grow"])
                for f in range(NF):
                    bi, bt = banks.acquire(te)
                    ins = te.matmul(bt[:], onesrow[:],
                                    growrow[0:1, 512 * f:512 * (f + 1)],
                                    start=True, stop=True)
                    ins.then_inc(*lb.bump("mm"))
                    holders[("g", f)] = (bi, bt, lb.mark("mm"))
            lb.t(t_gbc)

            def v_gbc(v, e=e, holders=holders):
                for f in range(NF):
                    bi, bt, mk = holders[("g", f)]
                    lb.wait(v, "mm", mk)
                    ins = v.tensor_copy(gbc[:, 512 * f:512 * (f + 1)], bt[:])
                    ins.then_inc(*lb.bump("evv"))
                    banks.set_guard(bi, "evv", lb.mark("evv"))
                holders["gbc"] = lb.mark("evv")
            lb.v(v_gbc)

            for half in range(2):
                hh = {}

                def t_h(te, e=e, half=half, hh=hh):
                    w = wbuf.consume(te, wi0 + 4 + 4 * e + half)
                    for j4 in range(4):
                        for f in range(NF):
                            bi, bt = banks.acquire(te)
                            for c in range(KC):
                                ins = te.matmul(
                                    bt[:], w[:, c, 128 * j4:128 * (j4 + 1)],
                                    xbf[:, c, 512 * f:512 * (f + 1)],
                                    start=(c == 0), stop=(c == KC - 1))
                            ins.then_inc(*lb.bump("mm"))
                            hh[(j4, f)] = (bi, bt, lb.mark("mm"))
                    wbuf.release_at(wi0 + 4 + 4 * e + half, "mm",
                                    lb.mark("mm"))
                lb.t(t_h)

                def s_h(se, e=e, half=half, hh=hh, holders=holders):
                    if holders.get("prev_hg") is not None:
                        lb.wait(se, "evv", holders["prev_hg"])
                    for j4 in range(4):
                        for f in range(NF):
                            bi, bt, mk = hh[(j4, f)]
                            lb.wait(se, "mm", mk)
                            ins = se.activation(
                                htmp[:, j4, 512 * f:512 * (f + 1)],
                                bt[:], AF.Gelu,
                                bias=be1s[:, 8 * e + half * 4 + j4:
                                          8 * e + half * 4 + j4 + 1])
                            ins.then_inc(*lb.bump("evs"))
                            hh[("gel", j4, f)] = lb.mark("evs")
                            banks.set_guard(bi, "evs", lb.mark("evs"))
                lb.s(s_h)

                def v_hg(v, e=e, half=half, hh=hh, holders=holders):
                    if DBG_L1FFN:
                        lb.wait(v, "st", lb.mark("st"))
                    if half == 0:
                        lb.wait(v, "evv", holders["gbc"])
                    for j4 in range(4):
                        j = half * 4 + j4
                        for f in range(NF):
                            lb.wait(v, "evs", hh[("gel", j4, f)])
                            ins = v.tensor_mul(
                                hbuf[:, j, 512 * f:512 * (f + 1)],
                                htmp[:, j4, 512 * f:512 * (f + 1)],
                                gbc[:, 512 * f:512 * (f + 1)])
                            ins.then_inc(*lb.bump("evv"))
                lb.v(v_hg)
                holders["prev_hg"] = lb.mark("evv")
            holders["h_done"] = lb.mark("evv")
            carry["prev_hg"] = holders["h_done"]

            for half in range(2):
                hh = {}

                def t_c(te, e=e, half=half, hh=hh, holders=holders):
                    if half == 0:
                        lb.wait(te, "evv", holders["h_done"])
                    w = wbuf.consume(te, wi0 + 4 + 4 * e + 2 + half)
                    for j4 in range(4):
                        for f in range(NF):
                            bi, bt = banks.acquire(te)
                            for c in range(KC):
                                ins = te.matmul(
                                    bt[:], w[:, c, 128 * j4:128 * (j4 + 1)],
                                    hbuf[:, c, 512 * f:512 * (f + 1)],
                                    start=(c == 0), stop=(c == KC - 1))
                            ins.then_inc(*lb.bump("mm"))
                            hh[(j4, f)] = (bi, bt, lb.mark("mm"))
                    wbuf.release_at(wi0 + 4 + 4 * e + 2 + half, "mm",
                                    lb.mark("mm"))
                lb.t(t_c)

                def v_c(v, e=e, half=half, hh=hh):
                    for j4 in range(4):
                        j = half * 4 + j4
                        for f in range(NF):
                            bi, bt, mk = hh[(j4, f)]
                            lb.wait(v, "mm", mk)
                            sl = routed[:, j, 512 * f:512 * (f + 1)]
                            ins = v.tensor_add(sl, sl, bt[:])
                            ins.then_inc(*lb.bump("evv"))
                            banks.set_guard(bi, "evv", lb.mark("evv"))
                lb.v(v_c)
            pump(4)
            if DBG_L1FFN and t == 0 and e == 0:
                def y_dbg4(sy):
                    lb.wait(sy, "evv", lb.mark("evv"))
                    sy.dma_start(rearr_pct(dbg_e0_h, 0, TT), routed[:]
                                 ).then_inc(*lb.bump("st", 16))
                lb.y(y_dbg4)

                def g_dbg4(g):
                    lb.wait(g, "evv", lb.mark("evv"))
                    g.dma_start(rearr_pct(dbg_h0_h, 0, TT), hbuf[:]
                                ).then_inc(*lb.bump("st", 16))
                lb.g(g_dbg4)
        routed_done = ("evv", lb.mark("evv"))

        for st_i in range(TT // 512):
            zh = {}

            def v_z(v, t=t, st_i=st_i, zh=zh):
                lb.wait(v, *routed_done)
                for c in range(KC):
                    sl = routed[:, c, 512 * st_i:512 * (st_i + 1)]
                    ins = v.scalar_tensor_tensor(
                        sl, xf32[:, c, 512 * st_i:512 * (st_i + 1)],
                        2.0, sl, ALU.mult, ALU.add)
                ins.then_inc(*lb.bump("evv"))
                zh["z"] = lb.mark("evv")
            lb.v(v_z)

            ln_dep = emit_ln(lb, banks, aux, routed, yout,
                             ("evv", zh["z"]), nt=512, z_off=512 * st_i,
                             out_extra_deps=[("st", lb.mark("st"))])

            def y_store(sy, t=t, st_i=st_i, ln_dep=ln_dep):
                lb.wait(sy, *ln_dep)
                sy.dma_start(rearr_pct(xout_h, TT * t + 512 * st_i, 512),
                             yout[:]).then_inc(*lb.bump("st", 16))
            lb.y(y_store)
        xfree_mark[t] = lb.mark("mm")
        xf32_free[t] = lb.mark("evv")
        if t + 1 < NT:
            pump_x(t + 1)

    return (lb.qual("st"), lb.mark("st"))


def host_l1ffn_inputs(inputs, xT_slices):
    ws1 = np.ascontiguousarray(inputs["L1_ffn_ws1"]).astype(np.float32)
    ws2 = np.ascontiguousarray(inputs["L1_ffn_ws2"]).astype(np.float32)
    we1 = np.ascontiguousarray(inputs["L1_ffn_We1"]).astype(np.float32)
    we2 = np.ascontiguousarray(inputs["L1_ffn_We2"]).astype(np.float32)
    wr = np.ascontiguousarray(inputs["L1_ffn_Wr"]).astype(np.float32)
    br_bc = np.ascontiguousarray(
        np.broadcast_to(inputs["L1_ffn_br"][None, :], (128, 31))).astype(np.float32)
    bs1pc = np.ascontiguousarray(inputs["L1_ffn_bs1"].reshape(8, 128).T)
    bs2pc = np.ascontiguousarray(inputs["L1_ffn_bs2"].reshape(8, 128).T)
    be1 = inputs["L1_ffn_Be1"].reshape(31, 8, 128)
    be1pc = np.ascontiguousarray(be1.transpose(2, 0, 1).reshape(128, 31 * 8))
    be2r = np.ascontiguousarray(inputs["L1_ffn_Be2"]).astype(np.float32)
    common = dict(ws1=ws1, ws2=ws2, We1=we1, We2=we2, Wr=wr, br_bc=br_bc,
                  bs1pc=bs1pc, bs2pc=bs2pc, be1pc=be1pc, be2r=be2r,
                  onescol=np.ones((128, 1), np.float32),
                  onesrow=np.ones((1, 128), np.float32),
                  ident128=np.eye(128, dtype=np.float32))
    if xT_slices is None:
        return [dict(common) for c in range(NCORES)]
    return [dict(common, xinT=np.ascontiguousarray(xT_slices[c]))
            for c in range(NCORES)]




def emit_ag_prefix(lb, tag, src_sb, P, X, maskpfx, dst_sb, deps, gath):
    nc = lb.nc
    ag_in = lb.dram(f"ag_{tag}_in", [P, X])
    ag_out = lb.dram(f"ag_{tag}_out", [8 * P, X])

    xcap = gath.shape[2]
    npieces = ceil_div(X, xcap)

    def g_ag(g):
        lb.wait(g, *deps)
        g.dma_start(ag_in[:], src_sb[0:P, 0:X]).then_inc(*lb.bump("agd", 16))
        lb.wait(g, "agd", lb.mark("agd"))
        g.collective_compute(
            "AllGather", ALU.bypass, RG,
            ins=[ag_in.opt()], outs=[ag_out.opt()]).then_inc(*lb.bump("cc"))
        lb.wait(g, "cc", lb.mark("cc"))
    lb.g(g_ag)

    for pi in range(npieces):
        x0 = pi * xcap
        x1 = min(X, x0 + xcap)

        def g_ld(g, x0=x0, x1=x1):
            if x0 > 0:
                lb.wait(g, "evv", lb.mark("evv"))
            g.dma_start(gath[:, :, 0:x1 - x0],
                        ag_out[:, x0:x1].rearrange("(r p) x -> p r x", p=P),
                        allow_slow_non_contiguous=(x1 - x0) <= 4
                        ).then_inc(*lb.bump("agd", 16))
        lb.g(g_ld)
        AGD = lb.mark("agd")

        def v_pfx(v, x0=x0, x1=x1, AGD=AGD):
            lb.wait(v, "agd", AGD)
            lb.ss(v, "evv", v.memset(dst_sb[0:P, x0:x1], 0.0))
            for r in range(8):
                lb.ss(v, "evv", v.scalar_tensor_tensor(
                    dst_sb[0:P, x0:x1], gath[:, r, 0:x1 - x0],
                    maskpfx[0:P, r:r + 1],
                    dst_sb[0:P, x0:x1], ALU.mult, ALU.add))
            v.engine_nop().then_inc(*lb.bump("evv"))
        lb.v(v_pfx)
    return ("evv", lb.mark("evv"))


def build_l1attn():
    lb = LB()
    xin_h = lb.inp("xinT", [D, TC])
    ocol_h = lb.inp("onescol", [128, 1])
    orow_h = lb.inp("onesrow", [1, 128])
    xout_h = lb.out("xoutT", [D, TC])
    lb.start_block(["shared"], allow_lp=True)
    onescol = lb.sb("sh_onescol", [128, 1], F32)
    onesrow = lb.sb("sh_onesrow", [1, 128], F32)

    def y_shc(sy):
        sy.dma_start(onescol[:, :], ocol_h[:]).then_inc(*lb.bump("@shared", 16))
        sy.dma_start(onesrow[:, :], orow_h[:]).then_inc(*lb.bump("@shared", 16))
    lb.y(y_shc)
    io = {"xin": xin_h, "xout": xout_h, "onescol": onescol,
          "onesrow": onesrow, "shc_dep": ("@shared", lb.mark("@shared")),
          "stream_dep": None}
    dep = emit_l1attn(lb, io)

    def y_final(sy):
        lb.wait(sy, *dep)
    lb.y(y_final)
    return lb.done()


def emit_l1attn(lb, io):
    xin_h = io["xin"]
    xout_h = io["xout"]
    onescol = io["onescol"]
    onesrow = io["onesrow"]
    bg = io.get("bg_pump")
    wq_h = lb.inp("Wq", [D, D])
    wk_h = lb.inp("Wk", [D, D])
    wv_h = lb.inp("Wv", [D, D])
    wo_h = lb.inp("Wo", [D, D])
    bq_h = lb.inp("bqpc", [128, 8])
    bk_h = lb.inp("bkpc", [128, 8])
    bv_h = lb.inp("bvpc", [128, 8])
    bo_h = lb.inp("bopc", [128, 8])
    nrm_h = lb.inp("nrm16", [16, TC])
    hsum_h = lb.inp("hsum", [128, 8, 16], BF16)
    hbc_h = lb.inp("hbcast", [16, 1024], BF16)
    tri_h = lb.inp("trimask", [128, 128], BF16)
    idb_h = lb.inp("id128b", [128, 128], BF16)
    mp_h = lb.inp("maskpfx", [128, 8])
    if DBG_L1ATTN:
        dbg_k_h = lb.out("dbg_k", [128, KC, TC], BF16)
        dbg_q_h = lb.out("dbg_q", [128, KC, TC], BF16)
        dbg_si_h = lb.out("dbg_si", [16, TC], BF16)
        dbg_son_h = lb.out("dbg_son", [16, TC], BF16)
        dbg_sa_h = lb.out("dbg_sa", [16, TC], BF16)
        dbg_cs_h = lb.out("dbg_cs", [16, TC], BF16)
        dbg_sc_h = lb.out("dbg_sc", [16, TC], BF16)
        dbg_scal_h = lb.out("dbg_scal", [16, TC], BF16)
        dbg_kv_h = lb.out("dbg_kv", [128, KC, 64])
        dbg_ofm_h = lb.out("dbg_ofm", [128, KC, TC], BF16)
        dbg_carry_h = lb.out("dbg_carry", [128, 16])
        dbg_x_h = lb.out("dbg_x", [128, KC, TC], BF16)
        dbg_q2_h = lb.out("dbg_q2", [128, KC, TC], BF16)
        dbg_kso_h = lb.out("dbg_kso", [128, 512], BF16)
        dbg_acc1_h = lb.out("dbg_acc1", [16, 512])
        dbg_acc_h = lb.out("dbg_acc", [16, 512])

    NSLOT = 3
    NTCH = TC // 512
    NTC = TC // 128

    q = lb.sb("q", [128, KC, TC], BF16)
    ofm = lb.sb("ofm", [128, KC, TC], BF16)
    hsum = lb.sb("hsum_s", [128, 8, 16], BF16)
    hbc = lb.sb("hbc_s", [16, 1024], BF16)
    tri = lb.sb("tri_s", [128, 128], BF16)
    idb = lb.sb("idb_s", [128, 128], BF16)
    mpfx = lb.sb("mpfx_s", [128, 8], F32)
    bqs = lb.sb("bqs", [128, 8], F32)
    bks = lb.sb("bks", [128, 8], F32)
    bvs = lb.sb("bvs", [128, 8], F32)
    bos = lb.sb("bos", [128, 8], F32)
    wk_t = [lb.sb(f"wkt{i}", [128, KC, 128], BF16) for i in range(3)]
    nrm16 = lb.sb("nrm16_s", [16, TC], BF16)
    rn16 = lb.sb("rn16_s", [16, TC], F32)
    lsum = lb.sb("lsum", [128, 16], F32)
    carry = lb.sb("carry", [128, 16], F32)
    lsum2 = lb.sb("lsum2", [128, 16], F32)
    carry2 = lb.sb("carry2", [128, 16], F32)
    cs_sum = lb.sb("cs_sum", [16, 1], F32)
    cs_carry = lb.sb("cs_carry", [16, 1], F32)
    kv_loc = lb.sb("kv_loc", [128, KC, 64], F32)
    kv_carry = lb.sb("kv_carry", [128, KC, 64], F32)
    kv_bf = lb.sb("kv_bf", [128, KC, 64], BF16)
    zeros = lb.sb("zeros", [128, 512], BF16)
    agg1 = lb.sb("agg1", [128, 8, 16], F32)
    agg3 = lb.sb("agg3", [16, 8, 1], F32)
    agg2 = agg1

    k = lb.sb_scoped("k", [128, KC, TC], BF16)
    v = lb.sb_scoped("v", [128, KC, TC], BF16)
    xbf = lb.sb_scoped("xbf", [128, KC, TC], BF16)

    pbf32 = [lb.ps(f"pb{i}", [128, 512]) for i in range(6)]
    pbf16 = [lb.ps(f"pt{i}", [128, 256], BF16) for i in range(2)]

    banks = Banks(lb, pbf32[0:4])
    dotA_bank, dotB_bank = pbf32[4], pbf32[5]
    tbanks = Banks(lb, pbf16)
    wbuf = WBuf(lb, wk_t, "wkt")

    def y_const(sy):
        lb.wait(sy, *io["shc_dep"])
        for dst, src in [(bqs[:, :], bq_h[:]), (bks[:, :], bk_h[:]),
                         (bvs[:, :], bv_h[:]), (bos[:, :], bo_h[:]),
                         (hsum[:], hsum_h[:]),
                         (tri[:, :], tri_h[:]),
                         (mpfx[:, :], mp_h[:])]:
            sy.dma_start(dst, src).then_inc(*lb.bump("cld", 16))
    lb.y(y_const)

    def g_const(g):
        for dst, src in [(hbc[:, :], hbc_h[:]), (idb[:, :], idb_h[:]),
                         (nrm16[:, :], nrm_h[:])]:
            g.dma_start(dst, src).then_inc(*lb.bump("cld", 16))
        if io["stream_dep"] is not None:
            lb.wait(g, *io["stream_dep"])
        g.dma_start(xbf[:], rearr_pct(xin_h, 0, TC)).then_inc(*lb.bump("xld", 16))
    lb.g(g_const)
    CLD = lb.mark("cld")

    def v_zero(ve):
        lb.ss(ve, "evv", ve.memset(zeros[:], 0.0))
        lb.wait(ve, "cld", CLD)
        ins = ve.reciprocal(rn16[0:16, :], nrm16[0:16, :])
        ins.then_inc(*lb.bump("evv"))
    lb.v(v_zero)
    ZR = lb.mark("evv")

    plan = []
    for w_h in (wq_h, wk_h, wv_h):
        for j in range(KC):
            plan.append(rearr_pcn(w_h, 128 * j, 128))
    pumped = [0]

    def pump(n):
        n = min(n, len(plan) - pumped[0])
        if n <= 0:
            return

        def g_loads(g):
            for _ in range(n):
                wbuf.emit_load(g, plan[pumped[0]], KC)
                pumped[0] += 1
        lb.g(g_loads)
    pump(NSLOT)

    def proj(widx0, dst, act, bias_sb):
        for j in range(KC):
            holders = {}

            def t_p(te, j=j, holders=holders, widx0=widx0):
                if widx0 == 0 and j == 0:
                    lb.wait(te, "cld", CLD)
                    lb.wait(te, "xld", 16)
                w = wbuf.consume(te, widx0 + j)
                for f in range(4):
                    bi, bt = banks.acquire(te)
                    for c in range(KC):
                        ins = te.matmul(bt[:], w[:, c, :],
                                        xbf[:, c, 512 * f:512 * (f + 1)],
                                        start=(c == 0), stop=(c == KC - 1))
                    ins.then_inc(*lb.bump("mm"))
                    holders[f] = (bi, bt, lb.mark("mm"))
                wbuf.release_at(widx0 + j, "mm", lb.mark("mm"))
            lb.t(t_p)

            def s_p(se, j=j, holders=holders, dst=dst, act=act,
                    bias_sb=bias_sb):
                for f in range(4):
                    bi, bt, mk = holders[f]
                    lb.wait(se, "mm", mk)
                    if act is None:
                        ins = se.add(dst[:, j, 512 * f:512 * (f + 1)], bt[:],
                                     bias_sb[:, j:j + 1])
                    else:
                        ins = se.activation(dst[:, j, 512 * f:512 * (f + 1)],
                                            bt[:], act,
                                            bias=bias_sb[:, j:j + 1])
                    ins.then_inc(*lb.bump("evs"))
                    banks.set_guard(bi, "evs", lb.mark("evs"))
            lb.s(s_p)
            pump(1)
            if bg is not None:
                def g_bg(g):
                    bg(g, 5)
                lb.g(g_bg)

    proj(0, q, AF.Sigmoid, bqs)
    proj(KC, k, AF.Sigmoid, bks)
    proj(2 * KC, v, None, bvs)
    P1_done = ("evs", lb.mark("evs"))
    if DBG_L1ATTN:
        def y_dbgx(sy):
            lb.wait(sy, *P1_done)
            sy.dma_start(dbg_x_h[:], xbf[:]).then_inc(*lb.bump("st", 16))
            sy.dma_start(dbg_q2_h[:], q[:]).then_inc(*lb.bump("st", 16))
            lb.wait(sy, "st", lb.mark("st"))
        lb.y(y_dbgx)
    lb.free_sb("xbf")
    CSf = lb.sb_scoped("CSf", [16, TC], F32)
    si = lb.sb_scoped("si", [16, TC], BF16)
    sa = lb.sb_scoped("sa", [16, TC], BF16)
    csb = lb.sb_scoped("csb", [16, TC], BF16)
    scb = lb.sb_scoped("scb", [16, TC], BF16)
    scal = lb.sb_scoped("scal", [16, TC], BF16)
    son = lb.sb_scoped("son", [16, TC], BF16)
    sinb = lb.sb_scoped("sinb", [16, TC], BF16)
    KCch = lb.sb_scoped("KCch", [128, 512], F32)
    eps1 = lb.sb_scoped("eps1", [128, 512], F32)
    prodA = lb.sb_scoped("prodA", [128, 2, 512], BF16)
    prodB = lb.sb_scoped("prodB", [128, 2, 512], BF16)
    ksoc = lb.sb_scoped("ksoc", [128, 512], BF16)
    qsic = lb.sb_scoped("qsic", [128, 512], BF16)
    QCch = KCch
    eps2 = eps1

    def v_lsum(ve):
        lb.wait(ve, *P1_done)
        lb.ss(ve, "evv", ve.reduce_sum(lsum[:, 0:8], k[:], axis=AX.X))
        lb.ss(ve, "evv", ve.reduce_sum(lsum[:, 8:16], q[:], axis=AX.X))
        ve.engine_nop().then_inc(*lb.bump("evv"))
    lb.v(v_lsum)
    ag1 = emit_ag_prefix(lb, "s1", lsum, 128, 16, mpfx, carry,
                         ("evv", lb.mark("evv")), agg1)

    dotm = {}
    dot_guard = [None]
    for tch in range(NTCH):
        ts = slice(512 * tch, 512 * (tch + 1))
        th = {}

        for ch in range(KC):
            par = ch % 2
            hh = {}

            def v_b(ve, tch=tch, ts=ts, ch=ch, par=par, hh=hh):
                if tch == 0 and ch == 0:
                    lb.wait(ve, *ag1)
                prev = (tch, ch - 2) if ch >= 2 else (tch - 1, KC - 2 + ch)
                if prev in dotm:
                    lb.wait(ve, "mm", dotm[prev])
                lb.ss(ve, "evv", ve.tensor_tensor_scan(
                    KCch[:], k[:, ch, ts], zeros[:],
                    carry[:, ch:ch + 1], ALU.add, ALU.add))
                lb.ss(ve, "evv", ve.tensor_copy(carry[:, ch:ch + 1],
                                                KCch[:, 511:512]))
                lb.ss(ve, "evv", ve.tensor_scalar_add(eps1[:], q[:, ch, ts],
                                                      1e-6))
                lb.ss(ve, "evv", ve.scalar_tensor_tensor(
                    prodA[:, par, :], KCch[:], 1e-6, eps1[:],
                    ALU.add, ALU.mult))
                lb.ss(ve, "evv", ve.tensor_tensor_scan(
                    QCch[:], q[:, ch, ts], zeros[:],
                    carry[:, 8 + ch:9 + ch], ALU.add, ALU.add))
                lb.ss(ve, "evv", ve.tensor_copy(carry[:, 8 + ch:9 + ch],
                                                QCch[:, 511:512]))
                lb.ss(ve, "evv", ve.tensor_scalar_add(eps2[:], k[:, ch, ts],
                                                      1e-6))
                ins = ve.scalar_tensor_tensor(
                    prodB[:, par, :], QCch[:], 1e-6, eps2[:],
                    ALU.add, ALU.mult)
                ins.then_inc(*lb.bump("evv"))
                hh["pr"] = lb.mark("evv")
            lb.v(v_b)

            def t_d(te, tch=tch, ch=ch, par=par, hh=hh, th=th):
                if ch == 0 and dot_guard[0] is not None:
                    lb.wait(te, "evv", dot_guard[0])
                lb.wait(te, "evv", hh["pr"])
                te.matmul(dotA_bank[0:16, :], hsum[:, ch, :],
                          prodA[:, par, :],
                          start=(ch == 0), stop=(ch == KC - 1),
                          skip_group_check=True)
                ins = te.matmul(dotB_bank[0:16, :], hsum[:, ch, :],
                                prodB[:, par, :],
                                start=(ch == 0), stop=(ch == KC - 1),
                                skip_group_check=True)
                ins.then_inc(*lb.bump("mm"))
                dotm[(tch, ch)] = lb.mark("mm")
            lb.t(t_d)

        def v_si(ve, tch=tch, ts=ts, th=th):
            lb.wait(ve, "mm", dotm[(tch, KC - 1)])
            lb.ss(ve, "evv", ve.reciprocal(si[0:16, ts], dotA_bank[0:16, :]))
            lb.ss(ve, "evv", ve.reciprocal(eps1[0:16, 0:512],
                                           dotB_bank[0:16, :]))
            dot_guard[0] = lb.mark("evv")
            lb.ss(ve, "evv", ve.tensor_mul(son[0:16, ts], eps1[0:16, 0:512],
                                           nrm16[0:16, ts]))
            ins = ve.tensor_mul(sinb[0:16, ts], si[0:16, ts], nrm16[0:16, ts])
            ins.then_inc(*lb.bump("evv"))
        lb.v(v_si)
    P3_done = ("evv", lb.mark("evv"))
    if DBG_L1ATTN:
        def y_dbg3(sy):
            lb.wait(sy, *P3_done)
            sy.dma_start(dbg_k_h[:], k[:]).then_inc(*lb.bump("st", 16))
            sy.dma_start(dbg_si_h[:], si[0:16, :]).then_inc(*lb.bump("st", 16))
            sy.dma_start(dbg_son_h[:], son[0:16, :]).then_inc(*lb.bump("st", 16))
            sy.dma_start(dbg_carry_h[:], carry[:]).then_inc(*lb.bump("st", 16))
        lb.y(y_dbg3)

    def v_init_ls2(ve):
        lb.wait(ve, *P3_done)
        ve.memset(lsum2[:], 0.0).then_inc(*lb.bump("evv"))
    lb.v(v_init_ls2)

    for tch in range(NTCH):
        ts = slice(512 * tch, 512 * (tch + 1))
        for ch in range(KC):
            holders = {}

            def t_bc(te, tch=tch, ts=ts, ch=ch, holders=holders):
                i1, b1 = banks.acquire(te)
                te.matmul(b1[:], hbc[:, 128 * ch:128 * (ch + 1)],
                          son[0:16, ts], start=True, stop=True
                          ).then_inc(*lb.bump("mm"))
                holders["so"] = (i1, b1, lb.mark("mm"))
                i2, b2 = banks.acquire(te)
                te.matmul(b2[:], hbc[:, 128 * ch:128 * (ch + 1)],
                          sinb[0:16, ts], start=True, stop=True
                          ).then_inc(*lb.bump("mm"))
                holders["si"] = (i2, b2, lb.mark("mm"))
            lb.t(t_bc)

            def v_ls2(ve, tch=tch, ts=ts, ch=ch, holders=holders):
                i1, b1, mk1 = holders["so"]
                lb.wait(ve, "evv", ZR)
                lb.wait(ve, "mm", mk1)
                lb.ss(ve, "evv", ve.tensor_mul(ksoc[:], k[:, ch, ts], b1[:]))
                banks.set_guard(i1, "evv", lb.mark("evv"))
                lb.ss(ve, "evv", ve.reduce_sum(eps1[:, 0:1], ksoc[:],
                                               axis=AX.X))
                lb.ss(ve, "evv", ve.tensor_add(lsum2[:, ch:ch + 1],
                                               lsum2[:, ch:ch + 1],
                                               eps1[:, 0:1]))
                i2, b2, mk2 = holders["si"]
                lb.wait(ve, "mm", mk2)
                lb.ss(ve, "evv", ve.tensor_mul(qsic[:], q[:, ch, ts], b2[:]))
                banks.set_guard(i2, "evv", lb.mark("evv"))
                lb.ss(ve, "evv", ve.reduce_sum(eps1[:, 0:1], qsic[:],
                                               axis=AX.X))
                ins = ve.tensor_add(lsum2[:, 8 + ch:9 + ch],
                                    lsum2[:, 8 + ch:9 + ch], eps1[:, 0:1])
                ins.then_inc(*lb.bump("evv"))
            lb.v(v_ls2)
    ag2 = emit_ag_prefix(lb, "s2", lsum2, 128, 16, mpfx, carry2,
                         ("evv", lb.mark("evv")), agg2)

    dotm2 = {}
    sacs_mark = {}
    for tch in range(NTCH):
        ts = slice(512 * tch, 512 * (tch + 1))
        for ch in range(KC):
            par = ch % 2
            hh = {}

            def t_bc2(te, tch=tch, ts=ts, ch=ch, hh=hh):
                if tch == 0 and ch == 0:
                    lb.wait(te, *ag2)
                i1, b1 = banks.acquire(te)
                te.matmul(b1[:], hbc[:, 128 * ch:128 * (ch + 1)],
                          son[0:16, ts], start=True, stop=True
                          ).then_inc(*lb.bump("mm"))
                hh["so"] = (i1, b1, lb.mark("mm"))
                i2, b2 = banks.acquire(te)
                te.matmul(b2[:], hbc[:, 128 * ch:128 * (ch + 1)],
                          sinb[0:16, ts], start=True, stop=True
                          ).then_inc(*lb.bump("mm"))
                hh["si"] = (i2, b2, lb.mark("mm"))
            lb.t(t_bc2)

            def v_cd(ve, tch=tch, ts=ts, ch=ch, par=par, hh=hh):
                if DBG_L1ATTN:
                    lb.wait(ve, "st", lb.mark("st"))
                if ch == 0 and tch > 0:
                    lb.wait(ve, "evs", sacs_mark[tch - 1])
                prev = (tch, ch - 2) if ch >= 2 else (tch - 1, KC - 2 + ch)
                if prev in dotm2:
                    lb.wait(ve, "mm", dotm2[prev])
                i1, b1, mk1 = hh["so"]
                lb.wait(ve, "mm", mk1)
                lb.ss(ve, "evv", ve.tensor_mul(ksoc[:], k[:, ch, ts], b1[:]))
                banks.set_guard(i1, "evv", lb.mark("evv"))
                lb.ss(ve, "evv", ve.tensor_tensor_scan(
                    KCch[:], ksoc[:], zeros[:], carry2[:, ch:ch + 1],
                    ALU.add, ALU.add))
                lb.ss(ve, "evv", ve.tensor_copy(carry2[:, ch:ch + 1],
                                                KCch[:, 511:512]))
                lb.ss(ve, "evv", ve.tensor_scalar_add(eps1[:], q[:, ch, ts],
                                                      1e-6))
                lb.ss(ve, "evv", ve.scalar_tensor_tensor(
                    prodA[:, par, :], KCch[:], 1e-6, eps1[:],
                    ALU.add, ALU.mult))
                i2, b2, mk2 = hh["si"]
                lb.wait(ve, "mm", mk2)
                lb.ss(ve, "evv", ve.tensor_mul(qsic[:], q[:, ch, ts], b2[:]))
                banks.set_guard(i2, "evv", lb.mark("evv"))
                lb.ss(ve, "evv", ve.tensor_tensor_scan(
                    QCch[:], qsic[:], zeros[:], carry2[:, 8 + ch:9 + ch],
                    ALU.add, ALU.add))
                lb.ss(ve, "evv", ve.tensor_copy(carry2[:, 8 + ch:9 + ch],
                                                QCch[:, 511:512]))
                lb.ss(ve, "evv", ve.tensor_scalar_add(eps2[:], k[:, ch, ts],
                                                      1e-6))
                ins = ve.scalar_tensor_tensor(
                    prodB[:, par, :], QCch[:], 1e-6, eps2[:],
                    ALU.add, ALU.mult)
                ins.then_inc(*lb.bump("evv"))
                hh["pr"] = lb.mark("evv")
            lb.v(v_cd)

            def t_d2(te, tch=tch, ch=ch, par=par, hh=hh):
                if ch == 0 and dot_guard[0] is not None:
                    lb.wait(te, "evv", dot_guard[0])
                lb.wait(te, "evv", hh["pr"])
                te.matmul(dotA_bank[0:16, :], hsum[:, ch, :],
                          prodA[:, par, :],
                          start=(ch == 0), stop=(ch == KC - 1),
                          skip_group_check=True)
                ins = te.matmul(dotB_bank[0:16, :], hsum[:, ch, :],
                                prodB[:, par, :],
                                start=(ch == 0), stop=(ch == KC - 1),
                                skip_group_check=True)
                ins.then_inc(*lb.bump("mm"))
                dotm2[(tch, ch)] = lb.mark("mm")
            lb.t(t_d2)

        def v_cd_fin(ve, tch=tch, ts=ts):
            if DBG_L1ATTN:
                lb.wait(ve, "st", lb.mark("st"))
            lb.wait(ve, "mm", dotm2[(tch, KC - 1)])
            lb.wait(ve, "evv", ZR)
            lb.ss(ve, "evv", ve.tensor_mul(eps1[0:16, 0:512],
                                           dotA_bank[0:16, :],
                                           rn16[0:16, ts]))
            lb.ss(ve, "evv", ve.tensor_mul(KCch[0:16, 0:512],
                                           dotB_bank[0:16, :],
                                           rn16[0:16, ts]))
            dot_guard[0] = lb.mark("evv")
            ve.engine_nop().then_inc(*lb.bump("evv"))
        lb.v(v_cd_fin)
        CDM = lb.mark("evv")

        def s_sa_cs(se, tch=tch, ts=ts, CDM=CDM):
            lb.wait(se, "evv", CDM)
            se.activation(sa[0:16, ts], eps1[0:16, 0:512], AF.Sigmoid
                          ).then_inc(*lb.bump("evs"))
            ins = se.activation(csb[0:16, ts], KCch[0:16, 0:512], AF.Exp)
            ins.then_inc(*lb.bump("evs"))
        lb.s(s_sa_cs)
        sacs_mark[tch] = lb.mark("evs")

    P4_done = ("evs", lb.mark("evs"))
    if DBG_L1ATTN:
        def y_dbg4(sy):
            lb.wait(sy, *P4_done)
            sy.dma_start(dbg_sa_h[:], sa[0:16, :]).then_inc(*lb.bump("st", 16))
            sy.dma_start(dbg_cs_h[:], csb[0:16, :]).then_inc(*lb.bump("st", 16))
        lb.y(y_dbg4)

    def v_cs(ve):
        lb.wait(ve, *P4_done)
        st0 = zeros[0:16, 0:1]
        for tch in range(NTCH):
            ts = slice(512 * tch, 512 * (tch + 1))
            init = st0 if tch == 0 else CSf[0:16, 512 * tch - 1:512 * tch]
            lb.ss(ve, "evv", ve.tensor_tensor_scan(
                CSf[0:16, ts], csb[0:16, ts], zeros[0:16, 0:512],
                init, ALU.add, ALU.add))
        ins = ve.tensor_copy(cs_sum[:, :], CSf[0:16, TC - 1:TC])
        ins.then_inc(*lb.bump("evv"))
    lb.v(v_cs)
    ag3 = emit_ag_prefix(lb, "s3", cs_sum, 16, 1, mpfx, cs_carry,
                         ("evv", lb.mark("evv")), agg3)

    def v_sc(ve):
        lb.wait(ve, *ag3)
        for tch in range(NTCH):
            ts = slice(512 * tch, 512 * (tch + 1))
            lb.ss(ve, "evv", ve.tensor_scalar_add(CSf[0:16, ts],
                                                  CSf[0:16, ts],
                                                  cs_carry[0:16, 0:1]))
            lb.ss(ve, "evv", ve.reciprocal(CSf[0:16, ts], CSf[0:16, ts]))
            lb.ss(ve, "evv", ve.tensor_mul(eps1[0:16, 0:512],
                                           csb[0:16, ts], CSf[0:16, ts]))
            lb.ss(ve, "evv", ve.tensor_mul(scb[0:16, ts], eps1[0:16, 0:512],
                                           nrm16[0:16, ts]))
        for tch in range(NTCH):
            ts = slice(512 * tch, 512 * (tch + 1))
            lb.ss(ve, "evv", ve.tensor_mul(scal[0:16, ts], si[0:16, ts],
                                           sa[0:16, ts]))
        ve.engine_nop().then_inc(*lb.bump("evv"))
    lb.v(v_sc)
    P5_done = ("evv", lb.mark("evv"))
    if DBG_L1ATTN:
        def y_dbg5(sy):
            lb.wait(sy, *P5_done)
            sy.dma_start(dbg_sc_h[:], scb[0:16, :]).then_inc(*lb.bump("st", 16))
            sy.dma_start(dbg_scal_h[:], scal[0:16, :]).then_inc(*lb.bump("st", 16))
        lb.y(y_dbg5)

    for tch in range(NTCH):
        ts = slice(512 * tch, 512 * (tch + 1))
        for ch in range(KC):
            hh = {}

            def t_bc3(te, tch=tch, ts=ts, ch=ch, hh=hh):
                if tch == 0 and ch == 0:
                    lb.wait(te, *P5_done)
                i1, b1 = banks.acquire(te)
                te.matmul(b1[:], hbc[:, 128 * ch:128 * (ch + 1)],
                          scal[0:16, ts], start=True, stop=True
                          ).then_inc(*lb.bump("mm"))
                hh["q"] = (i1, b1, lb.mark("mm"))
                i2, b2 = banks.acquire(te)
                te.matmul(b2[:], hbc[:, 128 * ch:128 * (ch + 1)],
                          scb[0:16, ts], start=True, stop=True
                          ).then_inc(*lb.bump("mm"))
                hh["v"] = (i2, b2, lb.mark("mm"))
            lb.t(t_bc3)

            def v_scale(ve, tch=tch, ts=ts, ch=ch, hh=hh):
                i1, b1, mk1 = hh["q"]
                lb.wait(ve, "mm", mk1)
                lb.ss(ve, "evv", ve.tensor_mul(q[:, ch, ts], q[:, ch, ts],
                                               b1[:]))
                banks.set_guard(i1, "evv", lb.mark("evv"))
                i2, b2, mk2 = hh["v"]
                lb.wait(ve, "mm", mk2)
                ins = ve.tensor_mul(v[:, ch, ts], v[:, ch, ts], b2[:])
                ins.then_inc(*lb.bump("evv"))
                banks.set_guard(i2, "evv", lb.mark("evv"))
            lb.v(v_scale)
    P6_done = ("evv", lb.mark("evv"))
    if DBG_L1ATTN:
        def y_dbg6(sy):
            lb.wait(sy, *P6_done)
            sy.dma_start(dbg_q_h[:], q[:]).then_inc(*lb.bump("st", 16))
        lb.y(y_dbg6)

    for nm in ("qsic", "ksoc", "prodB", "prodA",
               "eps1", "KCch", "sinb", "son", "scal", "scb", "csb", "sa",
               "si", "CSf"):
        lb.free_sb(nm)
    ktmp = lb.sb_scoped("ktmp", [128, KC, 128], BF16)
    vtmp = lb.sb_scoped("vtmp", [128, KC, 128], BF16)
    Ss = lb.sb_scoped("Ss", [128, 2, 128], BF16)
    agg4 = lb.sb_scoped("agg4", [128, 8, 128], F32)

    def v_kvinit(ve):
        lb.wait(ve, *P6_done)
        lb.ss(ve, "evv", ve.memset(kv_loc[:], 0.0))
        ins = ve.memset(kv_bf[:], 0.0)
        ins.then_inc(*lb.bump("evv"))
    lb.v(v_kvinit)
    KVI = lb.mark("evv")


    for tc in range(NTC):
        tsl = slice(128 * tc, 128 * (tc + 1))
        hh = {}
        for ch in range(KC):
            def t_tp(te, tc=tc, ch=ch, hh=hh, tsl=tsl):
                if tc == 0 and ch == 0:
                    lb.wait(te, "evv", KVI)
                i1, b1 = tbanks.acquire(te)
                te.transpose(b1[:, 0:128], k[:, ch, tsl], idb[:]
                             ).then_inc(*lb.bump("mm"))
                hh[("tk", ch)] = (i1, b1, lb.mark("mm"))
                i2, b2 = tbanks.acquire(te)
                te.transpose(b2[:, 0:128], v[:, ch, tsl], idb[:]
                             ).then_inc(*lb.bump("mm"))
                hh[("tv", ch)] = (i2, b2, lb.mark("mm"))
            lb.t(t_tp)

            def v_tp(ve, ch=ch, hh=hh):
                i1, b1, mk1 = hh[("tk", ch)]
                lb.wait(ve, "mm", mk1)
                ins = ve.tensor_copy(ktmp[:, ch, :], b1[:, 0:128])
                ins.then_inc(*lb.bump("evv"))
                tbanks.set_guard(i1, "evv", lb.mark("evv"))
                i2, b2, mk2 = hh[("tv", ch)]
                lb.wait(ve, "mm", mk2)
                ins = ve.tensor_copy(vtmp[:, ch, :], b2[:, 0:128])
                ins.then_inc(*lb.bump("evv"))
                tbanks.set_guard(i2, "evv", lb.mark("evv"))
            lb.v(v_tp)
        tp_mark = lb.mark("evv")

        for ch in range(KC):
            for hp in range(2):
                r0 = 64 * hp
                eh = {}

                def t_S(te, tc=tc, ch=ch, hp=hp, r0=r0, eh=eh, tsl=tsl,
                        tp_mark=tp_mark):
                    iS, bS = banks.acquire(te)
                    te.matmul(bS[:, 0:128], k[r0:r0 + 64, ch, tsl],
                              q[r0:r0 + 64, ch, tsl], start=True, stop=True
                              ).then_inc(*lb.bump("mm"))
                    eh["S"] = (iS, bS, lb.mark("mm"))
                lb.t(t_S)

                def v_mask(ve, ch=ch, hp=hp, eh=eh):
                    iS, bS, mk = eh["S"]
                    lb.wait(ve, "mm", mk)
                    ins = ve.tensor_mul(Ss[:, hp, :], bS[:, 0:128], tri[:, :])
                    ins.then_inc(*lb.bump("evv"))
                    banks.set_guard(iS, "evv", lb.mark("evv"))
                    eh["Ss"] = lb.mark("evv")
                lb.v(v_mask)

                def t_O(te, tc=tc, ch=ch, hp=hp, r0=r0, eh=eh, tsl=tsl,
                        tp_mark=tp_mark):
                    lb.wait(te, "evv", eh["Ss"])
                    lb.wait(te, "evv", tp_mark)
                    iO, bO = banks.acquire(te)
                    te.matmul(bO[0:64, 0:128], vtmp[:, ch, r0:r0 + 64],
                              Ss[:, hp, :], start=True, stop=False,
                              skip_group_check=True)
                    te.matmul(bO[0:64, 0:128], kv_bf[r0:r0 + 64, ch, :],
                              q[r0:r0 + 64, ch, tsl], start=False, stop=True,
                              skip_group_check=True).then_inc(*lb.bump("mm"))
                    eh["O"] = (iO, bO, lb.mark("mm"))
                    iC, bC = banks.acquire(te)
                    te.matmul(bC[0:64, 0:64], ktmp[:, ch, r0:r0 + 64],
                              vtmp[:, ch, r0:r0 + 64], start=True, stop=True
                              ).then_inc(*lb.bump("mm"))
                    eh["C"] = (iC, bC, lb.mark("mm"))
                lb.t(t_O)

                def v_O(ve, tc=tc, ch=ch, hp=hp, r0=r0, eh=eh, tsl=tsl):
                    iO, bO, mkO = eh["O"]
                    lb.wait(ve, "mm", mkO)
                    ins = ve.tensor_copy(ofm[r0:r0 + 64, ch, tsl],
                                         bO[0:64, 0:128])
                    ins.then_inc(*lb.bump("evv"))
                    banks.set_guard(iO, "evv", lb.mark("evv"))
                    iC, bC, mkC = eh["C"]
                    lb.wait(ve, "mm", mkC)
                    lb.ss(ve, "evv", ve.tensor_add(kv_loc[r0:r0 + 64, ch, :],
                                                   kv_loc[r0:r0 + 64, ch, :],
                                                   bC[0:64, 0:64]))
                    banks.set_guard(iC, "evv", lb.mark("evv"))
                    ins = ve.tensor_copy(kv_bf[r0:r0 + 64, ch, :],
                                         kv_loc[r0:r0 + 64, ch, :])
                    ins.then_inc(*lb.bump("evv"))
                lb.v(v_O)
    P7_done = ("evv", lb.mark("evv"))
    if DBG_L1ATTN:
        def y_dbg7(sy):
            lb.wait(sy, *P7_done)
            sy.dma_start(dbg_kv_h[:], kv_loc[:]).then_inc(*lb.bump("st", 16))
        lb.y(y_dbg7)

    ag4 = emit_ag_prefix(lb, "s4", kv_loc[:].rearrange("p c x -> p (c x)"),
                         128, 512, mpfx,
                         kv_carry[:].rearrange("p c x -> p (c x)"), P7_done,
                         agg4)

    def v_kvc(ve):
        lb.wait(ve, *ag4)
        ins = ve.tensor_copy(kv_bf[:], kv_carry[:])
        ins.then_inc(*lb.bump("evv"))
    lb.v(v_kvc)
    KVC = lb.mark("evv")

    for tc in range(NTC):
        tsl = slice(128 * tc, 128 * (tc + 1))
        for ch in range(KC):
            for hp in range(2):
                r0 = 64 * hp
                eh = {}

                def t_oc(te, tc=tc, ch=ch, hp=hp, r0=r0, eh=eh, tsl=tsl):
                    if tc == 0 and ch == 0 and hp == 0:
                        lb.wait(te, "evv", KVC)
                    iO, bO = banks.acquire(te)
                    te.matmul(bO[0:64, 0:128], kv_bf[r0:r0 + 64, ch, :],
                              q[r0:r0 + 64, ch, tsl], start=True, stop=True
                              ).then_inc(*lb.bump("mm"))
                    eh["O"] = (iO, bO, lb.mark("mm"))
                lb.t(t_oc)

                def v_oc(ve, tc=tc, ch=ch, hp=hp, r0=r0, eh=eh, tsl=tsl):
                    iO, bO, mk = eh["O"]
                    lb.wait(ve, "mm", mk)
                    ins = ve.tensor_add(ofm[r0:r0 + 64, ch, tsl],
                                        ofm[r0:r0 + 64, ch, tsl],
                                        bO[0:64, 0:128])
                    ins.then_inc(*lb.bump("evv"))
                    banks.set_guard(iO, "evv", lb.mark("evv"))
                lb.v(v_oc)
    P8_done = ("evv", lb.mark("evv"))
    if DBG_L1ATTN:
        def y_dbg8(sy):
            lb.wait(sy, *P8_done)
            sy.dma_start(dbg_ofm_h[:], ofm[:]).then_inc(*lb.bump("st", 16))
        lb.y(y_dbg8)

    for nm in ("agg4", "Ss", "vtmp", "ktmp", "v", "k"):
        lb.free_sb(nm)
    NLN = 256
    wo_sb = lb.sb_scoped("wo_sb", [128, KC, 1024], BF16)
    resid = lb.sb_scoped("resid", [128, KC, NLN], F32)
    ytile = lb.sb_scoped("ytile", [128, KC, NLN], F32)
    yout = lb.sb_scoped("yout", [128, KC, NLN], F32)
    zsq = lb.sb_scoped("zsq", [128, KC, NLN], F32)
    statrow = lb.sb_scoped("statrow", [1, 3 * NLN], F32)
    aux = {"onescol": onescol, "onesrow": onesrow, "zsq": zsq,
           "statrow": statrow}

    def g_wo(g):
        g.dma_start(wo_sb[:], wo_h.rearrange("(c p) n -> p c n", p=128)
                    ).then_inc(*lb.bump("xld", 16))
    lb.g(g_wo)
    WOLD = lb.mark("xld")

    for sti in range(TC // NLN):
        ts = slice(NLN * sti, NLN * (sti + 1))

        def y_resid(sy, sti=sti):
            lb.wait(sy, "evv", lb.mark("evv"))
            if io["stream_dep"] is not None:
                lb.wait(sy, *io["stream_dep"])
            sy.dma_start(resid[:], rearr_pct(xin_h, NLN * sti, NLN)
                         ).then_inc(*lb.bump("rld", 16))
        lb.y(y_resid)
        rld_mark = lb.mark("rld")

        for j in range(KC):
            eh = {}

            def t_op(te, sti=sti, j=j, eh=eh, ts=ts):
                if sti == 0 and j == 0:
                    lb.wait(te, *P8_done)
                    lb.wait(te, "xld", WOLD)
                bi, bt = banks.acquire(te)
                for c in range(KC):
                    ins = te.matmul(bt[:, 0:NLN],
                                    wo_sb[:, c, 128 * j:128 * (j + 1)],
                                    ofm[:, c, ts],
                                    start=(c == 0), stop=(c == KC - 1))
                ins.then_inc(*lb.bump("mm"))
                eh["m"] = lb.mark("mm")
                eh["b"] = (bi, bt)
            lb.t(t_op)

            def v_y(ve, sti=sti, j=j, eh=eh, rld_mark=rld_mark):
                if j == 0:
                    lb.wait(ve, "rld", rld_mark)
                lb.wait(ve, "mm", eh["m"])
                bi, bt = eh["b"]
                lb.ss(ve, "evv", ve.tensor_add(ytile[:, j, :], bt[:, 0:NLN],
                                               resid[:, j, :]))
                banks.set_guard(bi, "evv", lb.mark("evv"))
                ins = ve.tensor_scalar_add(ytile[:, j, :], ytile[:, j, :],
                                           bos[:, j:j + 1])
                ins.then_inc(*lb.bump("evv"))
            lb.v(v_y)
        z_dep = ("evv", lb.mark("evv"))

        ln_dep = emit_ln(lb, banks, aux, ytile, yout, z_dep, nt=NLN,
                         out_extra_deps=[("st", lb.mark("st"))])

        def y_store(sy, sti=sti, ln_dep=ln_dep):
            lb.wait(sy, *ln_dep)
            sy.dma_start(rearr_pct(xout_h, NLN * sti, NLN), yout[:]
                         ).then_inc(*lb.bump("st", 16))
        lb.y(y_store)

    return (lb.qual("st"), lb.mark("st"))


def host_l1attn_inputs(inputs, xT_slices):
    def pc(b):
        return np.ascontiguousarray(b.reshape(8, 128).T).astype(np.float32)
    wq = np.ascontiguousarray(inputs["L1_attn_Wq"]).astype(np.float32)
    wkk = np.ascontiguousarray(inputs["L1_attn_Wk"]).astype(np.float32)
    wv = np.ascontiguousarray(inputs["L1_attn_Wv"]).astype(np.float32)
    wo = np.ascontiguousarray(inputs["L1_attn_Wo"]).astype(np.float32)
    import ml_dtypes as _md
    hsum = np.zeros((128, 8, 16), np.float32)
    for ch in range(8):
        for p in range(128):
            hsum[p, ch, (128 * ch + p) // 64] = 1.0
    hsum = hsum.astype(_md.bfloat16)
    import ml_dtypes
    hbc = np.zeros((16, 1024), np.float32)
    for ch in range(8):
        for m in range(128):
            hbc[2 * ch + (m >= 64), 128 * ch + m] = 1.0
    hbc = hbc.astype(ml_dtypes.bfloat16)
    tri = np.tril(np.ones((128, 128), np.float32)).T.copy().astype(
        ml_dtypes.bfloat16)
    idb = np.eye(128, dtype=np.float32).astype(ml_dtypes.bfloat16)
    common = dict(Wq=wq, Wk=wkk, Wv=wv, Wo=wo,
                  bqpc=pc(inputs["L1_attn_bq"]), bkpc=pc(inputs["L1_attn_bk"]),
                  bvpc=pc(inputs["L1_attn_bv"]), bopc=pc(inputs["L1_attn_bo"]),
                  hsum=hsum, hbcast=hbc, trimask=tri, id128b=idb,
                  onescol=np.ones((128, 1), np.float32),
                  onesrow=np.ones((1, 128), np.float32))
    maps = []
    for c in range(NCORES):
        nrm = (np.arange(TC * c, TC * (c + 1), dtype=np.float32) + 1.0)
        nrm16 = np.ascontiguousarray(np.broadcast_to(nrm[None, :], (16, TC))
                                     ).astype(np.float32)
        mp = np.zeros((128, 8), np.float32)
        mp[:, :c] = 1.0
        m = dict(common, nrm16=nrm16, maskpfx=mp)
        if xT_slices is not None:
            m["xinT"] = np.ascontiguousarray(xT_slices[c])
        maps.append(m)
    return maps




def emit_ag_full(lb, tag, src_ap, P, X, gath_sb, deps):
    ag_in = lb.dram(f"agf_{tag}_in", [P, X])
    ag_out = lb.dram(f"agf_{tag}_out", [8 * P, X])

    def g_ag(g):
        lb.wait(g, *deps)
        g.dma_start(ag_in[:], src_ap).then_inc(*lb.bump("agd", 16))
        lb.wait(g, "agd", lb.mark("agd"))
        g.collective_compute(
            "AllGather", ALU.bypass, RG,
            ins=[ag_in.opt()], outs=[ag_out.opt()]).then_inc(*lb.bump("cc"))
        lb.wait(g, "cc", lb.mark("cc"))
        g.dma_start(gath_sb[:, :, 0:X],
                    ag_out.rearrange("(r p) x -> p r x", p=P),
                    allow_slow_non_contiguous=X <= 4
                    ).then_inc(*lb.bump("agd", 16))
    lb.g(g_ag)
    return ("agd", lb.mark("agd"))


def build_l0attn():
    lb = LB()
    xq_h = lb.inp("xqT", [D, TC + 1])
    ocol_h = lb.inp("onescol", [128, 1])
    orow_h = lb.inp("onesrow", [1, 128])
    xout_h = lb.out("xoutT", [D, TC])
    lb.start_block(["shared"], allow_lp=True)
    onescol = lb.sb("sh_onescol", [128, 1], F32)
    onesrow = lb.sb("sh_onesrow", [1, 128], F32)

    def y_shc(sy):
        sy.dma_start(onescol[:, :], ocol_h[:]).then_inc(*lb.bump("@shared", 16))
        sy.dma_start(onesrow[:, :], orow_h[:]).then_inc(*lb.bump("@shared", 16))
    lb.y(y_shc)
    io = {"xq": xq_h, "xout": xout_h, "onescol": onescol,
          "onesrow": onesrow, "shc_dep": ("@shared", lb.mark("@shared"))}
    dep = emit_l0attn(lb, io)

    def y_final(sy):
        lb.wait(sy, *dep)
    lb.y(y_final)
    return lb.done()


def emit_l0attn(lb, io):
    xq_h = io["xq"]
    xout_h = io["xout"]
    onescol = io["onescol"]
    onesrow = io["onesrow"]
    wqT_h = lb.inp("WqT", [D, D])
    wvT_h = lb.inp("WvT", [D, D])
    wcT_h = lb.inp("WcT", [D, D])
    wp1_h = lb.inp("Wp1", [D, D])
    wp2_h = lb.inp("Wp2", [D, D])
    idn_h = lb.inp("identcol", [128, KC])
    fm_h = lb.inp("fmask", [128, 2])
    ohp_h = lb.inp("onehotprev", [128, 8])
    wsel_h = lb.inp("wsel", [128, 3, 16])
    if DBG_L0ATTN:
        dbg_p1_h = lb.out("dbg_p1", [128, KC, D], BF16)
        dbg_q_h = lb.out("dbg_q", [128, KC, TC + 1], BF16)
        dbg_h_h = lb.out("dbg_h", [128, KC, TC + 2], BF16)

    p1t = lb.sb("p1t", [128, KC, D], BF16)
    p2t = lb.sb("p2t", [128, KC, D], BF16)
    idn = lb.sb("idn_s", [128, KC], F32)
    fmk = lb.sb("fmk_s", [128, 2], F32)
    ohp = lb.sb("ohp_s", [128, 8], F32)
    wsel = lb.sb("wsel_s", [128, 3, 16], F32)
    wk_t = [lb.sb(f"wkt{i}", [128, KC, 128], BF16) for i in range(3)]
    rowx = lb.sb("rowx", [1, 2 * 512], F32)
    agrow = lb.sb("agrow", [128, 8, 16], F32)
    statrow = lb.sb("statrow", [1, 3 * 512], F32)

    q = lb.sb_scoped("q", [128, KC, TC + 1], BF16)

    pb = [lb.ps(f"pb{i}", [128, 512]) for i in range(8)]

    banks = Banks(lb, pb[0:6])
    statbank, bcbank = pb[6], pb[7]
    statguard = [None]
    bcguard = [None]
    wbuf = WBuf(lb, wk_t, "wkt")

    def y_const(sy):
        lb.wait(sy, *io["shc_dep"])
        for dst, srcc in [(idn[:, :], idn_h[:]), (fmk[:, :], fm_h[:]),
                          (ohp[:, :], ohp_h[:]), (wsel[:], wsel_h[:])]:
            sy.dma_start(dst, srcc).then_inc(*lb.bump("cld", 16))
    lb.y(y_const)
    CLD = lb.mark("cld")

    xbf = lb.sb_scoped("xbf", [128, KC, TC + 1], BF16)

    def g_x(g):
        g.dma_start(xbf[:], rearr_pct(xq_h, 0, TC + 1)
                    ).then_inc(*lb.bump("xld", 16))
    lb.g(g_x)

    plan = [(rearr_pcn(wqT_h, 128 * j, 128), KC) for j in range(KC)]
    pumped = [0]

    def pump(n):
        n = min(n, len(plan) - pumped[0])
        if n <= 0:
            return

        def g_loads(g):
            for _ in range(n):
                ap, kch = plan[pumped[0]]
                wbuf.emit_load(g, ap, kch)
                pumped[0] += 1
        lb.g(g_loads)
    pump(3)

    NQ = TC + 1
    for j in range(KC):
        hh = {}

        def t_q(te, j=j, hh=hh):
            if j == 0:
                lb.wait(te, "cld", CLD)
                lb.wait(te, "xld", 16)
            w = wbuf.consume(te, j)
            for f in range(5):
                f0 = 512 * f
                f1 = min(NQ, f0 + 512)
                bi, bt = banks.acquire(te)
                for c in range(KC):
                    ins = te.matmul(bt[:, 0:f1 - f0], w[:, c, :],
                                    xbf[:, c, f0:f1],
                                    start=(c == 0), stop=(c == KC - 1))
                ins.then_inc(*lb.bump("mm"))
                hh[f] = (bi, bt, lb.mark("mm"), f0, f1)
            wbuf.release_at(j, "mm", lb.mark("mm"))
        lb.t(t_q)

        def v_q(ve, j=j, hh=hh):
            for f in range(5):
                bi, bt, mk, f0, f1 = hh[f]
                lb.wait(ve, "mm", mk)
                ins = ve.tensor_copy(q[:, j, f0:f1], bt[:, 0:f1 - f0])
                ins.then_inc(*lb.bump("evv"))
                banks.set_guard(bi, "evv", lb.mark("evv"))
        lb.v(v_q)
        pump(1)

    def v_s0(ve):
        lb.wait(ve, "evv", lb.mark("evv"))
        lb.ss(ve, "evv", ve.tensor_scalar_mul(q[:, :, 0:1].rearrange(
            "p c x -> p (c x)"), q[:, :, 0:1].rearrange("p c x -> p (c x)"),
            fmk[:, 1:2]))
        ins = ve.scalar_tensor_tensor(
            q[:, :, 0:1].rearrange("p c x -> p (c x)"), idn[:, :],
            fmk[:, 0:1], q[:, :, 0:1].rearrange("p c x -> p (c x)"),
            ALU.mult, ALU.add)
        ins.then_inc(*lb.bump("evv"))
    lb.v(v_s0)
    Q_done = ("evv", lb.mark("evv"))
    lb.free_sb("xbf")

    KCD = KC * D
    PAY = KCD + 8
    NS_TILES = [t for t in range(16) if not (t % 2 == 0 and t // 2 >= 4)]
    NPAY = 512 * len(NS_TILES) + 8

    Gb = [lb.sb_scoped(f"nsG{m}", [128, PAY], F32R) for m in range(2)]
    stg = [lb.sb_scoped(f"nsStg{m}", [128, 2048], F32) for m in range(2)]
    nXr = [lb.sb_scoped(f"nsX{m}", [128, D], F32) for m in range(2)]
    nXb = [lb.sb_scoped(f"nsXb{m}", [128, D], F32R) for m in range(2)]
    nYr = [lb.sb_scoped(f"nsY{m}", [128, D], F32R) for m in range(2)]
    ntrT = [lb.sb_scoped(f"nsT{m}", [128, KC, 128], F32R) for m in range(2)]
    trTf = lb.sb_scoped("nsTf", [128, KC, 128], F32)
    nscl = [lb.sb_scoped(f"nsScl{m}", [128, 4], F32) for m in range(2)]
    nsqr = [lb.sb_scoped(f"nsSq{m}", [128, 8], F32) for m in range(2)]
    srow = lb.sb_scoped("srow", [1, 8], F32)
    idf = lb.sb_scoped("idf", [128, 128], F32)
    idfr = lb.sb_scoped("idfr", [128, 128], F32R)

    wr1_h = lb.inp("Wp1r", [128, D])
    wr2_h = lb.inp("Wp2r", [128, D])
    idf_h = lb.inp("ident128f", [128, 128])
    wr_h = [wr1_h, wr2_h]
    ar_in = [lb.dram(f"ns_ari{m}", [128, NPAY]) for m in range(2)]
    ar_out = [lb.dram(f"ns_aro{m}", [128, NPAY]) for m in range(2)]
    nag_in = [lb.dram(f"ns_agi{m}", [128, D]) for m in range(2)]
    nag_out = [lb.dram(f"ns_ago{m}", [8 * 128, D]) for m in range(2)]

    NSC0 = lb.mark("cld")

    def y_nsld(sy):
        lb.wait(sy, "cld", NSC0)
        sy.dma_start(idf[:, :], idf_h[:]).then_inc(*lb.bump("cld", 16))
        for m in range(2):
            sy.dma_start(nXr[m][:, :], wr_h[m][:]).then_inc(*lb.bump("cld", 16))
    lb.y(y_nsld)

    def g_nsld(g):
        lb.wait(g, "cld", NSC0)
        g.dma_start(idfr[:, :], idf_h[:]).then_inc(*lb.bump("cldg", 16))
        for m in range(2):
            g.dma_start(nXb[m][:, :], wr_h[m][:]
                        ).then_inc(*lb.bump("cldg", 16))
    lb.g(g_nsld)
    NSLD = lb.mark("cld")
    NSLDG = lb.mark("cldg")

    nst = [{"xdep": ("cld", NSLD), "cc_mk": None, "ari_busy": None,
            "aro_busy": None, "trT_busy": None, "stg_mk": [None, None],
            "scl_mk": None, "yr_free": None}
           for _ in range(2)]

    def emit_ns_h1(m, it):
        s = nst[m]
        if it == 0:
            def s_sq(se, s=s, m=m):
                lb.wait(se, *s["xdep"])
                lb.wait(se, "cldg", NSLDG)
                for h2 in range(2):
                    sl = slice(512 * h2, 512 * (h2 + 1))
                    ins = se.activation(nYr[m][:, sl], nXr[m][:, sl],
                                        AF.Square)
                ins.then_inc(*lb.bump("sq"))
            lb.s(s_sq)
            SQ = lb.mark("sq")

            def v_sq(ve, m=m, SQ=SQ):
                lb.wait(ve, "sq", SQ)
                lb.ss(ve, "evv", ve.memset(nsqr[m][:, :], 0.0))
                ins = ve.reduce_sum(nsqr[m][:, 0:1], nYr[m][:, :], axis=AX.X)
                ins.then_inc(*lb.bump("evv"))
            lb.v(v_sq)
            SQV = lb.mark("evv")

            def y_sq(sy, m=m, SQV=SQV):
                lb.wait(sy, "evv", SQV)
                sy.dma_start(ar_in[m][:, NPAY - 8:NPAY], nsqr[m][:, :]
                             ).then_inc(*lb.bump(f"nstg{m}a", 16))
            lb.y(y_sq)

        for grp in range(6):
            bigslot = grp % 2
            hh = {}
            for k in range(2):
                t_i = NS_TILES[grp * 2 + k]
                cc, h2 = divmod(t_i, 2)
                first = (grp == 0 and k == 0)

                def t_cr(te, s=s, m=m, cc=cc, h2=h2, hh=hh, k=k,
                         first=first, it=it):
                    if first:
                        lb.wait(te, *s["xdep"])
                        if it == 0:
                            lb.wait(te, "cldg", NSLDG)
                    bi, bt = banks.acquire(te)
                    te.matmul(bt[:], nXb[m][:, 128 * cc:128 * (cc + 1)],
                              nXb[m][:, 512 * h2:512 * (h2 + 1)],
                              start=True, stop=True).then_inc(*lb.bump("mm"))
                    hh[k] = (bi, bt, lb.mark("mm"))
                lb.t(t_cr)

                def v_cr(ve, s=s, m=m, bigslot=bigslot, k=k, hh=hh):
                    bi, bt, mk = hh[k]
                    lb.wait(ve, "mm", mk)
                    if k == 0 and s["stg_mk"][bigslot] is not None:
                        lb.wait(ve, f"nstg{m}" + "ab"[bigslot],
                                s["stg_mk"][bigslot])
                    ins = ve.tensor_copy(
                        stg[m][:, 1024 * bigslot + 512 * k:
                               1024 * bigslot + 512 * (k + 1)], bt[:])
                    ins.then_inc(*lb.bump("evv"))
                    banks.set_guard(bi, "evv", lb.mark("evv"))
                lb.v(v_cr)
            EVG = lb.mark("evv")

            def body_st(eng, s=s, m=m, grp=grp, bigslot=bigslot, EVG=EVG):
                lb.wait(eng, "evv", EVG)
                if grp in (0, 1) and s["ari_busy"] is not None:
                    lb.wait(eng, f"nscc{m}", s["ari_busy"])
                eng.dma_start(ar_in[m][:, 1024 * grp:1024 * (grp + 1)],
                              stg[m][:, 1024 * bigslot:1024 * (bigslot + 1)]
                              ).then_inc(*lb.bump(f"nstg{m}" + "ab"[bigslot],
                                                  16))
            if grp % 2 == 0:
                lb.y(body_st)
            else:
                lb.g(body_st)
            s["stg_mk"][bigslot] = lb.mark(f"nstg{m}" + "ab"[bigslot])
        ALLST = (lb.mark(f"nstg{m}a"), lb.mark(f"nstg{m}b"))

        def g_ar(g, s=s, m=m, it=it, ALLST=ALLST):
            lb.wait(g, f"nstg{m}a", ALLST[0])
            lb.wait(g, f"nstg{m}b", ALLST[1])
            if s["aro_busy"] is not None:
                lb.wait(g, f"nsgl{m}", s["aro_busy"])
            g.collective_compute(
                "AllReduce", ALU.add, RG,
                ins=[ar_in[m].opt()], outs=[ar_out[m].opt()]
            ).then_inc(*lb.bump(f"nscc{m}"))
        lb.g(g_ar)
        s["cc_mk"] = lb.mark(f"nscc{m}")
        s["ari_busy"] = s["cc_mk"]

    def emit_ns_h2(m, it, coef):
        a_c, b_c, c_c = coef
        s = nst[m]
        quintic = (c_c != 0.0)

        def g_gl(g, s=s, m=m):
            lb.wait(g, f"nscc{m}", s["cc_mk"])
            g.dma_start(Gb[m][:, 0:4096], ar_out[m][:, 0:4096]
                        ).then_inc(*lb.bump(f"nsgl{m}", 16))
            g.dma_start(Gb[m][:, 4096:8192]
                        .rearrange("p (c n) -> p c n", n=1024)[:, :, 512:1024],
                        ar_out[m][:, 4096:6144]
                        .rearrange("p (c n) -> p c n", n=512)
                        ).then_inc(*lb.bump(f"nsgl{m}", 16))
            g.dma_start(Gb[m][:, KCD:PAY], ar_out[m][:, NPAY - 8:NPAY]
                        ).then_inc(*lb.bump(f"nsgl{m}", 16))
        lb.g(g_gl)
        GMK = lb.mark(f"nsgl{m}")
        s["aro_busy"] = GMK

        hhr = {}
        for idx, (cc, bl) in enumerate((cc, bl) for cc in range(4, 8)
                                       for bl in range(4)):
            def t_rc(te, cc=cc, bl=bl, idx=idx, hhr=hhr, m=m, GMK=GMK):
                if idx == 0:
                    lb.wait(te, f"nsgl{m}", GMK)
                bi, bt = banks.acquire(te)
                te.matmul(bt[:, 0:128],
                          Gb[m][:, 1024 * bl + 128 * cc:
                                1024 * bl + 128 * cc + 128],
                          idfr[:, :], start=True, stop=True
                          ).then_inc(*lb.bump("mm"))
                hhr[idx] = (bi, bt, lb.mark("mm"))
            lb.t(t_rc)

            def v_rc(ve, idx=idx, cc=cc, bl=bl, hhr=hhr, m=m):
                bi, bt, mk = hhr[idx]
                lb.wait(ve, "mm", mk)
                ins = ve.tensor_copy(
                    Gb[m][:, 1024 * cc + 128 * bl:1024 * cc + 128 * bl + 128],
                    bt[:, 0:128])
                ins.then_inc(*lb.bump("evv"))
                banks.set_guard(bi, "evv", lb.mark("evv"))
            lb.v(v_rc)
        G_READY = lb.mark("evv")

        if it == 0:
            def v_tl(ve, m=m, GMK=GMK):
                lb.wait(ve, f"nsgl{m}", GMK)
                ins = ve.tensor_copy(nsqr[m][:, :], Gb[m][:, KCD:PAY])
                ins.then_inc(*lb.bump("evv"))
            lb.v(v_tl)
            VTL = lb.mark("evv")

            def t_s2(te, s=s, m=m, VTL=VTL):
                if statguard[0] is not None:
                    lb.wait(te, "evv", statguard[0])
                lb.wait(te, "evv", VTL)
                te.matmul(statbank[0:1, 0:8], onescol[:, :],
                          nsqr[m][:, :],
                          start=True, stop=True).then_inc(*lb.bump("mm"))
            lb.t(t_s2)
            TS2 = lb.mark("mm")

            def v_s2(ve, TS2=TS2):
                lb.wait(ve, "mm", TS2)
                lb.ss(ve, "evv", ve.tensor_copy(srow[0:1, 0:1],
                                                statbank[0:1, 0:1]))
                statguard[0] = lb.mark("evv")
                ins = ve.reciprocal(srow[0:1, 1:2], srow[0:1, 0:1])
                ins.then_inc(*lb.bump("evv"))
            lb.v(v_s2)
            VS2 = lb.mark("evv")

            def s_rt(se, VS2=VS2):
                lb.wait(se, "evv", VS2)
                ins = se.activation(srow[0:1, 2:3], srow[0:1, 0:1], AF.Sqrt)
                ins.then_inc(*lb.bump("sq"))
            lb.s(s_rt)
            SRT = lb.mark("sq")

            def v_s3(ve, SRT=SRT, a_c=a_c, b_c=b_c, c_c=c_c):
                lb.wait(ve, "sq", SRT)
                lb.ss(ve, "evv", ve.reciprocal(srow[0:1, 3:4], srow[0:1, 2:3]))
                lb.ss(ve, "evv", ve.tensor_mul(srow[0:1, 4:5], srow[0:1, 3:4],
                                               srow[0:1, 1:2]))
                lb.ss(ve, "evv", ve.tensor_mul(srow[0:1, 5:6], srow[0:1, 4:5],
                                               srow[0:1, 1:2]))
                lb.ss(ve, "evv", ve.tensor_scalar_mul(srow[0:1, 3:4],
                                                      srow[0:1, 3:4], a_c))
                lb.ss(ve, "evv", ve.tensor_scalar_mul(srow[0:1, 4:5],
                                                      srow[0:1, 4:5], b_c))
                ins = ve.tensor_scalar_mul(srow[0:1, 5:6], srow[0:1, 5:6], c_c)
                ins.then_inc(*lb.bump("evv"))
            lb.v(v_s3)
            VS3 = lb.mark("evv")

            def t_bc(te, VS3=VS3):
                if bcguard[0] is not None:
                    lb.wait(te, "evv", bcguard[0])
                lb.wait(te, "evv", VS3)
                te.matmul(bcbank[:, 0:3], onesrow[:, :], srow[0:1, 3:6],
                          start=True, stop=True).then_inc(*lb.bump("mm"))
            lb.t(t_bc)
            TBC = lb.mark("mm")

            def v_bc(ve, m=m, TBC=TBC):
                lb.wait(ve, "mm", TBC)
                ins = ve.tensor_copy(nscl[m][:, 0:3], bcbank[:, 0:3])
                ins.then_inc(*lb.bump("evv"))
                bcguard[0] = lb.mark("evv")
            lb.v(v_bc)
            s["scl_mk"] = lb.mark("evv")

        def emit_tr(src, src_dep=None, s=s, m=m):
            for cc in range(KC):
                hh = {}

                def t_tp(te, cc=cc, hh=hh, src=src, s=s, src_dep=src_dep):
                    if cc == 0 and src_dep is not None:
                        lb.wait(te, *src_dep)
                    if cc == 0 and s["trT_busy"] is not None:
                        lb.wait(te, "mm", s["trT_busy"])
                    bi, bt = banks.acquire(te)
                    te.matmul(bt[:, 0:128],
                              src[:, 128 * cc:128 * (cc + 1)],
                              idfr[:, :], start=True, stop=True
                              ).then_inc(*lb.bump("mm"))
                    hh["m"] = (bi, bt, lb.mark("mm"))
                lb.t(t_tp)

                def v_tp(ve, cc=cc, hh=hh, m=m):
                    bi, bt, mk = hh["m"]
                    lb.wait(ve, "mm", mk)
                    ins = ve.tensor_copy(ntrT[m][:, cc, :], bt[:, 0:128])
                    ins.then_inc(*lb.bump("evv"))
                    banks.set_guard(bi, "evv", lb.mark("evv"))
                lb.v(v_tp)
            return ("evv", lb.mark("evv"))

        def emit_prod(dep, need_g):
            out = {}
            for h2 in range(2):
                hh = {}

                def t_m(te, h2=h2, hh=hh, dep=dep, need_g=need_g, m=m):
                    if h2 == 0:
                        lb.wait(te, *dep)
                        if need_g:
                            lb.wait(te, "evv", G_READY)
                    bi, bt = banks.acquire(te)
                    for cc in range(KC):
                        ins = te.matmul(
                            bt[:], ntrT[m][:, cc, :],
                            Gb[m][:, 1024 * cc + 512 * h2:
                                  1024 * cc + 512 * (h2 + 1)],
                            start=(cc == 0), stop=(cc == KC - 1))
                    ins.then_inc(*lb.bump("mm"))
                    hh["m"] = (bi, bt, lb.mark("mm"))
                lb.t(t_m)
                out[h2] = hh
            return out

        dep = emit_tr(nXb[m])
        xr_snap_done = dep
        yh = emit_prod(dep, True)
        s["trT_busy"] = lb.mark("mm")

        if not quintic:
            for h2 in range(2):
                def v_up(ve, h2=h2, s=s, m=m, a_c=a_c, b_c=b_c, yh=yh,
                         xr_snap_done=xr_snap_done):
                    bi, bt, mk = yh[h2]["m"]
                    lb.wait(ve, "mm", mk)
                    if h2 == 0:
                        lb.wait(ve, *xr_snap_done)
                    sl = slice(512 * h2, 512 * (h2 + 1))
                    lb.ss(ve, "evv", ve.tensor_scalar_mul(
                        nXr[m][:, sl], nXr[m][:, sl], a_c))
                    lb.ss(ve, "evv", ve.scalar_tensor_tensor(
                        nXr[m][:, sl], bt[:], b_c, nXr[m][:, sl],
                        ALU.mult, ALU.add))
                    ins = ve.tensor_copy(nXb[m][:, sl], nXr[m][:, sl])
                    ins.then_inc(*lb.bump("evv"))
                    banks.set_guard(bi, "evv", lb.mark("evv"))
                lb.v(v_up)
            s["xdep"] = ("evv", lb.mark("evv"))
            return

        for h2 in range(2):
            def v_ev(ve, h2=h2, m=m, s=s, yh=yh):
                bi, bt, mk = yh[h2]["m"]
                lb.wait(ve, "mm", mk)
                if h2 == 0 and s["yr_free"] is not None:
                    lb.wait(ve, "evv", s["yr_free"])
                ins = ve.tensor_copy(nYr[m][:, 512 * h2:512 * (h2 + 1)],
                                     bt[:])
                ins.then_inc(*lb.bump("evv"))
                banks.set_guard(bi, "evv", lb.mark("evv"))
            lb.v(v_ev)
        ydep = ("evv", lb.mark("evv"))

        dep = emit_tr(nYr[m], src_dep=ydep)
        th = emit_prod(dep, False)
        s["trT_busy"] = lb.mark("mm")

        for h2 in range(2):
            def v_up(ve, h2=h2, s=s, m=m, it=it, a_c=a_c, b_c=b_c, c_c=c_c,
                     th=th, xr_snap_done=xr_snap_done):
                bi, bt, mk = th[h2]["m"]
                lb.wait(ve, "mm", mk)
                if h2 == 0:
                    lb.wait(ve, *xr_snap_done)
                    if it == 0:
                        lb.wait(ve, "evv", s["scl_mk"])
                        lb.wait(ve, "cldg", NSLDG)
                sl = slice(512 * h2, 512 * (h2 + 1))
                if it == 0:
                    lb.ss(ve, "evv", ve.tensor_scalar_mul(
                        nXr[m][:, sl], nXr[m][:, sl], nscl[m][:, 0:1]))
                    lb.ss(ve, "evv", ve.tensor_scalar_mul(
                        nYr[m][:, sl], nYr[m][:, sl], nscl[m][:, 1:2]))
                    lb.ss(ve, "evv", ve.tensor_add(
                        nXr[m][:, sl], nXr[m][:, sl], nYr[m][:, sl]))
                    lb.ss(ve, "evv", ve.tensor_scalar_mul(
                        nYr[m][:, sl], bt[:], nscl[m][:, 2:3]))
                    lb.ss(ve, "evv", ve.tensor_add(
                        nXr[m][:, sl], nXr[m][:, sl], nYr[m][:, sl]))
                    ins = ve.tensor_copy(nXb[m][:, sl], nXr[m][:, sl])
                else:
                    lb.ss(ve, "evv", ve.tensor_scalar_mul(
                        nXr[m][:, sl], nXr[m][:, sl], a_c))
                    lb.ss(ve, "evv", ve.scalar_tensor_tensor(
                        nXr[m][:, sl], nYr[m][:, sl], b_c, nXr[m][:, sl],
                        ALU.mult, ALU.add))
                    lb.ss(ve, "evv", ve.scalar_tensor_tensor(
                        nXr[m][:, sl], bt[:], c_c, nXr[m][:, sl],
                        ALU.mult, ALU.add))
                ins = ve.tensor_copy(nXb[m][:, sl], nXr[m][:, sl])
                ins.then_inc(*lb.bump("evv"))
                banks.set_guard(bi, "evv", lb.mark("evv"))
            lb.v(v_up)
        s["xdep"] = ("evv", lb.mark("evv"))
        s["yr_free"] = lb.mark("evv")

    trtf_busy = [None]

    def emit_ns_final(m, out_pT):
        s = nst[m]

        def emit_tr_f(src, s=s, m=m):
            for cc in range(KC):
                hh = {}

                def t_tp(te, cc=cc, hh=hh, src=src, s=s):
                    if cc == 0:
                        lb.wait(te, *s["xdep"])
                        if s["trT_busy"] is not None:
                            lb.wait(te, "mm", s["trT_busy"])
                        if trtf_busy[0] is not None:
                            lb.wait(te, "agd", trtf_busy[0])
                    bi, bt = banks.acquire(te)
                    te.transpose(bt[:, 0:128],
                                 src[:, 128 * cc:128 * (cc + 1)],
                                 idf[:, :]).then_inc(*lb.bump("mm"))
                    hh["m"] = (bi, bt, lb.mark("mm"))
                lb.t(t_tp)

                def v_tp(ve, cc=cc, hh=hh, m=m):
                    bi, bt, mk = hh["m"]
                    lb.wait(ve, "mm", mk)
                    ins = ve.tensor_copy(trTf[:, cc, :], bt[:, 0:128])
                    ins.then_inc(*lb.bump("evv"))
                    banks.set_guard(bi, "evv", lb.mark("evv"))
                lb.v(v_tp)
            return ("evv", lb.mark("evv"))

        dep = emit_tr_f(nXr[m])

        def g_ag(g, m=m, dep=dep):
            lb.wait(g, *dep)
            g.dma_start(nag_in[m][:],
                        trTf[:].rearrange("p c n -> p (c n)")
                        ).then_inc(*lb.bump("agd", 16))
            trtf_busy[0] = lb.mark("agd")
            lb.wait(g, "agd", lb.mark("agd"))
            g.collective_compute(
                "AllGather", ALU.bypass, RG,
                ins=[nag_in[m].opt()], outs=[nag_out[m].opt()]
            ).then_inc(*lb.bump("cc"))
            lb.wait(g, "cc", lb.mark("cc"))
            for r in range(8):
                g.dma_start(out_pT[:, :, 128 * r:128 * (r + 1)],
                            nag_out[m][128 * r:128 * (r + 1), :]
                            .rearrange("p (c n) -> p c n", n=128)
                            ).then_inc(*lb.bump("agd", 16))
        lb.g(g_ag)
        return ("agd", lb.mark("agd"))

    if DBG_NS:
        dbg_nsg_h = lb.out("dbg_nsg", [128, PAY])
        dbg_srow_h = lb.out("dbg_srow", [1, 8])
        dbg_scl_h = lb.out("dbg_scl", [128, 4])
        dbg_nsx_h = lb.out("dbg_nsx", [128, len(NS_SCHED), D])
        dbg_nsg2_h = lb.out("dbg_nsg2", [128, PAY])
        dbg_nsy_h = lb.out("dbg_nsy", [128, D])
        dbg_nst_h = lb.out("dbg_nst", [128, KC * 128])
        dbg_ari_h = lb.out("dbg_ari", [128, PAY])

    bg = io.get("bg_pump")

    def bgp(n):
        if bg is None:
            return

        def g_bg(g, n=n):
            bg(g, n)
        lb.g(g_bg)

    with lb.nc.named_scope("l0a_ns"):
        for m in range(2):
            emit_ns_h1(m, 0)
            bgp(3)
        for it in range(len(NS_SCHED)):
            for m in range(2):
                emit_ns_h2(m, it, NS_SCHED[it])
                bgp(3)
                if DBG_NS and m == 0:
                    def y_dbg(sy, it=it):
                        lb.wait(sy, *nst[0]["xdep"])
                        sy.dma_start(dbg_nsx_h[:, it, :], nXr[0][:, :]
                                     ).then_inc(*lb.bump("agd", 16))
                        if it == DBG_NS_IT:
                            sy.dma_start(dbg_ari_h[:], ar_in[0][:, :]
                                         ).then_inc(*lb.bump("agd", 16))
                            sy.dma_start(dbg_nsg2_h[:], Gb[0][:, :]
                                         ).then_inc(*lb.bump("agd", 16))
                            sy.dma_start(dbg_nsy_h[:], nYr[0][:, :]
                                         ).then_inc(*lb.bump("agd", 16))
                            sy.dma_start(dbg_nst_h[:],
                                         ntrT[0][:].rearrange("p c n -> p (c n)")
                                         ).then_inc(*lb.bump("agd", 16))
                        if it == 0:
                            sy.dma_start(dbg_nsg_h[:], Gb[0][:, :]
                                         ).then_inc(*lb.bump("agd", 16))
                            sy.dma_start(dbg_srow_h[:], srow[0:1, :]
                                         ).then_inc(*lb.bump("agd", 16))
                            sy.dma_start(dbg_scl_h[:], nscl[0][:, :]
                                         ).then_inc(*lb.bump("agd", 16))
                    lb.y(y_dbg)
                if it + 1 < len(NS_SCHED):
                    emit_ns_h1(m, it + 1)
                    bgp(3)
        p1_dep = emit_ns_final(0, p1t)
        p2_dep = emit_ns_final(1, p2t)

    for nm in ("idfr", "idf", "srow", "nsSq1", "nsSq0", "nsScl1", "nsScl0",
               "nsTf", "nsT1", "nsT0", "nsY1", "nsY0", "nsXb1", "nsXb0",
               "nsX1", "nsX0", "nsStg1", "nsStg0", "nsG1", "nsG0"):
        lb.free_sb(nm)

    zsq = lb.sb_scoped("zsq", [128, KC, 512], F32)
    resB = lb.sb_scoped("resB", [128, KC, 2050], BF16)
    elems = lb.sb_scoped("elems", [128, KC, 2047], BF16)
    resA = lb.sb_scoped("resA", [128, KC, 1026], BF16)
    e16g = lb.sb_scoped("e16g", [128, KC, 16], BF16)
    sc16 = lb.sb_scoped("sc16", [128, KC, 64], BF16)
    agx = lb.sb_scoped("agx", [128, 8, 8], F32)
    agx16 = lb.sb_scoped("agx16", [128, 8, 16], F32)
    slotrow = lb.sb_scoped("slotrow", [128, 8, 1], F32)
    slotrow2 = lb.sb_scoped("slotrow2", [128, 8, 1], F32)

    LVL_OFF = {}
    off = 0
    npc = 1024
    while npc >= 1:
        LVL_OFF[npc] = off
        off += npc
        npc //= 2

    def lvl_ap(npc):
        o = LVL_OFF[npc]
        return elems[:, :, o:o + npc]

    def emit_combine(ncols, aT_fn, bT_fn, out_fn, dep, sdep=None):
        NB = ceil_div(ncols, 512)
        last = None
        for cb in range(NB):
            c0 = 512 * cb
            c1 = min(ncols, c0 + 512)
            w = c1 - c0
            jm = {}
            for j in range(KC):
                hh = {}

                def t_c(te, j=j, hh=hh, c0=c0, c1=c1, w=w, cb=cb, dep=dep,
                        sdep=sdep):
                    if j == 0 and cb == 0:
                        lb.wait(te, *dep)
                        if sdep is not None:
                            lb.wait(te, *sdep)
                    bi, bt = banks.acquire(te)
                    for cc in range(KC):
                        te.matmul(bt[:, 0:w], p1t[:, cc, 128 * j:128 * (j + 1)],
                                  aT_fn(cc, c0, c1),
                                  start=(cc == 0), stop=False,
                                  skip_group_check=True)
                        ins = te.matmul(bt[:, 0:w],
                                        p2t[:, cc, 128 * j:128 * (j + 1)],
                                        bT_fn(cc, c0, c1),
                                        start=False, stop=(cc == KC - 1),
                                        skip_group_check=True)
                    ins.then_inc(*lb.bump("mm"))
                    hh["m"] = (bi, bt, lb.mark("mm"))
                lb.t(t_c)

                def s_c(se, j=j, hh=hh, w=w):
                    bi, bt, mk = hh["m"]
                    lb.wait(se, "mm", mk)
                    ins = se.activation(zsq[:, j, 0:w], bt[:, 0:w], AF.Square)
                    ins.then_inc(*lb.bump("sq"))
                    hh["sq"] = lb.mark("sq")
                lb.s(s_c)

                def v_c(ve, j=j, hh=hh, w=w, c0=c0, c1=c1):
                    bi, bt, mk = hh["m"]
                    lb.wait(ve, "mm", mk)
                    lb.wait(ve, "sq", hh["sq"])
                    ins = ve.tensor_copy(out_fn(j, c0, c1), bt[:, 0:w])
                    ins.then_inc(*lb.bump("evv"))
                    banks.set_guard(bi, "evv", lb.mark("evv"))
                lb.v(v_c)
                jm[j] = hh

            def t_st(te, jm=jm, w=w):
                if statguard[0] is not None:
                    lb.wait(te, "evv", statguard[0])
                for j in range(KC):
                    lb.wait(te, "sq", jm[j]["sq"])
                for j in range(KC):
                    ins = te.matmul(statbank[0:1, 0:w], onescol[:, :],
                                    zsq[:, j, 0:w],
                                    start=(j == 0), stop=(j == KC - 1))
                ins.then_inc(*lb.bump("mm"))
            lb.t(t_st)
            STM = lb.mark("mm")

            def v_st(ve, w=w):
                lb.wait(ve, "mm", STM)
                lb.ss(ve, "evv", ve.tensor_scalar_mul(
                    statrow[0:1, 0:w], statbank[0:1, 0:w], 1.0 / D))
                ins = ve.tensor_scalar_add(statrow[0:1, 0:w],
                                           statrow[0:1, 0:w], 1e-8)
                ins.then_inc(*lb.bump("evv"))
            lb.v(v_st)
            VST = lb.mark("evv")
            statguard[0] = VST

            def s_st(se, w=w):
                lb.wait(se, "evv", VST)
                ins = se.activation(statrow[0:1, 0:w], statrow[0:1, 0:w],
                                    AF.Sqrt)
                ins.then_inc(*lb.bump("sq"))
            lb.s(s_st)
            SST = lb.mark("sq")

            def v_rc(ve, w=w):
                lb.wait(ve, "sq", SST)
                ins = ve.reciprocal(statrow[0:1, 512:512 + w],
                                    statrow[0:1, 0:w])
                ins.then_inc(*lb.bump("evv"))
            lb.v(v_rc)
            VRC = lb.mark("evv")

            def t_bc(te, w=w):
                if bcguard[0] is not None:
                    lb.wait(te, "evv", bcguard[0])
                lb.wait(te, "evv", VRC)
                te.matmul(bcbank[:, 0:w], onesrow[:, :],
                          statrow[0:1, 512:512 + w],
                          start=True, stop=True).then_inc(*lb.bump("mm"))
            lb.t(t_bc)
            BCM = lb.mark("mm")

            def v_out(ve, c0=c0, c1=c1, w=w):
                lb.wait(ve, "mm", BCM)
                for j in range(KC):
                    o = out_fn(j, c0, c1)
                    ins = ve.tensor_mul(o, o, bcbank[:, 0:w])
                ins.then_inc(*lb.bump("evv"))
            lb.v(v_out)
            last = ("evv", lb.mark("evv"))
            bcguard[0] = last[1]
        return last

    dep = p2_dep

    dep = emit_combine(
        1024,
        lambda cc, c0, c1: q[:, cc, 2 * c0:2 * (c1 - 1) + 1:2],
        lambda cc, c0, c1: q[:, cc, 2 * c0 + 1:2 * (c1 - 1) + 2:2],
        lambda j, c0, c1: lvl_ap(1024)[:, j, c0:c1],
        dep)
    npc = 1024
    while npc >= 4:
        src_l = lvl_ap(npc)
        dst = lvl_ap(npc // 2)
        dep = emit_combine(
            npc // 2,
            lambda cc, c0, c1, src_l=src_l: src_l[:, cc, 2 * c0:2 * (c1 - 1) + 1:2],
            lambda cc, c0, c1, src_l=src_l: src_l[:, cc, 2 * c0 + 1:2 * (c1 - 1) + 2:2],
            lambda j, c0, c1, dst=dst: dst[:, j, c0:c1],
            dep)
        npc //= 2

    ag16 = emit_ag_full(lb, "e16", lvl_ap(2), 128, 16, agx16, dep)

    def v_e16(ve):
        lb.wait(ve, *ag16)
        for r in range(8):
            ins = ve.tensor_copy(
                e16g[:, :, 2 * r:2 * r + 2],
                agx16[:, r, :].rearrange("p (c l) -> p c l", l=2))
        ins.then_inc(*lb.bump("evv"))
    lb.v(v_e16)
    dep = ("evv", lb.mark("evv"))

    def sc(o, n):
        return sc16[:, :, o:o + n]

    def cp(dst_ap, src_ap, dep):
        def v_cp(ve, dep=dep):
            lb.wait(ve, *dep)
            ins = ve.tensor_copy(dst_ap, src_ap)
            ins.then_inc(*lb.bump("evv"))
        lb.v(v_cp)
        return ("evv", lb.mark("evv"))

    E16, L8, L4, L2, L1 = e16g, sc(0, 8), sc(8, 4), sc(12, 2), sc(14, 1)
    R2, R4, R8, R16 = sc(16, 2), sc(18, 4), sc(22, 8), sc(30, 16)
    dep = emit_combine(8, lambda cc, c0, c1: E16[:, cc, 2 * c0:2 * c1:2],
                       lambda cc, c0, c1: E16[:, cc, 2 * c0 + 1:2 * c1:2],
                       lambda j, c0, c1: L8[:, j, c0:c1], dep)
    dep = emit_combine(4, lambda cc, c0, c1: L8[:, cc, 2 * c0:2 * c1:2],
                       lambda cc, c0, c1: L8[:, cc, 2 * c0 + 1:2 * c1:2],
                       lambda j, c0, c1: L4[:, j, c0:c1], dep)
    dep = emit_combine(2, lambda cc, c0, c1: L4[:, cc, 2 * c0:2 * c1:2],
                       lambda cc, c0, c1: L4[:, cc, 2 * c0 + 1:2 * c1:2],
                       lambda j, c0, c1: L2[:, j, c0:c1], dep)
    dep = emit_combine(1, lambda cc, c0, c1: L2[:, cc, 0:1],
                       lambda cc, c0, c1: L2[:, cc, 1:2],
                       lambda j, c0, c1: L1[:, j, c0:c1], dep)
    dep = cp(R2[:, :, 0:1], L2[:, :, 0:1], dep)
    dep = cp(R2[:, :, 1:2], L1[:, :, 0:1], dep)
    dep = emit_combine(1, lambda cc, c0, c1: R2[:, cc, 0:1],
                       lambda cc, c0, c1: L4[:, cc, 2:3],
                       lambda j, c0, c1: R4[:, j, 2:3], dep)
    dep = cp(R4[:, :, 0:1], L4[:, :, 0:1], dep)
    dep = cp(R4[:, :, 1:2], R2[:, :, 0:1], dep)
    dep = cp(R4[:, :, 3:4], R2[:, :, 1:2], dep)
    dep = emit_combine(3, lambda cc, c0, c1: R4[:, cc, c0:c1],
                       lambda cc, c0, c1: L8[:, cc, 2 + 2 * c0:2 + 2 * (c1 - 1) + 1:2],
                       lambda j, c0, c1: R8[:, j, 2 + 2 * c0:2 + 2 * (c1 - 1) + 1:2],
                       dep)
    dep = cp(R8[:, :, 0:1], L8[:, :, 0:1], dep)
    dep = cp(R8[:, :, 1::2], R4[:, :, 0:4], dep)
    dep = emit_combine(7, lambda cc, c0, c1: R8[:, cc, c0:c1],
                       lambda cc, c0, c1: E16[:, cc, 2 + 2 * c0:2 + 2 * (c1 - 1) + 1:2],
                       lambda j, c0, c1: R16[:, j, 2 + 2 * c0:2 + 2 * (c1 - 1) + 1:2],
                       dep)
    dep = cp(R16[:, :, 0:1], E16[:, :, 0:1], dep)
    dep = cp(R16[:, :, 1::2], R8[:, :, 0:8], dep)

    def v_sel(ve, dep=dep):
        lb.wait(ve, *dep)
        for l in range(3):
            for cc in range(KC):
                lb.ss(ve, "evv", ve.tensor_mul(
                    agx[:, 0, 0:8].rearrange("p x -> p x") if False else
                    zsq[:, 0, 0:16], R16[:, cc, :], wsel[:, l, :]))
                lb.ss(ve, "evv", ve.reduce_sum(zsq[:, 1, 0:1],
                                               zsq[:, 0, 0:16], axis=AX.X))
                lb.ss(ve, "evv", ve.tensor_copy(resB[:, cc, l:l + 1],
                                                zsq[:, 1, 0:1]))
        ve.engine_nop().then_inc(*lb.bump("evv"))
    lb.v(v_sel)
    dep = ("evv", lb.mark("evv"))

    def emit_down(Ltot, elems_ap, ncols, half_buf, nc_half, out_buf, dep):
        if Ltot >= 64:
            def v_last(ve, dep=dep):
                lb.wait(ve, *dep)
                ins = ve.tensor_copy(slotrow[:],
                                     half_buf[:, :, nc_half:nc_half + 1])
                ins.then_inc(*lb.bump("evv"))
            lb.v(v_last)
            agdep = emit_ag_prefix(
                lb, f"dn{Ltot}", slotrow[:, :, 0], 128, 8, ohp,
                slotrow2[:, :, 0], ("evv", lb.mark("evv")), agx)

            def v_slot(ve, agdep=agdep):
                lb.wait(ve, *agdep)
                ins = ve.tensor_copy(half_buf[:, :, 0:1], slotrow2[:])
                ins.then_inc(*lb.bump("evv"))
            lb.v(v_slot)
            dep = ("evv", lb.mark("evv"))
        nE = (ncols + 1) // 2
        dep = emit_combine(
            nE,
            lambda cc, c0, c1: half_buf[:, cc, c0:c1],
            lambda cc, c0, c1: elems_ap[:, cc, 2 * c0:2 * (c1 - 1) + 1:2],
            lambda j, c0, c1: out_buf[:, j, 1 + 2 * c0:1 + 2 * (c1 - 1) + 1:2],
            dep)

        def v_fm(ve, dep=dep):
            lb.wait(ve, *dep)
            lb.ss(ve, "evv", ve.tensor_scalar_mul(
                out_buf[:, :, 1:2].rearrange("p c x -> p (c x)"),
                out_buf[:, :, 1:2].rearrange("p c x -> p (c x)"),
                fmk[:, 1:2]))
            lb.ss(ve, "evv", ve.scalar_tensor_tensor(
                out_buf[:, :, 1:2].rearrange("p c x -> p (c x)"),
                elems_ap[:, :, 0:1].rearrange("p c x -> p (c x)"),
                fmk[:, 0:1],
                out_buf[:, :, 1:2].rearrange("p c x -> p (c x)"),
                ALU.mult, ALU.add))
            ins = ve.tensor_copy(out_buf[:, :, 2:2 + 2 * nc_half - 1:2],
                                 half_buf[:, :, 1:1 + nc_half])
            ins.then_inc(*lb.bump("evv"))
        lb.v(v_fm)
        return ("evv", lb.mark("evv"))

    half, hb = resB, 1
    dep = emit_down(32, lvl_ap(4), 4, resB, 2, resA, dep)
    dep = emit_down(64, lvl_ap(8), 8, resA, 4, resB, dep)
    dep = emit_down(128, lvl_ap(16), 16, resB, 8, resA, dep)
    dep = emit_down(256, lvl_ap(32), 32, resA, 16, resB, dep)
    dep = emit_down(512, lvl_ap(64), 64, resB, 32, resA, dep)
    dep = emit_down(1024, lvl_ap(128), 128, resA, 64, resB, dep)
    dep = emit_down(2048, lvl_ap(256), 256, resB, 128, resA, dep)
    dep = emit_down(4096, lvl_ap(512), 512, resA, 256, resB, dep)
    dep = emit_down(8192, lvl_ap(1024), 1024, resB, 512, resA, dep)
    dep = emit_down(16385, q, 2049, resA, 1024, resB, dep)
    if DBG_L0ATTN:
        def y_dbgh(sy, dep=dep):
            lb.wait(sy, *dep)
            sy.dma_start(dbg_h_h[:], resB[:]).then_inc(*lb.bump("st", 16))
            sy.dma_start(dbg_q_h[:], q[:]).then_inc(*lb.bump("st", 16))
            sy.dma_start(dbg_p1_h[:], p1t[:]).then_inc(*lb.bump("st", 16))
        lb.y(y_dbgh)
        def t_dbg_guard(te, dep=dep):
            lb.wait(te, "st", lb.mark("st"))
        lb.t(t_dbg_guard)

    for nm in ("slotrow2", "slotrow", "agx16", "agx", "sc16", "e16g",
               "resA", "elems"):
        lb.free_sb(nm)
    vq = lb.sb_scoped("vq", [128, KC, TC], BF16)
    xch = lb.sb_scoped("xch", [128, KC, 512], BF16)
    NLN = 256
    resid = lb.sb_scoped("resid", [128, KC, NLN], F32)
    ytile = lb.sb_scoped("ytile", [128, KC, NLN], F32)
    yout = lb.sb_scoped("yout", [128, KC, NLN], F32)

    plan.extend([(rearr_pcn(wvT_h, 128 * j, 128), KC) for j in range(KC)]
                * 4)
    plan.extend([(rearr_pcn(wcT_h, 128 * j, 128), KC) for j in range(KC)]
                * (TC // NLN))
    wv_base = KC

    vdep = dep
    xch_free = {}
    for f in range(4):
        def g_xch(g, f=f, vdep=vdep):
            if f > 0:
                lb.wait(g, "mm", xch_free[f - 1])
            else:
                lb.wait(g, *vdep)
            g.dma_start(xch[:], rearr_pct(xq_h, 1 + 512 * f, 512)
                        ).then_inc(*lb.bump("xld", 16))
        lb.g(g_xch)
        xld_m = lb.mark("xld")
        pump(3 if f == 0 else 0)
        for j in range(KC):
            hh = {}

            def t_v(te, f=f, j=j, hh=hh, xld_m=xld_m):
                if j == 0:
                    lb.wait(te, "xld", xld_m)
                w = wbuf.consume(te, wv_base + 8 * f + j)
                bi, bt = banks.acquire(te)
                for cc in range(KC):
                    ins = te.matmul(bt[:], w[:, cc, :], xch[:, cc, :],
                                    start=(cc == 0), stop=(cc == KC - 1))
                ins.then_inc(*lb.bump("mm"))
                hh["m"] = (bi, bt, lb.mark("mm"))
                wbuf.release_at(wv_base + 8 * f + j, "mm", lb.mark("mm"))
                if j == KC - 1:
                    xch_free[f] = lb.mark("mm")
            lb.t(t_v)

            def v_v(ve, f=f, j=j, hh=hh):
                bi, bt, mk = hh["m"]
                lb.wait(ve, "mm", mk)
                ins = ve.tensor_mul(vq[:, j, 512 * f:512 * (f + 1)],
                                    bt[:], resB[:, j, 2 + 512 * f:2 + 512 * (f + 1)])
                ins.then_inc(*lb.bump("evv"))
                banks.set_guard(bi, "evv", lb.mark("evv"))
            lb.v(v_v)
            pump(1)
    YV = ("evv", lb.mark("evv"))

    aux = {"onescol": onescol, "onesrow": onesrow, "zsq": zsq[:, :, 0:NLN],
           "statrow": statrow}
    wc_base = wv_base + 32
    for sti in range(TC // NLN):
        ts = slice(NLN * sti, NLN * (sti + 1))

        def y_resid(sy, sti=sti):
            lb.wait(sy, "evv", lb.mark("evv"))
            sy.dma_start(resid[:], rearr_pct(xq_h, 1 + NLN * sti, NLN)
                         ).then_inc(*lb.bump("rld", 16))
        lb.y(y_resid)
        rld_mark = lb.mark("rld")

        for j in range(KC):
            eh = {}

            def t_y(te, sti=sti, j=j, eh=eh, ts=ts):
                if sti == 0 and j == 0:
                    lb.wait(te, *YV)
                w = wbuf.consume(te, wc_base + 8 * sti + j)
                bi, bt = banks.acquire(te)
                for cc in range(KC):
                    ins = te.matmul(bt[:, 0:NLN], w[:, cc, :], vq[:, cc, ts],
                                    start=(cc == 0), stop=(cc == KC - 1))
                ins.then_inc(*lb.bump("mm"))
                eh["m"] = lb.mark("mm")
                eh["b"] = (bi, bt)
                wbuf.release_at(wc_base + 8 * sti + j, "mm", lb.mark("mm"))
            lb.t(t_y)

            def v_y(ve, sti=sti, j=j, eh=eh, rld_mark=rld_mark):
                if j == 0:
                    lb.wait(ve, "rld", rld_mark)
                lb.wait(ve, "mm", eh["m"])
                bi, bt = eh["b"]
                ins = ve.tensor_add(ytile[:, j, :], bt[:, 0:NLN],
                                    resid[:, j, :])
                ins.then_inc(*lb.bump("evv"))
                banks.set_guard(bi, "evv", lb.mark("evv"))
            lb.v(v_y)
            pump(1)
        z_dep = ("evv", lb.mark("evv"))

        ln_dep = emit_ln(lb, banks, aux, ytile, yout, z_dep, nt=NLN,
                         out_extra_deps=[("st", lb.mark("st"))])

        def y_store(sy, sti=sti, ln_dep=ln_dep):
            lb.wait(sy, *ln_dep)
            sy.dma_start(rearr_pct(xout_h, NLN * sti, NLN), yout[:]
                         ).then_inc(*lb.bump("st", 16))
        lb.y(y_store)

    return (lb.qual("st"), lb.mark("st"))


def host_l0attn_inputs(inputs, xT_full):
    wqT = np.ascontiguousarray(inputs["L0_attn_Wq"]).astype(np.float32)
    wvT = np.ascontiguousarray(inputs["L0_attn_Wv"]).astype(np.float32)
    wcT = np.ascontiguousarray(inputs["L0_attn_Wc"]).astype(np.float32)
    wqT = np.ascontiguousarray(inputs["L0_attn_Wq"].T).astype(np.float32)
    wvT = np.ascontiguousarray(inputs["L0_attn_Wv"].T).astype(np.float32)
    wcT = np.ascontiguousarray(inputs["L0_attn_Wc"].T).astype(np.float32)
    idn = inputs["L0_attn_identity"].astype(np.float32)
    idncol = np.ascontiguousarray(idn.reshape(8, 128).T)
    wp1 = inputs["L0_attn_Wp1"].astype(np.float32)
    wp2 = inputs["L0_attn_Wp2"].astype(np.float32)
    common = dict(WqT=wqT, WvT=wvT, WcT=wcT, identcol=idncol,
                  Wp1=wp1, Wp2=wp2,
                  ident128f=np.eye(128, dtype=np.float32),
                  onescol=np.ones((128, 1), np.float32),
                  onesrow=np.ones((1, 128), np.float32))
    maps = []
    for c in range(NCORES):
        xq = np.zeros((D, TC + 1), np.float32)
        lo = TC * c - 1
        if c == 0:
            xq[:, 1:] = xT_full[:, 0:TC]
        else:
            xq[:, :] = xT_full[:, lo:lo + TC + 1]
        fm = np.zeros((128, 2), np.float32)
        fm[:, 0] = 1.0 if c == 0 else 0.0
        fm[:, 1] = 0.0 if c == 0 else 1.0
        ohp = np.zeros((128, 8), np.float32)
        if c > 0:
            ohp[:, c - 1] = 1.0
        wsel = np.zeros((128, 3, 16), np.float32)
        for l in range(3):
            g = 2 * c - 1 + l
            if 0 <= g < 16:
                wsel[:, l, g] = 1.0
        maps.append(dict(common, xqT=np.ascontiguousarray(xq),
                         fmask=fm, onehotprev=ohp, wsel=wsel,
                         Wp1r=np.ascontiguousarray(wp1[128 * c:128 * (c + 1)]),
                         Wp2r=np.ascontiguousarray(wp2[128 * c:128 * (c + 1)])))
    return maps


def build_fused():
    lb = LB()
    xq_h = lb.inp("xqT", [D, TC + 1])
    ocol_h = lb.inp("onescol", [128, 1])
    orow_h = lb.inp("onesrow", [1, 128])
    xout_h = lb.out("xoutT", [D, TC])
    x1T = lb.dram("x1T", [D, TC])
    x2T = lb.dram("x2T", [D, TC])
    x3T = lb.dram("x3T", [D, TC])

    SEMS = ["mm", "evv", "evs", "sq", "xld", "xfld", "cld", "rld", "st",
            "agd", "cc", "wk_s0", "wk_s1", "wk_s2", "wk_s3",
            "wkt_s0", "wkt_s1", "wkt_s2"]
    lb.sem("shared")
    for pfx in ("b0_", "b1_", "b2_", "b3_"):
        for nm in SEMS:
            lb.sem("@" + pfx + nm)
    lb.start_block([], allow_lp=True)
    onescol = lb.sb("sh_onescol", [128, 1], F32)
    onesrow = lb.sb("sh_onesrow", [1, 128], F32)
    id128sh = lb.sb("sh_id128", [128, 128], F32)

    def y_shc(sy):
        sy.dma_start(onescol[:, :], ocol_h[:]).then_inc(*lb.bump("@shared", 16))
        sy.dma_start(onesrow[:, :], orow_h[:]).then_inc(*lb.bump("@shared", 16))
    lb.y(y_shc)
    shc = ("@shared", lb.mark("@shared"))

    def gate(dep):
        for emit in (lb.t, lb.v, lb.s, lb.g, lb.y):
            def g_(e, dep=dep):
                lb.wait(e, *dep)
            emit(g_)

    wc1_d = lb.dram("wc1cache", [136, 128, 4096], BF16)
    wc3_d = lb.dram("wc3cache", [128, 128, 4096], BF16)

    lb.sem_prefix = "b0_"
    lb.push_zone()
    io0 = {"xq": xq_h, "xout": x1T, "onescol": onescol, "onesrow": onesrow,
           "shc_dep": shc}
    with lb.nc.named_scope("b0_l0attn"):
        dep = emit_l0attn(lb, io0)
    lb.pop_zone()

    lb.sem_prefix = "b1_"
    gate(dep)
    lb.push_zone()
    io1 = {"xin": x1T, "xout": x2T, "onescol": onescol, "onesrow": onesrow,
           "shc_dep": dep, "stream_dep": None}
    if WCACHE1:
        io1["wc1"] = wc1_d
    with lb.nc.named_scope("b1_l0ffn"):
        dep = emit_l0ffn(lb, io1)
    lb.pop_zone()

    lb.sem_prefix = "b2_"
    gate(dep)
    lb.push_zone()
    io2 = {"xin": x2T, "xout": x3T, "onescol": onescol, "onesrow": onesrow,
           "shc_dep": dep, "stream_dep": None}
    with lb.nc.named_scope("b2_l1attn"):
        dep = emit_l1attn(lb, io2)
    lb.pop_zone()

    lb.sem_prefix = "b3_"
    gate(dep)

    def g_id(g):
        g.dma_start(id128sh[:, :],
                    lb.nc.dram_tensor("ident128", [128, 128], F32,
                                      kind="ExternalInput").ap()[:]
                    ).then_inc(*lb.bump("@shared", 16))
    lb.g(g_id)
    shc3 = ("@shared", lb.mark("@shared"))
    lb.push_zone()
    io3 = {"xin": x3T, "xout": xout_h, "onescol": onescol,
           "onesrow": onesrow, "id128": id128sh, "shc_dep": shc3,
           "stream_dep": dep}
    if WCACHE3:
        io3["wc3"] = wc3_d
    with lb.nc.named_scope("b3_l1ffn"):
        dep = emit_l1ffn(lb, io3)
    lb.pop_zone()
    lb.sem_prefix = ""

    def y_final(sy):
        lb.wait(sy, *dep)
    lb.y(y_final)
    return lb.done()


def host_fused_inputs(inputs, xT_full):
    maps0 = host_l0attn_inputs(inputs, xT_full)
    maps1 = host_l0ffn_inputs(inputs, None)
    maps2 = host_l1attn_inputs(inputs, None)
    maps3 = host_l1ffn_inputs(inputs, None)
    out = []
    for c in range(NCORES):
        m = {}
        m.update(maps0[c])
        m.update(maps1[c])
        m.update(maps2[c])
        m.update(maps3[c])
        out.append(m)
    return out


FUSED = True

_BUILT = {}


def _get(name, builder):
    if name not in _BUILT:
        _BUILT[name] = builder()
    return _BUILT[name]


def kernel(**inputs):
    x = np.ascontiguousarray(inputs["L0_attn_x"][0]).astype(np.float32)
    xT = np.ascontiguousarray(x.T)

    if FUSED:
        ncf = _get("fused", build_fused)
        resf = run_bass_kernel_spmd(ncf, host_fused_inputs(inputs, xT),
                                    core_ids=list(range(NCORES)))
        out = np.concatenate([resf.results[c]["xoutT"]
                              for c in range(NCORES)], axis=1).T
        return np.ascontiguousarray(out)[None].astype(np.float32)

    nc0 = _get("l0attn", build_l0attn)
    res0 = run_bass_kernel_spmd(nc0, host_l0attn_inputs(inputs, xT),
                                core_ids=list(range(NCORES)))
    s1 = [np.ascontiguousarray(res0.results[c]["xoutT"])
          for c in range(NCORES)]

    nc1 = _get("l0ffn", build_l0ffn)
    res1 = run_bass_kernel_spmd(nc1, host_l0ffn_inputs(inputs, s1),
                                core_ids=list(range(NCORES)))
    s2 = [np.ascontiguousarray(res1.results[c]["xoutT"])
          for c in range(NCORES)]

    nc2 = _get("l1attn", build_l1attn)
    res2 = run_bass_kernel_spmd(nc2, host_l1attn_inputs(inputs, s2),
                                core_ids=list(range(NCORES)))
    s3 = [np.ascontiguousarray(res2.results[c]["xoutT"])
          for c in range(NCORES)]

    nc3 = _get("l1ffn", build_l1ffn)
    res3 = run_bass_kernel_spmd(nc3, host_l1ffn_inputs(inputs, s3),
                                core_ids=list(range(NCORES)))
    out = np.concatenate([res3.results[c]["xoutT"]
                          for c in range(NCORES)], axis=1).T
    return np.ascontiguousarray(out)[None].astype(np.float32)


# version 09
# speedup vs baseline: 1.0575x; 1.0575x over previous
import contextlib
import numpy as np
import concourse.bass as bass
import concourse.mybir as mybir
from concourse.bass_utils import run_bass_kernel_spmd

F32 = mybir.dt.float32
F32R = mybir.dt.float32r
BF16 = mybir.dt.bfloat16
AF = mybir.ActivationFunctionType
ALU = mybir.AluOpType
AX = mybir.AxisListType

NCORES = 8
D = 1024
T = 16384
TC = T // NCORES
KC = D // 128
RG = [list(range(NCORES))]

NS_SCHED = [
    (50.392372131347656, -20565.65625, 2108726.5),
    (4.029094219207764, -10.593995094299316, 7.42457914352417),
    (4.029093265533447, -10.593989372253418, 7.424572944641113),
    (4.029092788696289, -10.593984603881836, 7.424567222595215),
    (4.0290937423706055, -10.593992233276367, 7.424575328826904),
    (4.029094219207764, -10.593997955322266, 7.424582004547119),
    (4.029094696044922, -10.594000816345215, 7.424585819244385),
    (4.029094696044922, -10.593998908996582, 7.4245829582214355),
    (4.029094219207764, -10.59399700164795, 7.424581050872803),
    (4.029094696044922, -10.593999862670898, 7.42458438873291),
    (3.9041237831115723, -9.692113876342773, 6.666903018951416),
    (3.1152637004852295, -5.149963855743408, 3.0012638568878174),
    (2.136584997177124, -1.7934309244155884, 0.6653010249137878),
    (1.5, -0.5, 0.0),
]
LN_EPS = 1e-5
DBG_L1FFN = False
DBG_NT1 = False
DBG_L1ATTN = False
DBG_L0ATTN = False
DBG_NS = False
DBG_NS_IT = 2
WCACHE1 = True
WCACHE3 = True
PP_ON = True
PP_LIMIT = None
PP_SMALL3 = False



class LB:

    def __init__(self):
        self.nc = bass.Bass("TRN2", target_bir_lowering=False)
        self.stack = contextlib.ExitStack()
        self.blk = None
        self._sems = {}
        self.sem_prefix = ""
        self._scoped = []
        self._scoped_ps = []
        self._zones = []

    def _q(self, name):
        if name.startswith("@"):
            return name[1:]
        return self.sem_prefix + name

    def qual(self, name):
        return "@" + self._q(name)

    def push_zone(self):
        self._zones.append((len(self._scoped), len(self._scoped_ps)))

    def pop_zone(self):
        base, base_ps = self._zones.pop()
        while len(self._scoped) > base:
            nm, cm = self._scoped.pop()
            cm.__exit__(None, None, None)
        while len(self._scoped_ps) > base_ps:
            nm, cm = self._scoped_ps.pop()
            cm.__exit__(None, None, None)

    def inp(self, name, shape, dt=F32):
        return self.nc.dram_tensor(name, shape, dt, kind="ExternalInput").ap()

    def out(self, name, shape, dt=F32):
        return self.nc.dram_tensor(name, shape, dt, kind="ExternalOutput").ap()

    def dram(self, name, shape, dt=F32, addr_space="Local"):
        return self.nc.dram_tensor(self.sem_prefix + name, shape, dt,
                                   kind="Internal",
                                   addr_space=addr_space).ap()

    def sb(self, name, shape, dt):
        name = self.sem_prefix + name
        if self._zones:
            return self.sb_scoped_raw(name, shape, dt)
        return self.stack.enter_context(self.nc.sbuf_tensor(name, shape, dt))

    def sb_scoped_raw(self, name, shape, dt):
        cm = self.nc.sbuf_tensor(name, shape, dt)
        h = cm.__enter__()
        self._scoped.append((name, cm))
        return h

    def sb_scoped(self, name, shape, dt):
        return self.sb_scoped_raw(self.sem_prefix + name, shape, dt)

    def free_sb(self, name):
        name = self.sem_prefix + name
        nm, cm = self._scoped.pop()
        assert nm == name, f"free_sb out of order: {nm} != {name}"
        cm.__exit__(None, None, None)

    def ps(self, name, shape, dt=F32):
        name = self.sem_prefix + name
        if self._zones:
            cm = self.nc.psum_tensor(name, shape, dt)
            h = cm.__enter__()
            self._scoped_ps.append((name, cm))
            return h
        return self.stack.enter_context(self.nc.psum_tensor(name, shape, dt))

    def start_block(self, sem_names, allow_lp=False):
        for s in sem_names:
            self.sem(s)
        if allow_lp:
            self.stack.enter_context(self.nc.allow_low_precision(
                "bf16 intermediate storage; validated against fp32 reference "
                "within the measured tolerance"))
        self.blk = self.stack.enter_context(self.nc.Block())

    def sem(self, name):
        name = self._q(name)
        if name not in self._sems:
            h = self.stack.enter_context(self.nc.semaphore(name))
            self._sems[name] = [h, 0]
        return self._sems[name][0]

    def bump(self, name, n=1):
        h = self.sem(name)
        self._sems[self._q(name)][1] += n
        return h, n

    def mark(self, name):
        self.sem(name)
        return self._sems[self._q(name)][1]

    def wait(self, eng, name, target):
        if target > 0:
            eng.wait_ge(self.sem(name), target)

    def ss(self, eng, sem_name, ins):
        ins.then_inc(*self.bump(sem_name))
        eng.wait_ge(self.sem(sem_name), self.mark(sem_name))

    def t(self, fn):
        self.blk.tensor(fn)

    def v(self, fn):
        self.blk.vector(fn)

    def s(self, fn):
        self.blk.scalar(fn)

    def g(self, fn):
        self.blk.gpsimd(fn)

    def y(self, fn):
        self.blk.sync(fn)

    def done(self):
        while self._scoped:
            nm, cm = self._scoped.pop()
            cm.__exit__(None, None, None)
        while getattr(self, "_scoped_ps", None):
            nm, cm = self._scoped_ps.pop()
            cm.__exit__(None, None, None)
        self.stack.close()
        return self.nc


class Banks:

    def __init__(self, lb, tiles):
        self.lb = lb
        self.tiles = tiles
        self.next = 0
        self.guard = [None] * len(tiles)

    def acquire(self, teng):
        i = self.next
        self.next = (self.next + 1) % len(self.tiles)
        if self.guard[i] is not None:
            self.lb.wait(teng, *self.guard[i])
            self.guard[i] = None
        return i, self.tiles[i]

    def set_guard(self, i, sem_name, target):
        self.guard[i] = (sem_name, target)


class WBuf:

    def __init__(self, lb, tiles, name):
        self.lb = lb
        self.tiles = tiles
        self.n = len(tiles)
        self.name = name
        self.n_loads = 0
        self.loads_per_slot = [0] * self.n
        self.free_mark = [None] * self.n
        self.load_target = []

    def _ssem(self, s):
        return f"{self.name}_s{s}"

    def sems(self):
        return [self._ssem(i) for i in range(self.n)]

    def emit_load(self, g, hbm_ap, kchunks=None):
        i = self.n_loads
        s = i % self.n
        if self.free_mark[s] is not None:
            for dep in self.free_mark[s]:
                self.lb.wait(g, *dep)
            self.free_mark[s] = None
        dst = self.tiles[s][:] if kchunks is None else \
            self.tiles[s][:, 0:kchunks, :]
        g.dma_start(dst, hbm_ap).then_inc(
            *self.lb.bump(self._ssem(s), 16))
        self.loads_per_slot[s] += 1
        self.load_target.append((self._ssem(s), 16 * self.loads_per_slot[s]))
        self.n_loads += 1

    def consume(self, teng, i):
        self.lb.wait(teng, *self.load_target[i])
        return self.tiles[i % self.n]

    def release_at(self, i, sem_name, target):
        s = i % self.n
        if self.free_mark[s] is None:
            self.free_mark[s] = []
        self.free_mark[s].append((sem_name, target))


def ceil_div(a, b):
    return (a + b - 1) // b


def get_inp(lb, name, shape, dt=F32):
    cache = getattr(lb, "_inp_cache", None)
    if cache is None:
        cache = lb._inp_cache = {}
    if name not in cache:
        cache[name] = lb.inp(name, shape, dt)
    return cache[name]


def rearr_pcn(hbm_2d, col0, ncols):
    return hbm_2d[:, col0:col0 + ncols].rearrange("(c p) n -> p c n", p=128)


def rearr_pct(hbm_2d, t0, nt):
    return hbm_2d[:, t0:t0 + nt].rearrange("(c p) t -> p c t", p=128)



def emit_ln(lb, banks, aux, z_sb, out_sb, z_dep, nt=512, nchunks=KC,
            out_bf=None, out_bf_off=0, out_extra_deps=(), z_off=0):
    zsq, statrow = aux["zsq"], aux["statrow"]
    onescol, onesrow = aux["onescol"], aux["onesrow"]
    m = {}

    def s_squares(s):
        lb.wait(s, *z_dep)
        for c in range(nchunks):
            ins = s.activation(zsq[:, c, 0:nt], z_sb[:, c, z_off:z_off + nt], AF.Square)
        ins.then_inc(*lb.bump("sq"))
        m["sq"] = lb.mark("sq")
    lb.s(s_squares)

    holders = {}

    def t_sums(t):
        lb.wait(t, *z_dep)
        i1, b1 = banks.acquire(t)
        for c in range(nchunks):
            ins = t.matmul(b1[0:1, 0:nt], onescol[:], z_sb[:, c, z_off:z_off + nt],
                           start=(c == 0), stop=(c == nchunks - 1))
        ins.then_inc(*lb.bump("mm"))
        m["sums"] = lb.mark("mm")
        lb.wait(t, "sq", m["sq"])
        i2, b2 = banks.acquire(t)
        for c in range(nchunks):
            ins = t.matmul(b2[0:1, 0:nt], onescol[:], zsq[:, c, 0:nt],
                           start=(c == 0), stop=(c == nchunks - 1))
        ins.then_inc(*lb.bump("mm"))
        m["sumsq"] = lb.mark("mm")
        holders.update(i1=i1, b1=b1, i2=i2, b2=b2)
    lb.t(t_sums)

    def v_rows(v):
        lb.wait(v, "mm", m["sums"])
        v.tensor_scalar_mul(statrow[0:1, 0:nt], holders["b1"][0:1, 0:nt],
                            1.0 / D)
        lb.wait(v, "mm", m["sumsq"])
        ins = v.tensor_scalar_mul(statrow[0:1, nt:2 * nt],
                                  holders["b2"][0:1, 0:nt], 1.0 / D)
        ins.then_inc(*lb.bump("evv"))
        m["rows"] = lb.mark("evv")
    lb.v(v_rows)
    banks.set_guard(holders["i1"], "evv", m["rows"])
    banks.set_guard(holders["i2"], "evv", m["rows"])

    def s_m2(s):
        lb.wait(s, "evv", m["rows"])
        ins = s.activation(statrow[0:1, 2 * nt:3 * nt], statrow[0:1, 0:nt],
                           AF.Square)
        ins.then_inc(*lb.bump("sq"))
        m["m2"] = lb.mark("sq")
    lb.s(s_m2)

    def v_var(v):
        lb.wait(v, "sq", m["m2"])
        lb.ss(v, "evv", v.tensor_sub(statrow[0:1, nt:2 * nt],
                                     statrow[0:1, nt:2 * nt],
                                     statrow[0:1, 2 * nt:3 * nt]))
        ins = v.tensor_scalar_add(statrow[0:1, nt:2 * nt],
                                  statrow[0:1, nt:2 * nt], LN_EPS)
        ins.then_inc(*lb.bump("evv"))
        m["var"] = lb.mark("evv")
    lb.v(v_var)

    def s_sd(s):
        lb.wait(s, "evv", m["var"])
        ins = s.activation(statrow[0:1, nt:2 * nt], statrow[0:1, nt:2 * nt],
                           AF.Sqrt)
        ins.then_inc(*lb.bump("sq"))
        m["sd"] = lb.mark("sq")
    lb.s(s_sd)

    def v_rstd(v):
        lb.wait(v, "sq", m["sd"])
        lb.ss(v, "evv", v.reciprocal(statrow[0:1, nt:2 * nt],
                                     statrow[0:1, nt:2 * nt]))
        ins = v.tensor_mul(statrow[0:1, 0:nt], statrow[0:1, 0:nt],
                           statrow[0:1, nt:2 * nt])
        ins.then_inc(*lb.bump("evv"))
        m["mrb"] = lb.mark("evv")
    lb.v(v_rstd)

    def t_bcast(t):
        lb.wait(t, "evv", m["mrb"])
        i3, b3 = banks.acquire(t)
        t.matmul(b3[:, 0:nt], onesrow[:], statrow[0:1, nt:2 * nt],
                 start=True, stop=True).then_inc(*lb.bump("mm"))
        m["rb"] = lb.mark("mm")
        i4, b4 = banks.acquire(t)
        t.matmul(b4[:, 0:nt], onesrow[:], statrow[0:1, 0:nt],
                 start=True, stop=True).then_inc(*lb.bump("mm"))
        m["mb"] = lb.mark("mm")
        holders.update(i3=i3, b3=b3, i4=i4, b4=b4)
    lb.t(t_bcast)

    def v_final(v):
        lb.wait(v, "mm", m["mb"])
        for dep in out_extra_deps:
            lb.wait(v, *dep)
        for c in range(nchunks):
            lb.ss(v, "evv",
                  v.tensor_mul(out_sb[:, c, 0:nt],
                               z_sb[:, c, z_off:z_off + nt],
                               holders["b3"][:, 0:nt]))
            ins = v.tensor_sub(out_sb[:, c, 0:nt], out_sb[:, c, 0:nt],
                               holders["b4"][:, 0:nt])
            if out_bf is not None:
                lb.ss(v, "evv", ins)
                ins = v.tensor_copy(
                    out_bf[:, c, out_bf_off:out_bf_off + nt],
                    out_sb[:, c, 0:nt])
        ins.then_inc(*lb.bump("evv"))
        m["out"] = lb.mark("evv")
    lb.v(v_final)
    banks.set_guard(holders["i3"], "evv", m["out"])
    banks.set_guard(holders["i4"], "evv", m["out"])
    return ("evv", m["out"])



def build_l0ffn():
    lb = LB()
    xin_h = lb.inp("xinT", [D, TC])
    ocol_h = lb.inp("onescol", [128, 1])
    orow_h = lb.inp("onesrow", [1, 128])
    xout_h = lb.out("xoutT", [D, TC])
    lb.start_block(["mm", "evv", "evs", "sq", "xld", "cld", "rld",
                    "st", "wk_s0", "wk_s1", "wk_s2", "wk_s3", "shared"])
    onescol = lb.sb("sh_onescol", [128, 1], F32)
    onesrow = lb.sb("sh_onesrow", [1, 128], F32)

    def y_shc(sy):
        sy.dma_start(onescol[:, :], ocol_h[:]).then_inc(*lb.bump("@shared", 16))
        sy.dma_start(onesrow[:, :], orow_h[:]).then_inc(*lb.bump("@shared", 16))
    lb.y(y_shc)
    io = {"xin": xin_h, "xout": xout_h, "onescol": onescol,
          "onesrow": onesrow, "shc_dep": ("@shared", lb.mark("@shared")),
          "stream_dep": None}
    dep = emit_l0ffn(lb, io)

    def y_final(sy):
        lb.wait(sy, *dep)
    lb.y(y_final)
    return lb.done()


def emit_l0ffn(lb, io):
    xin_h = io["xin"]
    xout_h = io["xout"]
    onescol = io["onescol"]
    onesrow = io["onesrow"]
    w1T_h = get_inp(lb, "w1T", [D, 4096])
    w2T_h = get_inp(lb, "w2T", [D, 4096])
    h1T_h = get_inp(lb, "h1T", [4096, 4096])
    h2T_h = get_inp(lb, "h2T", [4096, 4096])
    w3T_h = get_inp(lb, "w3T", [4096, D])
    s1_h = lb.inp("s1", [128, 32])
    s2_h = lb.inp("s2", [128, 32])
    s3_h = lb.inp("s3", [128, 32])
    s4_h = lb.inp("s4", [128, 32])
    s5_h = lb.inp("s5", [128, 8])

    TT = 512
    NT = TC // TT
    NSLOT = 4

    xbf = lb.sb("xbf", [128, KC, TT], BF16)
    A = lb.sb("A", [128, 32, TT], BF16)
    B = lb.sb("Bf", [128, 32, TT], BF16)
    wk = [lb.sb(f"wk{i}", [128, 32, 128], BF16) for i in range(NSLOT)]
    tmp1 = lb.sb("tmp1", [128, 2, TT], F32)
    resid = lb.sb("resid", [128, KC, TT], F32)
    ytile = lb.sb("ytile", [128, KC, TT], F32)
    yout = lb.sb("yout", [128, KC, TT], F32)
    s1s = lb.sb("s1s", [128, 32], F32)
    s2s = lb.sb("s2s", [128, 32], F32)
    s3s = lb.sb("s3s", [128, 32], F32)
    s4s = lb.sb("s4s", [128, 32], F32)
    s5s = lb.sb("s5s", [128, 8], F32)
    zsq = lb.sb("zsq", [128, KC, TT], F32)
    statrow = lb.sb("statrow", [1, 3 * TT], F32)

    pb = [lb.ps(f"pb{i}", [128, 512]) for i in range(8)]

    banks = Banks(lb, pb)
    wbuf = WBuf(lb, wk, "wk")
    aux = {"onescol": onescol, "onesrow": onesrow, "zsq": zsq,
           "statrow": statrow}

    def y_const(sy):
        lb.wait(sy, *io["shc_dep"])
        for dst, src in [(s1s, s1_h), (s2s, s2_h), (s3s, s3_h), (s4s, s4_h),
                         (s5s, s5_h)]:
            sy.dma_start(dst[:], src[:]).then_inc(*lb.bump("cld", 16))
    lb.y(y_const)
    CLD = lb.mark("cld")

    wcd = io.get("wc1")
    unique = []
    for j in range(32):
        unique.append((rearr_pcn(w1T_h, 128 * j, 128), KC))
        unique.append((rearr_pcn(w2T_h, 128 * j, 128), KC))
    for j in range(32):
        unique.append((rearr_pcn(h1T_h, 128 * j, 128), 32))
    for j in range(32):
        unique.append((rearr_pcn(h2T_h, 128 * j, 128), 32))
    for j in range(8):
        unique.append((rearr_pcn(w3T_h, 128 * j, 128), 32))
    WPT = 136

    def cache_ap(idx, kch):
        if kch == KC:
            return wcd[idx][:, 0:1024].rearrange("p (c m) -> p c m", m=128)
        return wcd[idx][:].rearrange("p (c m) -> p c m", m=128)

    if wcd is None:
        plan = unique * NT
    else:
        plan = list(unique)
        for t in range(1, NT):
            for idx, (_, kch) in enumerate(unique):
                plan.append((cache_ap(idx, kch), kch))
    pumped = [0]
    store_total = {}

    def pump(n):
        n = min(n, len(plan) - pumped[0])
        if n <= 0:
            return

        for _ in range(n):
            idx = pumped[0]
            ap, kch = plan[idx]
            cached = wcd is not None and idx >= WPT

            def one_load(eng, idx=idx, ap=ap, kch=kch):
                if wcd is not None and idx == WPT:
                    for s2 in range(NSLOT):
                        lb.wait(eng, f"wkc{s2}", store_total[s2])
                wbuf.emit_load(eng, ap, kch)
                if wcd is not None and idx < WPT:
                    s2 = (wbuf.n_loads - 1) % NSLOT
                    tgt = wbuf.load_target[-1]
                    cap = cache_ap(idx, kch)

                    def y_st(sy, s2=s2, tgt=tgt, cap=cap, kch=kch):
                        lb.wait(sy, *tgt)
                        sy.dma_start(cap, wk[s2][:, 0:kch, :]
                                     ).then_inc(*lb.bump(f"wkc{s2}", 16))
                    lb.y(y_st)
                    store_total[s2] = lb.mark(f"wkc{s2}")
                    wbuf.release_at(wbuf.n_loads - 1, f"wkc{s2}",
                                    store_total[s2])
            if cached:
                lb.s(one_load)
            else:
                lb.g(one_load)
            pumped[0] += 1

    def pump_x(t):
        def g_x(g):
            if t > 0:
                lb.wait(g, "mm", xfree_mark[t - 1])
            elif io["stream_dep"] is not None:
                lb.wait(g, *io["stream_dep"])
            g.dma_start(xbf[:], rearr_pct(xin_h, TT * t, TT)
                        ).then_inc(*lb.bump("xld", 16))
        lb.g(g_x)

    xfree_mark = {}
    rfree_mark = {}
    pump_x(0)
    pump(NSLOT)

    for t in range(NT):
        wi0 = t * WPT
        cons_mark = {}

        for j in range(32):
            holders = {}

            def t_s1(te, t=t, j=j, holders=holders):
                if j == 0:
                    lb.wait(te, "cld", CLD)
                    lb.wait(te, "xld", 16 * (t + 1))
                w1t = wbuf.consume(te, wi0 + 2 * j)
                bi, bt = banks.acquire(te)
                for c in range(KC):
                    ins = te.matmul(bt[:], w1t[:, c, :], xbf[:, c, :],
                                    start=(c == 0), stop=(c == KC - 1))
                ins.then_inc(*lb.bump("mm"))
                holders["m1"] = lb.mark("mm")
                wbuf.release_at(wi0 + 2 * j, "mm", holders["m1"])
                holders["b1"] = (bi, bt)
                w2t = wbuf.consume(te, wi0 + 2 * j + 1)
                bi2, bt2 = banks.acquire(te)
                for c in range(KC):
                    ins = te.matmul(bt2[:], w2t[:, c, :], xbf[:, c, :],
                                    start=(c == 0), stop=(c == KC - 1))
                ins.then_inc(*lb.bump("mm"))
                holders["m2"] = lb.mark("mm")
                wbuf.release_at(wi0 + 2 * j + 1, "mm", holders["m2"])
                if j == 31:
                    xfree_mark[t] = holders["m2"]
                holders["b2"] = (bi2, bt2)
            lb.t(t_s1)

            def s_silu(se, j=j, holders=holders):
                if j == 0:
                    lb.wait(se, "cld", CLD)
                lb.wait(se, "mm", holders["m1"])
                if j >= 2:
                    lb.wait(se, "evv", cons_mark[j - 2])
                ins = se.activation(tmp1[:, j % 2, :], holders["b1"][1][:],
                                    AF.Silu, scale=s1s[:, j:j + 1])
                ins.then_inc(*lb.bump("evs"))
                holders["silu"] = lb.mark("evs")
                banks.set_guard(holders["b1"][0], "evs", holders["silu"])
            lb.s(s_silu)

            def v_h(ve, j=j, holders=holders):
                if j == 0:
                    lb.wait(ve, "cld", CLD)
                lb.wait(ve, "mm", holders["m2"])
                lb.wait(ve, "evs", holders["silu"])
                ins = ve.scalar_tensor_tensor(
                    A[:, j, :], holders["b2"][1][:], s2s[:, j:j + 1],
                    tmp1[:, j % 2, :], ALU.mult, ALU.mult)
                ins.then_inc(*lb.bump("evv"))
                cons_mark[j] = lb.mark("evv")
                banks.set_guard(holders["b2"][0], "evv", cons_mark[j])
            lb.v(v_h)
            pump(2)
            if t < NT - 1 and j == 0:
                pass
        h_done = ("evv", lb.mark("evv"))

        def hidden_stage(src, dst, scales, src_dep, wbase):
            for j in range(32):
                holders = {}

                def t_h(te, j=j, holders=holders, src=src, src_dep=src_dep,
                        wbase=wbase):
                    if j == 0:
                        lb.wait(te, *src_dep)
                    ht = wbuf.consume(te, wbase + j)
                    bi, bt = banks.acquire(te)
                    for c in range(32):
                        ins = te.matmul(bt[:], ht[:, c, :], src[:, c, :],
                                        start=(c == 0), stop=(c == 31))
                    ins.then_inc(*lb.bump("mm"))
                    holders["m"] = lb.mark("mm")
                    wbuf.release_at(wbase + j, "mm", holders["m"])
                    holders["b"] = (bi, bt)
                lb.t(t_h)

                def s_g(se, j=j, holders=holders, dst=dst, scales=scales):
                    lb.wait(se, "mm", holders["m"])
                    ins = se.activation(dst[:, j, :], holders["b"][1][:],
                                        AF.Gelu, scale=scales[:, j:j + 1])
                    ins.then_inc(*lb.bump("evs"))
                    banks.set_guard(holders["b"][0], "evs", lb.mark("evs"))
                lb.s(s_g)
                pump(1)
            return ("evs", lb.mark("evs"))

        g1_done = hidden_stage(A, B, s3s, h_done, wi0 + 64)
        g2_done = hidden_stage(B, A, s4s, g1_done, wi0 + 96)

        def y_resid(sy, t=t):
            if t > 0:
                lb.wait(sy, "evv", rfree_mark[t - 1])
            elif io["stream_dep"] is not None:
                lb.wait(sy, *io["stream_dep"])
            sy.dma_start(resid[:], rearr_pct(xin_h, TT * t, TT)
                         ).then_inc(*lb.bump("rld", 16))
        lb.y(y_resid)
        rld_mark = lb.mark("rld")

        for j in range(KC):
            holders = {}

            def t_w3(te, t=t, j=j, holders=holders, g2_done=g2_done):
                if j == 0:
                    lb.wait(te, *g2_done)
                wt = wbuf.consume(te, wi0 + 128 + j)
                bi, bt = banks.acquire(te)
                for c in range(32):
                    ins = te.matmul(bt[:], wt[:, c, :], A[:, c, :],
                                    start=(c == 0), stop=(c == 31))
                ins.then_inc(*lb.bump("mm"))
                holders["m"] = lb.mark("mm")
                wbuf.release_at(wi0 + 128 + j, "mm", holders["m"])
                holders["b"] = (bi, bt)
            lb.t(t_w3)

            def v_y(ve, j=j, holders=holders, rld_mark=rld_mark):
                if j == 0:
                    lb.wait(ve, "rld", rld_mark)
                lb.wait(ve, "mm", holders["m"])
                ins = ve.scalar_tensor_tensor(
                    ytile[:, j, :], holders["b"][1][:], s5s[:, j:j + 1],
                    resid[:, j, :], ALU.mult, ALU.add)
                ins.then_inc(*lb.bump("evv"))
                banks.set_guard(holders["b"][0], "evv", lb.mark("evv"))
            lb.v(v_y)
            pump(1)
        rfree_mark[t] = lb.mark("evv")
        z_dep = ("evv", lb.mark("evv"))

        ln_dep = emit_ln(lb, banks, aux, ytile, yout, z_dep, nt=TT,
                         out_extra_deps=[("st", 16 * t)] if t > 0 else ())

        def y_store(sy, t=t, ln_dep=ln_dep):
            lb.wait(sy, *ln_dep)
            sy.dma_start(rearr_pct(xout_h, TT * t, TT), yout[:]
                         ).then_inc(*lb.bump("st", 16))
        lb.y(y_store)
        if t == 0:
            pump_x(1)
        elif t < NT - 1:
            pump_x(t + 1)

    return (lb.qual("st"), lb.mark("st"))


def host_l0ffn_inputs(inputs, xT_slices):
    w1T = np.ascontiguousarray(inputs["L0_ffn_w1_w"].T).astype(np.float32)
    w2T = np.ascontiguousarray(inputs["L0_ffn_w2_w"].T).astype(np.float32)
    h1T = np.ascontiguousarray(inputs["L0_ffn_h1_w"].T).astype(np.float32)
    h2T = np.ascontiguousarray(inputs["L0_ffn_h2_w"].T).astype(np.float32)
    w3T = np.ascontiguousarray(inputs["L0_ffn_w3_w"].T).astype(np.float32)
    s1 = np.ascontiguousarray(inputs["L0_ffn_w1_s"].reshape(32, 128).T)
    s2 = np.ascontiguousarray(inputs["L0_ffn_w2_s"].reshape(32, 128).T)
    s3 = np.ascontiguousarray(inputs["L0_ffn_h1_s"].reshape(32, 128).T)
    s4 = np.ascontiguousarray(inputs["L0_ffn_h2_s"].reshape(32, 128).T)
    s5 = np.ascontiguousarray(inputs["L0_ffn_w3_s"].reshape(8, 128).T)
    common = dict(w1T=w1T, w2T=w2T, h1T=h1T, h2T=h2T, w3T=w3T,
                  s1=s1, s2=s2, s3=s3, s4=s4, s5=s5,
                  onescol=np.ones((128, 1), np.float32),
                  onesrow=np.ones((1, 128), np.float32))
    if xT_slices is None:
        return [dict(common) for c in range(NCORES)]
    return [dict(common, xinT=np.ascontiguousarray(xT_slices[c]))
            for c in range(NCORES)]




def build_l1ffn():
    lb = LB()
    xin_h = lb.inp("xinT", [D, TC])
    ocol_h = lb.inp("onescol", [128, 1])
    orow_h = lb.inp("onesrow", [1, 128])
    id128_h = lb.inp("ident128", [128, 128])
    xout_h = lb.out("xoutT", [D, TC])
    lb.start_block(["shared"])
    onescol = lb.sb("sh_onescol", [128, 1], F32)
    onesrow = lb.sb("sh_onesrow", [1, 128], F32)
    id128 = lb.sb("sh_id128", [128, 128], F32)

    def y_shc(sy):
        for dst, srcc in [(onescol[:, :], ocol_h[:]),
                          (onesrow[:, :], orow_h[:]),
                          (id128[:, :], id128_h[:])]:
            sy.dma_start(dst, srcc).then_inc(*lb.bump("@shared", 16))
    lb.y(y_shc)
    io = {"xin": xin_h, "xout": xout_h, "onescol": onescol,
          "onesrow": onesrow, "id128": id128,
          "shc_dep": ("@shared", lb.mark("@shared")), "stream_dep": None}
    dep = emit_l1ffn(lb, io)

    def y_final(sy):
        lb.wait(sy, *dep)
    lb.y(y_final)
    return lb.done()


def emit_l1ffn(lb, io):
    xin_h = io["xin"]
    xout_h = io["xout"]
    onescol = io["onescol"]
    onesrow = io["onesrow"]
    id128 = io["id128"]
    ws1_h = get_inp(lb, "ws1", [D, D])
    ws2_h = get_inp(lb, "ws2", [D, D])
    we1_h = get_inp(lb, "We1", [31, D, D])
    we2_h = get_inp(lb, "We2", [31, D, D])
    wr_h = lb.inp("Wr", [D, 31])
    brbc_h = lb.inp("br_bc", [128, 31])
    bs1_h = lb.inp("bs1pc", [128, 8])
    bs2_h = lb.inp("bs2pc", [128, 8])
    be1_h = lb.inp("be1pc", [128, 31 * 8])
    be2_h = lb.inp("be2r", [31, D])
    if DBG_L1FFN:
        dbg_gates_h = lb.out("dbg_gates", [128, 8, 31])
        dbg_gtmp_h = lb.out("dbg_gtmp", [31, 1024])
        dbg_rinit_h = lb.out("dbg_rinit", [D, 1024])
        dbg_shared_h = lb.out("dbg_shared", [D, 1024])
        dbg_e0_h = lb.out("dbg_e0", [D, 1024])
        dbg_h0_h = lb.out("dbg_h0", [D, 1024])
        dbg_gbc_h = lb.out("dbg_gbc", [128, 1024])
        dbg_grow_h = lb.out("dbg_grow", [1, 1024])

    TT = 1024
    NT = 1 if DBG_NT1 else TC // TT
    NF = TT // 512
    NSLOT = 4

    xbf = lb.sb("xbf", [128, KC, TT], BF16)
    xf32 = lb.sb("xf32", [128, KC, TT], F32)
    routed = lb.sb("routed", [128, KC, TT], F32)
    hbuf = lb.sb("hbuf", [128, KC, TT], BF16)
    s1buf = lb.sb("s1buf", [128, KC, TT], BF16)
    wk = [lb.sb(f"wk{i}", [128, KC, 512], BF16) for i in range(NSLOT)]
    wr_sb = lb.sb("wr_sb", [128, KC, 31], F32)
    brbc = lb.sb("brbc", [128, 31], F32)
    bs1s = lb.sb("bs1s", [128, 8], F32)
    bs2s = lb.sb("bs2s", [128, 8], F32)
    be1s = lb.sb("be1s", [128, 31 * 8], F32)
    be2s = lb.sb("be2s", [31, D], F32)
    growrow = lb.sb("growrow", [1, TT], F32)
    gaff = lb.sb("gaff", [128, KC, 31], F32)
    grow = lb.sb("grow", [128, 6], F32)
    gates = lb.sb("gates", [128, KC, 31], F32)
    gtmp = lb.sb("gtmp", [31, TT], F32)
    gbc = lb.sb("gbc", [128, TT], F32)
    htmp = lb.sb("htmp", [128, 4, TT], F32)
    gscr = lb.sb("gscr", [128, 31], F32)
    yout = lb.sb("yout2", [128, KC, 512], F32)
    statrow = lb.sb("statrow2", [1, 3 * 512], F32)

    pb = [lb.ps(f"pb{i}", [128, 512]) for i in range(8)]

    banks = Banks(lb, pb)
    wbuf = WBuf(lb, wk, "wk")
    zsq_view = htmp[:].rearrange("p a (b t) -> p (a b) t", t=512)
    aux = {"onescol": onescol, "onesrow": onesrow, "zsq": zsq_view,
           "statrow": statrow}

    def y_const(sy):
        lb.wait(sy, *io["shc_dep"])
        for dst, src in [(wr_sb, wr_h.rearrange("(c p) e -> p c e", p=128)),
                         (brbc, brbc_h[:]), (bs1s, bs1_h[:]),
                         (bs2s, bs2_h[:]), (be1s, be1_h[:]),
                         (be2s, be2_h[:])]:
            sy.dma_start(dst[:, :] if hasattr(dst, "shape") and len(dst.shape) == 2
                         else dst[:], src).then_inc(*lb.bump("cld", 16))
    lb.y(y_const)
    CLD = lb.mark("cld")

    wcd = io.get("wc3")
    unique = [rearr_pcn(ws1_h, 0, 512), rearr_pcn(ws1_h, 512, 512),
              rearr_pcn(ws2_h, 0, 512), rearr_pcn(ws2_h, 512, 512)]
    for e in range(31):
        unique.append(rearr_pcn(we1_h[e], 0, 512))
        unique.append(rearr_pcn(we1_h[e], 512, 512))
        unique.append(rearr_pcn(we2_h[e], 0, 512))
        unique.append(rearr_pcn(we2_h[e], 512, 512))
    WPT = 4 + 31 * 4
    if wcd is None:
        plan = unique * NT
    else:
        plan = list(unique)
        for t in range(1, NT):
            for k in range(WPT):
                plan.append(wcd[k][:].rearrange("p (c m) -> p c m", m=512))
    pumped = [0]
    store_total = {}

    def pump(n):
        n = min(n, len(plan) - pumped[0])
        if n <= 0:
            return

        for _ in range(n):
            idx = pumped[0]
            cached = wcd is not None and idx >= WPT

            def one_load(eng, idx=idx):
                if wcd is not None and idx == WPT:
                    for s2 in range(NSLOT):
                        lb.wait(eng, f"wkc{s2}", store_total[s2])
                wbuf.emit_load(eng, plan[idx], KC)
                if wcd is not None and idx < WPT:
                    s2 = (wbuf.n_loads - 1) % NSLOT
                    tgt = wbuf.load_target[-1]
                    cap = wcd[idx][:].rearrange("p (c m) -> p c m", m=512)

                    def y_st(sy, s2=s2, tgt=tgt, cap=cap):
                        lb.wait(sy, *tgt)
                        sy.dma_start(cap, wk[s2][:, 0:KC, :]
                                     ).then_inc(*lb.bump(f"wkc{s2}", 16))
                    lb.y(y_st)
                    store_total[s2] = lb.mark(f"wkc{s2}")
                    wbuf.release_at(wbuf.n_loads - 1, f"wkc{s2}",
                                    store_total[s2])
            if cached:
                lb.s(one_load)
            else:
                lb.g(one_load)
            pumped[0] += 1

    xfree_mark = {}

    def pump_x(t):
        def g_x(g):
            if t > 0:
                lb.wait(g, "mm", xfree_mark[t - 1])
            elif io["stream_dep"] is not None:
                lb.wait(g, *io["stream_dep"])
            g.dma_start(xbf[:], rearr_pct(xin_h, TT * t, TT)
                        ).then_inc(*lb.bump("xld", 16))
        lb.g(g_x)

        def y_x(sy):
            if t > 0:
                lb.wait(sy, "evv", xf32_free[t - 1])
            elif io["stream_dep"] is not None:
                lb.wait(sy, *io["stream_dep"])
            sy.dma_start(xf32[:], rearr_pct(xin_h, TT * t, TT)
                         ).then_inc(*lb.bump("xfld", 16))
        lb.y(y_x)

    xf32_free = {}
    pump_x(0)
    pump(NSLOT)

    for t in range(NT):
        wi0 = t * WPT

        tr_marks = []
        for sti in range(KC):
            holders = {}

            def t_router(te, sti=sti, t=t, holders=holders):
                if sti == 0:
                    lb.wait(te, "cld", CLD)
                    lb.wait(te, "xfld", 16 * (t + 1))
                bi, bt = banks.acquire(te)
                for c in range(KC):
                    ins = te.matmul(bt[:, 0:31],
                                    xf32[:, c, 128 * sti:128 * (sti + 1)],
                                    wr_sb[:, c, :],
                                    start=(c == 0), stop=(c == KC - 1))
                ins.then_inc(*lb.bump("mm"))
                holders["m"] = lb.mark("mm")
                holders["b"] = (bi, bt)
            lb.t(t_router)

            def v_softmax_gates(v, sti=sti, holders=holders):
                if sti == 0:
                    lb.wait(v, "cld", CLD)
                lb.wait(v, "mm", holders["m"])
                a = gaff[:, sti, :]
                lb.ss(v, "evv", v.tensor_add(a, holders["b"][1][:, 0:31],
                                             brbc[:]))
                lb.ss(v, "evv", v.reduce_max(grow[:, 3:4], a, axis=AX.X))
                ins = v.tensor_scalar_mul(grow[:, 3:4], grow[:, 3:4], -1.0)
                ins.then_inc(*lb.bump("evv"))
                holders["aff0"] = lb.mark("evv")
                banks.set_guard(holders["b"][0], "evv", holders["aff0"])
            lb.v(v_softmax_gates)

            def s_exp(se, sti=sti, holders=holders):
                lb.wait(se, "evv", holders["aff0"])
                ins = se.activation(gaff[:, sti, :], gaff[:, sti, :], AF.Exp,
                                    bias=grow[:, 3:4])
                ins.then_inc(*lb.bump("evs"))
                holders["exp"] = lb.mark("evs")
            lb.s(s_exp)

            def v_soft2(v, sti=sti, holders=holders):
                lb.wait(v, "evs", holders["exp"])
                a = gaff[:, sti, :]
                g = gates[:, sti, :]

                def step(ins):
                    ins.then_inc(*lb.bump("evv"))
                    lb.wait(v, "evv", lb.mark("evv"))
                step(v.reduce_sum(grow[:, 4:5], a, axis=AX.X))
                step(v.reciprocal(grow[:, 5:6], grow[:, 4:5]))
                step(v.tensor_scalar_mul(a, a, grow[:, 5:6]))
                step(v.reduce_max(grow[:, 0:1], a, axis=AX.X))
                step(v.tensor_single_scalar(g, a, grow[:, 0:1], ALU.is_equal))
                step(v.scalar_tensor_tensor(g, g, -1e30, a, ALU.mult, ALU.add))
                step(v.reduce_max(grow[:, 1:2], g, axis=AX.X))
                step(v.tensor_single_scalar(gscr[:], g, grow[:, 1:2],
                                            ALU.is_equal))
                step(v.scalar_tensor_tensor(g, gscr[:], -1e30, g,
                                            ALU.mult, ALU.add))
                step(v.reduce_max(grow[:, 2:3], g, axis=AX.X))
                step(v.tensor_single_scalar(g, a, grow[:, 2:3], ALU.is_ge))
                ins = v.tensor_mul(g, g, a)
                ins.then_inc(*lb.bump("evv"))
                holders["gates"] = lb.mark("evv")
            lb.v(v_soft2)

            def t_tr(te, sti=sti, holders=holders):
                lb.wait(te, "evv", holders["gates"])
                bi, bt = banks.acquire(te)
                te.transpose(bt[0:31, 0:128], gates[:, sti, :], id128[:]
                             ).then_inc(*lb.bump("mm"))
                holders["mtr"] = lb.mark("mm")
                holders["btr"] = (bi, bt)
            lb.t(t_tr)

            def v_trev(v, sti=sti, holders=holders):
                lb.wait(v, "mm", holders["mtr"])
                ins = v.tensor_copy(gtmp[0:31, 128 * sti:128 * (sti + 1)],
                                    holders["btr"][1][0:31, 0:128])
                ins.then_inc(*lb.bump("evv"))
                banks.set_guard(holders["btr"][0], "evv", lb.mark("evv"))
            lb.v(v_trev)
            tr_marks.append(lb.mark("evv"))
        gates_done = ("evv", lb.mark("evv"))
        if DBG_L1FFN and t == 0:
            def y_dbg1(sy):
                lb.wait(sy, *gates_done)
                sy.dma_start(dbg_gates_h[:], gates[:]).then_inc(*lb.bump("st", 16))
                sy.dma_start(dbg_gtmp_h[:], gtmp[:]).then_inc(*lb.bump("st", 16))
            lb.y(y_dbg1)

        for j in range(KC):
            holders = {}

            def t_binit(te, j=j, holders=holders):
                if j == 0:
                    lb.wait(te, *gates_done)
                for f in range(NF):
                    bi, bt = banks.acquire(te)
                    ins = te.matmul(bt[:], be2s[:, 128 * j:128 * (j + 1)],
                                    gtmp[:, 512 * f:512 * (f + 1)],
                                    start=True, stop=True)
                    ins.then_inc(*lb.bump("mm"))
                    holders[f] = (bi, bt, lb.mark("mm"))
            lb.t(t_binit)

            def s_binit(se, j=j, holders=holders):
                for f in range(NF):
                    bi, bt, mk = holders[f]
                    lb.wait(se, "mm", mk)
                    ins = se.add(routed[:, j, 512 * f:512 * (f + 1)],
                                 bt[:], bs2s[:, j:j + 1])
                    ins.then_inc(*lb.bump("evs"))
                    banks.set_guard(bi, "evs", lb.mark("evs"))
            lb.s(s_binit)
        rinit_done = ("evs", lb.mark("evs"))
        if DBG_L1FFN and t == 0:
            def y_dbg2(sy):
                lb.wait(sy, *rinit_done)
                sy.dma_start(rearr_pct(dbg_rinit_h, 0, TT), routed[:]
                             ).then_inc(*lb.bump("st", 16))
            lb.y(y_dbg2)

        for half in range(2):
            holders = {}

            def t_s1(te, half=half, holders=holders):
                if half == 0:
                    lb.wait(te, "xld", 16 * (t + 1))
                w = wbuf.consume(te, wi0 + half)
                for j4 in range(4):
                    j = half * 4 + j4
                    for f in range(NF):
                        bi, bt = banks.acquire(te)
                        for c in range(KC):
                            ins = te.matmul(
                                bt[:], w[:, c, 128 * j4:128 * (j4 + 1)],
                                xbf[:, c, 512 * f:512 * (f + 1)],
                                start=(c == 0), stop=(c == KC - 1))
                        ins.then_inc(*lb.bump("mm"))
                        holders[(j4, f)] = (bi, bt, lb.mark("mm"))
                wbuf.release_at(wi0 + half, "mm", lb.mark("mm"))
            lb.t(t_s1)

            def s_s1(se, half=half, holders=holders):
                for j4 in range(4):
                    j = half * 4 + j4
                    for f in range(NF):
                        bi, bt, mk = holders[(j4, f)]
                        lb.wait(se, "mm", mk)
                        ins = se.activation(
                            s1buf[:, j, 512 * f:512 * (f + 1)], bt[:],
                            AF.Gelu, bias=bs1s[:, j:j + 1])
                        ins.then_inc(*lb.bump("evs"))
                        banks.set_guard(bi, "evs", lb.mark("evs"))
            lb.s(s_s1)
        s1_done = ("evs", lb.mark("evs"))

        for half in range(2):
            holders = {}

            def t_s2(te, half=half, holders=holders):
                if half == 0:
                    lb.wait(te, *s1_done)
                w = wbuf.consume(te, wi0 + 2 + half)
                for j4 in range(4):
                    j = half * 4 + j4
                    for f in range(NF):
                        bi, bt = banks.acquire(te)
                        for c in range(KC):
                            ins = te.matmul(
                                bt[:], w[:, c, 128 * j4:128 * (j4 + 1)],
                                s1buf[:, c, 512 * f:512 * (f + 1)],
                                start=(c == 0), stop=(c == KC - 1))
                        ins.then_inc(*lb.bump("mm"))
                        holders[(j4, f)] = (bi, bt, lb.mark("mm"))
                wbuf.release_at(wi0 + 2 + half, "mm", lb.mark("mm"))
            lb.t(t_s2)

            def v_s2(v, half=half, holders=holders):
                if DBG_L1FFN:
                    lb.wait(v, "st", lb.mark("st"))
                lb.wait(v, *rinit_done)
                for j4 in range(4):
                    j = half * 4 + j4
                    for f in range(NF):
                        bi, bt, mk = holders[(j4, f)]
                        lb.wait(v, "mm", mk)
                        sl = routed[:, j, 512 * f:512 * (f + 1)]
                        ins = v.tensor_add(sl, sl, bt[:])
                        ins.then_inc(*lb.bump("evv"))
                        banks.set_guard(bi, "evv", lb.mark("evv"))
            lb.v(v_s2)
        pump(4)

        if DBG_L1FFN and t == 0:
            def y_dbg3(sy):
                lb.wait(sy, "evv", lb.mark("evv"))
                sy.dma_start(rearr_pct(dbg_shared_h, 0, TT), routed[:]
                             ).then_inc(*lb.bump("st", 16))
            lb.y(y_dbg3)
        carry = {"prev_hg": None}
        for e in range(31):
            holders = {"prev_hg": carry["prev_hg"]}

            if DBG_L1FFN and t == 0 and e == 1:
                def y_dbg5(sy):
                    lb.wait(sy, "evv", lb.mark("evv"))
                    sy.dma_start(dbg_gbc_h[:], gbc[:]).then_inc(*lb.bump("st", 16))
                    sy.dma_start(dbg_grow_h[:], growrow[:]).then_inc(*lb.bump("st", 16))
                lb.y(y_dbg5)

            def t_grow(te, e=e, holders=holders):
                if DBG_L1FFN:
                    lb.wait(te, "st", lb.mark("st"))
                for f in range(NF):
                    bi, bt = banks.acquire(te)
                    ins = te.matmul(bt[0:1, :], id128[0:31, e:e + 1],
                                    gtmp[:, 512 * f:512 * (f + 1)],
                                    start=True, stop=True)
                    ins.then_inc(*lb.bump("mm"))
                    holders[("gr", f)] = (bi, bt, lb.mark("mm"))
            lb.t(t_grow)

            def v_grow(v, e=e, holders=holders):
                for f in range(NF):
                    bi, bt, mk = holders[("gr", f)]
                    lb.wait(v, "mm", mk)
                    ins = v.tensor_copy(growrow[0:1, 512 * f:512 * (f + 1)],
                                        bt[0:1, :])
                    ins.then_inc(*lb.bump("evv"))
                    banks.set_guard(bi, "evv", lb.mark("evv"))
                holders["grow"] = lb.mark("evv")
            lb.v(v_grow)

            def t_gbc(te, e=e, holders=holders):
                lb.wait(te, "evv", holders["grow"])
                for f in range(NF):
                    bi, bt = banks.acquire(te)
                    ins = te.matmul(bt[:], onesrow[:],
                                    growrow[0:1, 512 * f:512 * (f + 1)],
                                    start=True, stop=True)
                    ins.then_inc(*lb.bump("mm"))
                    holders[("g", f)] = (bi, bt, lb.mark("mm"))
            lb.t(t_gbc)

            def v_gbc(v, e=e, holders=holders):
                for f in range(NF):
                    bi, bt, mk = holders[("g", f)]
                    lb.wait(v, "mm", mk)
                    ins = v.tensor_copy(gbc[:, 512 * f:512 * (f + 1)], bt[:])
                    ins.then_inc(*lb.bump("evv"))
                    banks.set_guard(bi, "evv", lb.mark("evv"))
                holders["gbc"] = lb.mark("evv")
            lb.v(v_gbc)

            for half in range(2):
                hh = {}

                def t_h(te, e=e, half=half, hh=hh):
                    w = wbuf.consume(te, wi0 + 4 + 4 * e + half)
                    for j4 in range(4):
                        for f in range(NF):
                            bi, bt = banks.acquire(te)
                            for c in range(KC):
                                ins = te.matmul(
                                    bt[:], w[:, c, 128 * j4:128 * (j4 + 1)],
                                    xbf[:, c, 512 * f:512 * (f + 1)],
                                    start=(c == 0), stop=(c == KC - 1))
                            ins.then_inc(*lb.bump("mm"))
                            hh[(j4, f)] = (bi, bt, lb.mark("mm"))
                    wbuf.release_at(wi0 + 4 + 4 * e + half, "mm",
                                    lb.mark("mm"))
                lb.t(t_h)

                def s_h(se, e=e, half=half, hh=hh, holders=holders):
                    if holders.get("prev_hg") is not None:
                        lb.wait(se, "evv", holders["prev_hg"])
                    for j4 in range(4):
                        for f in range(NF):
                            bi, bt, mk = hh[(j4, f)]
                            lb.wait(se, "mm", mk)
                            ins = se.activation(
                                htmp[:, j4, 512 * f:512 * (f + 1)],
                                bt[:], AF.Gelu,
                                bias=be1s[:, 8 * e + half * 4 + j4:
                                          8 * e + half * 4 + j4 + 1])
                            ins.then_inc(*lb.bump("evs"))
                            hh[("gel", j4, f)] = lb.mark("evs")
                            banks.set_guard(bi, "evs", lb.mark("evs"))
                lb.s(s_h)

                def v_hg(v, e=e, half=half, hh=hh, holders=holders):
                    if DBG_L1FFN:
                        lb.wait(v, "st", lb.mark("st"))
                    if half == 0:
                        lb.wait(v, "evv", holders["gbc"])
                    for j4 in range(4):
                        j = half * 4 + j4
                        for f in range(NF):
                            lb.wait(v, "evs", hh[("gel", j4, f)])
                            ins = v.tensor_mul(
                                hbuf[:, j, 512 * f:512 * (f + 1)],
                                htmp[:, j4, 512 * f:512 * (f + 1)],
                                gbc[:, 512 * f:512 * (f + 1)])
                            ins.then_inc(*lb.bump("evv"))
                lb.v(v_hg)
                holders["prev_hg"] = lb.mark("evv")
            holders["h_done"] = lb.mark("evv")
            carry["prev_hg"] = holders["h_done"]

            for half in range(2):
                hh = {}

                def t_c(te, e=e, half=half, hh=hh, holders=holders):
                    if half == 0:
                        lb.wait(te, "evv", holders["h_done"])
                    w = wbuf.consume(te, wi0 + 4 + 4 * e + 2 + half)
                    for j4 in range(4):
                        for f in range(NF):
                            bi, bt = banks.acquire(te)
                            for c in range(KC):
                                ins = te.matmul(
                                    bt[:], w[:, c, 128 * j4:128 * (j4 + 1)],
                                    hbuf[:, c, 512 * f:512 * (f + 1)],
                                    start=(c == 0), stop=(c == KC - 1))
                            ins.then_inc(*lb.bump("mm"))
                            hh[(j4, f)] = (bi, bt, lb.mark("mm"))
                    wbuf.release_at(wi0 + 4 + 4 * e + 2 + half, "mm",
                                    lb.mark("mm"))
                lb.t(t_c)

                def v_c(v, e=e, half=half, hh=hh):
                    for j4 in range(4):
                        j = half * 4 + j4
                        for f in range(NF):
                            bi, bt, mk = hh[(j4, f)]
                            lb.wait(v, "mm", mk)
                            sl = routed[:, j, 512 * f:512 * (f + 1)]
                            ins = v.tensor_add(sl, sl, bt[:])
                            ins.then_inc(*lb.bump("evv"))
                            banks.set_guard(bi, "evv", lb.mark("evv"))
                lb.v(v_c)
            pump(4)
            if DBG_L1FFN and t == 0 and e == 0:
                def y_dbg4(sy):
                    lb.wait(sy, "evv", lb.mark("evv"))
                    sy.dma_start(rearr_pct(dbg_e0_h, 0, TT), routed[:]
                                 ).then_inc(*lb.bump("st", 16))
                lb.y(y_dbg4)

                def g_dbg4(g):
                    lb.wait(g, "evv", lb.mark("evv"))
                    g.dma_start(rearr_pct(dbg_h0_h, 0, TT), hbuf[:]
                                ).then_inc(*lb.bump("st", 16))
                lb.g(g_dbg4)
        routed_done = ("evv", lb.mark("evv"))

        for st_i in range(TT // 512):
            zh = {}

            def v_z(v, t=t, st_i=st_i, zh=zh):
                lb.wait(v, *routed_done)
                for c in range(KC):
                    sl = routed[:, c, 512 * st_i:512 * (st_i + 1)]
                    ins = v.scalar_tensor_tensor(
                        sl, xf32[:, c, 512 * st_i:512 * (st_i + 1)],
                        2.0, sl, ALU.mult, ALU.add)
                ins.then_inc(*lb.bump("evv"))
                zh["z"] = lb.mark("evv")
            lb.v(v_z)

            ln_dep = emit_ln(lb, banks, aux, routed, yout,
                             ("evv", zh["z"]), nt=512, z_off=512 * st_i,
                             out_extra_deps=[("st", lb.mark("st"))])

            def y_store(sy, t=t, st_i=st_i, ln_dep=ln_dep):
                lb.wait(sy, *ln_dep)
                sy.dma_start(rearr_pct(xout_h, TT * t + 512 * st_i, 512),
                             yout[:]).then_inc(*lb.bump("st", 16))
            lb.y(y_store)
        xfree_mark[t] = lb.mark("mm")
        xf32_free[t] = lb.mark("evv")
        if t + 1 < NT:
            pump_x(t + 1)

    return (lb.qual("st"), lb.mark("st"))


def host_l1ffn_inputs(inputs, xT_slices):
    ws1 = np.ascontiguousarray(inputs["L1_ffn_ws1"]).astype(np.float32)
    ws2 = np.ascontiguousarray(inputs["L1_ffn_ws2"]).astype(np.float32)
    we1 = np.ascontiguousarray(inputs["L1_ffn_We1"]).astype(np.float32)
    we2 = np.ascontiguousarray(inputs["L1_ffn_We2"]).astype(np.float32)
    wr = np.ascontiguousarray(inputs["L1_ffn_Wr"]).astype(np.float32)
    br_bc = np.ascontiguousarray(
        np.broadcast_to(inputs["L1_ffn_br"][None, :], (128, 31))).astype(np.float32)
    bs1pc = np.ascontiguousarray(inputs["L1_ffn_bs1"].reshape(8, 128).T)
    bs2pc = np.ascontiguousarray(inputs["L1_ffn_bs2"].reshape(8, 128).T)
    be1 = inputs["L1_ffn_Be1"].reshape(31, 8, 128)
    be1pc = np.ascontiguousarray(be1.transpose(2, 0, 1).reshape(128, 31 * 8))
    be2r = np.ascontiguousarray(inputs["L1_ffn_Be2"]).astype(np.float32)
    common = dict(ws1=ws1, ws2=ws2, We1=we1, We2=we2, Wr=wr, br_bc=br_bc,
                  bs1pc=bs1pc, bs2pc=bs2pc, be1pc=be1pc, be2r=be2r,
                  onescol=np.ones((128, 1), np.float32),
                  onesrow=np.ones((1, 128), np.float32),
                  ident128=np.eye(128, dtype=np.float32))
    if xT_slices is None:
        return [dict(common) for c in range(NCORES)]
    return [dict(common, xinT=np.ascontiguousarray(xT_slices[c]))
            for c in range(NCORES)]




def emit_ag_prefix(lb, tag, src_sb, P, X, maskpfx, dst_sb, deps, gath):
    nc = lb.nc
    ag_in = lb.dram(f"ag_{tag}_in", [P, X])
    ag_out = lb.dram(f"ag_{tag}_out", [8 * P, X])

    xcap = gath.shape[2]
    npieces = ceil_div(X, xcap)

    def g_ag(g):
        lb.wait(g, *deps)
        g.dma_start(ag_in[:], src_sb[0:P, 0:X]).then_inc(*lb.bump("agd", 16))
        lb.wait(g, "agd", lb.mark("agd"))
        g.collective_compute(
            "AllGather", ALU.bypass, RG,
            ins=[ag_in.opt()], outs=[ag_out.opt()]).then_inc(*lb.bump("cc"))
        lb.wait(g, "cc", lb.mark("cc"))
    lb.g(g_ag)

    for pi in range(npieces):
        x0 = pi * xcap
        x1 = min(X, x0 + xcap)

        def g_ld(g, x0=x0, x1=x1):
            if x0 > 0:
                lb.wait(g, "evv", lb.mark("evv"))
            g.dma_start(gath[:, :, 0:x1 - x0],
                        ag_out[:, x0:x1].rearrange("(r p) x -> p r x", p=P),
                        allow_slow_non_contiguous=(x1 - x0) <= 4
                        ).then_inc(*lb.bump("agd", 16))
        lb.g(g_ld)
        AGD = lb.mark("agd")

        def v_pfx(v, x0=x0, x1=x1, AGD=AGD):
            lb.wait(v, "agd", AGD)
            lb.ss(v, "evv", v.memset(dst_sb[0:P, x0:x1], 0.0))
            for r in range(8):
                lb.ss(v, "evv", v.scalar_tensor_tensor(
                    dst_sb[0:P, x0:x1], gath[:, r, 0:x1 - x0],
                    maskpfx[0:P, r:r + 1],
                    dst_sb[0:P, x0:x1], ALU.mult, ALU.add))
            v.engine_nop().then_inc(*lb.bump("evv"))
        lb.v(v_pfx)
    return ("evv", lb.mark("evv"))


def build_l1attn():
    lb = LB()
    xin_h = lb.inp("xinT", [D, TC])
    ocol_h = lb.inp("onescol", [128, 1])
    orow_h = lb.inp("onesrow", [1, 128])
    xout_h = lb.out("xoutT", [D, TC])
    lb.start_block(["shared"], allow_lp=True)
    onescol = lb.sb("sh_onescol", [128, 1], F32)
    onesrow = lb.sb("sh_onesrow", [1, 128], F32)

    def y_shc(sy):
        sy.dma_start(onescol[:, :], ocol_h[:]).then_inc(*lb.bump("@shared", 16))
        sy.dma_start(onesrow[:, :], orow_h[:]).then_inc(*lb.bump("@shared", 16))
    lb.y(y_shc)
    io = {"xin": xin_h, "xout": xout_h, "onescol": onescol,
          "onesrow": onesrow, "shc_dep": ("@shared", lb.mark("@shared")),
          "stream_dep": None}
    dep = emit_l1attn(lb, io)

    def y_final(sy):
        lb.wait(sy, *dep)
    lb.y(y_final)
    return lb.done()


def emit_l1attn(lb, io):
    xin_h = io["xin"]
    xout_h = io["xout"]
    onescol = io["onescol"]
    onesrow = io["onesrow"]
    bg = io.get("bg_pump")
    wq_h = lb.inp("Wq", [D, D])
    wk_h = lb.inp("Wk", [D, D])
    wv_h = lb.inp("Wv", [D, D])
    wo_h = lb.inp("Wo", [D, D])
    bq_h = lb.inp("bqpc", [128, 8])
    bk_h = lb.inp("bkpc", [128, 8])
    bv_h = lb.inp("bvpc", [128, 8])
    bo_h = lb.inp("bopc", [128, 8])
    nrm_h = lb.inp("nrm16", [16, TC])
    hsum_h = lb.inp("hsum", [128, 8, 16], BF16)
    hbc_h = lb.inp("hbcast", [16, 1024], BF16)
    tri_h = lb.inp("trimask", [128, 128], BF16)
    idb_h = lb.inp("id128b", [128, 128], BF16)
    mp_h = lb.inp("maskpfx", [128, 8])
    if DBG_L1ATTN:
        dbg_k_h = lb.out("dbg_k", [128, KC, TC], BF16)
        dbg_q_h = lb.out("dbg_q", [128, KC, TC], BF16)
        dbg_si_h = lb.out("dbg_si", [16, TC], BF16)
        dbg_son_h = lb.out("dbg_son", [16, TC], BF16)
        dbg_sa_h = lb.out("dbg_sa", [16, TC], BF16)
        dbg_cs_h = lb.out("dbg_cs", [16, TC], BF16)
        dbg_sc_h = lb.out("dbg_sc", [16, TC], BF16)
        dbg_scal_h = lb.out("dbg_scal", [16, TC], BF16)
        dbg_kv_h = lb.out("dbg_kv", [128, KC, 64])
        dbg_ofm_h = lb.out("dbg_ofm", [128, KC, TC], BF16)
        dbg_carry_h = lb.out("dbg_carry", [128, 16])
        dbg_x_h = lb.out("dbg_x", [128, KC, TC], BF16)
        dbg_q2_h = lb.out("dbg_q2", [128, KC, TC], BF16)
        dbg_kso_h = lb.out("dbg_kso", [128, 512], BF16)
        dbg_acc1_h = lb.out("dbg_acc1", [16, 512])
        dbg_acc_h = lb.out("dbg_acc", [16, 512])

    NSLOT = 3
    NTCH = TC // 512
    NTC = TC // 128

    q = lb.sb("q", [128, KC, TC], BF16)
    ofm = lb.sb("ofm", [128, KC, TC], BF16)
    hsum = lb.sb("hsum_s", [128, 8, 16], BF16)
    hbc = lb.sb("hbc_s", [16, 1024], BF16)
    tri = lb.sb("tri_s", [128, 128], BF16)
    idb = lb.sb("idb_s", [128, 128], BF16)
    mpfx = lb.sb("mpfx_s", [128, 8], F32)
    bqs = lb.sb("bqs", [128, 8], F32)
    bks = lb.sb("bks", [128, 8], F32)
    bvs = lb.sb("bvs", [128, 8], F32)
    bos = lb.sb("bos", [128, 8], F32)
    wk_t = [lb.sb(f"wkt{i}", [128, KC, 128], BF16) for i in range(3)]
    nrm16 = lb.sb("nrm16_s", [16, TC], BF16)
    rn16 = lb.sb("rn16_s", [16, TC], F32)
    lsum = lb.sb("lsum", [128, 16], F32)
    carry = lb.sb("carry", [128, 16], F32)
    lsum2 = lb.sb("lsum2", [128, 16], F32)
    carry2 = lb.sb("carry2", [128, 16], F32)
    cs_sum = lb.sb("cs_sum", [16, 1], F32)
    cs_carry = lb.sb("cs_carry", [16, 1], F32)
    kv_loc = lb.sb("kv_loc", [128, KC, 64], F32)
    kv_carry = lb.sb("kv_carry", [128, KC, 64], F32)
    kv_bf = lb.sb("kv_bf", [128, KC, 64], BF16)
    zeros = lb.sb("zeros", [128, 512], BF16)
    agg1 = lb.sb("agg1", [128, 8, 16], F32)
    agg3 = lb.sb("agg3", [16, 8, 1], F32)
    agg2 = agg1

    k = lb.sb_scoped("k", [128, KC, TC], BF16)
    v = lb.sb_scoped("v", [128, KC, TC], BF16)
    xbf = lb.sb_scoped("xbf", [128, KC, TC], BF16)

    pbf32 = [lb.ps(f"pb{i}", [128, 512]) for i in range(6)]
    pbf16 = [lb.ps(f"pt{i}", [128, 256], BF16) for i in range(2)]

    banks = Banks(lb, pbf32[0:4])
    dotA_bank, dotB_bank = pbf32[4], pbf32[5]
    tbanks = Banks(lb, pbf16)
    wbuf = WBuf(lb, wk_t, "wkt")

    def y_const(sy):
        lb.wait(sy, *io["shc_dep"])
        for dst, src in [(bqs[:, :], bq_h[:]), (bks[:, :], bk_h[:]),
                         (bvs[:, :], bv_h[:]), (bos[:, :], bo_h[:]),
                         (hsum[:], hsum_h[:]),
                         (tri[:, :], tri_h[:]),
                         (mpfx[:, :], mp_h[:])]:
            sy.dma_start(dst, src).then_inc(*lb.bump("cld", 16))
    lb.y(y_const)

    def g_const(g):
        for dst, src in [(hbc[:, :], hbc_h[:]), (idb[:, :], idb_h[:]),
                         (nrm16[:, :], nrm_h[:])]:
            g.dma_start(dst, src).then_inc(*lb.bump("cld", 16))
        if io["stream_dep"] is not None:
            lb.wait(g, *io["stream_dep"])
        g.dma_start(xbf[:], rearr_pct(xin_h, 0, TC)).then_inc(*lb.bump("xld", 16))
    lb.g(g_const)
    CLD = lb.mark("cld")

    def v_zero(ve):
        lb.ss(ve, "evv", ve.memset(zeros[:], 0.0))
        lb.wait(ve, "cld", CLD)
        ins = ve.reciprocal(rn16[0:16, :], nrm16[0:16, :])
        ins.then_inc(*lb.bump("evv"))
    lb.v(v_zero)
    ZR = lb.mark("evv")

    plan = []
    for w_h in (wq_h, wk_h, wv_h):
        for j in range(KC):
            plan.append(rearr_pcn(w_h, 128 * j, 128))
    pumped = [0]

    def pump(n):
        n = min(n, len(plan) - pumped[0])
        if n <= 0:
            return

        def g_loads(g):
            for _ in range(n):
                wbuf.emit_load(g, plan[pumped[0]], KC)
                pumped[0] += 1
        lb.g(g_loads)
    pump(NSLOT)

    def proj(widx0, dst, act, bias_sb):
        for j in range(KC):
            holders = {}

            def t_p(te, j=j, holders=holders, widx0=widx0):
                if widx0 == 0 and j == 0:
                    lb.wait(te, "cld", CLD)
                    lb.wait(te, "xld", 16)
                w = wbuf.consume(te, widx0 + j)
                for f in range(4):
                    bi, bt = banks.acquire(te)
                    for c in range(KC):
                        ins = te.matmul(bt[:], w[:, c, :],
                                        xbf[:, c, 512 * f:512 * (f + 1)],
                                        start=(c == 0), stop=(c == KC - 1))
                    ins.then_inc(*lb.bump("mm"))
                    holders[f] = (bi, bt, lb.mark("mm"))
                wbuf.release_at(widx0 + j, "mm", lb.mark("mm"))
            lb.t(t_p)

            def s_p(se, j=j, holders=holders, dst=dst, act=act,
                    bias_sb=bias_sb):
                for f in range(4):
                    bi, bt, mk = holders[f]
                    lb.wait(se, "mm", mk)
                    if act is None:
                        ins = se.add(dst[:, j, 512 * f:512 * (f + 1)], bt[:],
                                     bias_sb[:, j:j + 1])
                    else:
                        ins = se.activation(dst[:, j, 512 * f:512 * (f + 1)],
                                            bt[:], act,
                                            bias=bias_sb[:, j:j + 1])
                    ins.then_inc(*lb.bump("evs"))
                    banks.set_guard(bi, "evs", lb.mark("evs"))
            lb.s(s_p)
            pump(1)
            if bg is not None:
                def g_bg(g):
                    bg(g, 5)
                lb.g(g_bg)

    proj(0, q, AF.Sigmoid, bqs)
    proj(KC, k, AF.Sigmoid, bks)
    proj(2 * KC, v, None, bvs)
    P1_done = ("evs", lb.mark("evs"))
    if DBG_L1ATTN:
        def y_dbgx(sy):
            lb.wait(sy, *P1_done)
            sy.dma_start(dbg_x_h[:], xbf[:]).then_inc(*lb.bump("st", 16))
            sy.dma_start(dbg_q2_h[:], q[:]).then_inc(*lb.bump("st", 16))
            lb.wait(sy, "st", lb.mark("st"))
        lb.y(y_dbgx)
    lb.free_sb("xbf")
    CSf = lb.sb_scoped("CSf", [16, TC], F32)
    si = lb.sb_scoped("si", [16, TC], BF16)
    sa = lb.sb_scoped("sa", [16, TC], BF16)
    csb = lb.sb_scoped("csb", [16, TC], BF16)
    scb = lb.sb_scoped("scb", [16, TC], BF16)
    scal = lb.sb_scoped("scal", [16, TC], BF16)
    son = lb.sb_scoped("son", [16, TC], BF16)
    sinb = lb.sb_scoped("sinb", [16, TC], BF16)
    KCch = lb.sb_scoped("KCch", [128, 512], F32)
    eps1 = lb.sb_scoped("eps1", [128, 512], F32)
    prodA = lb.sb_scoped("prodA", [128, 2, 512], BF16)
    prodB = lb.sb_scoped("prodB", [128, 2, 512], BF16)
    ksoc = lb.sb_scoped("ksoc", [128, 512], BF16)
    qsic = lb.sb_scoped("qsic", [128, 512], BF16)
    QCch = KCch
    eps2 = eps1

    def v_lsum(ve):
        lb.wait(ve, *P1_done)
        lb.ss(ve, "evv", ve.reduce_sum(lsum[:, 0:8], k[:], axis=AX.X))
        lb.ss(ve, "evv", ve.reduce_sum(lsum[:, 8:16], q[:], axis=AX.X))
        ve.engine_nop().then_inc(*lb.bump("evv"))
    lb.v(v_lsum)
    ag1 = emit_ag_prefix(lb, "s1", lsum, 128, 16, mpfx, carry,
                         ("evv", lb.mark("evv")), agg1)

    dotm = {}
    dot_guard = [None]
    for tch in range(NTCH):
        ts = slice(512 * tch, 512 * (tch + 1))
        th = {}

        for ch in range(KC):
            par = ch % 2
            hh = {}

            def v_b(ve, tch=tch, ts=ts, ch=ch, par=par, hh=hh):
                if tch == 0 and ch == 0:
                    lb.wait(ve, *ag1)
                prev = (tch, ch - 2) if ch >= 2 else (tch - 1, KC - 2 + ch)
                if prev in dotm:
                    lb.wait(ve, "mm", dotm[prev])
                lb.ss(ve, "evv", ve.tensor_tensor_scan(
                    KCch[:], k[:, ch, ts], zeros[:],
                    carry[:, ch:ch + 1], ALU.add, ALU.add))
                lb.ss(ve, "evv", ve.tensor_copy(carry[:, ch:ch + 1],
                                                KCch[:, 511:512]))
                lb.ss(ve, "evv", ve.tensor_scalar_add(eps1[:], q[:, ch, ts],
                                                      1e-6))
                lb.ss(ve, "evv", ve.scalar_tensor_tensor(
                    prodA[:, par, :], KCch[:], 1e-6, eps1[:],
                    ALU.add, ALU.mult))
                lb.ss(ve, "evv", ve.tensor_tensor_scan(
                    QCch[:], q[:, ch, ts], zeros[:],
                    carry[:, 8 + ch:9 + ch], ALU.add, ALU.add))
                lb.ss(ve, "evv", ve.tensor_copy(carry[:, 8 + ch:9 + ch],
                                                QCch[:, 511:512]))
                lb.ss(ve, "evv", ve.tensor_scalar_add(eps2[:], k[:, ch, ts],
                                                      1e-6))
                ins = ve.scalar_tensor_tensor(
                    prodB[:, par, :], QCch[:], 1e-6, eps2[:],
                    ALU.add, ALU.mult)
                ins.then_inc(*lb.bump("evv"))
                hh["pr"] = lb.mark("evv")
            lb.v(v_b)

            def t_d(te, tch=tch, ch=ch, par=par, hh=hh, th=th):
                if ch == 0 and dot_guard[0] is not None:
                    lb.wait(te, "evv", dot_guard[0])
                lb.wait(te, "evv", hh["pr"])
                te.matmul(dotA_bank[0:16, :], hsum[:, ch, :],
                          prodA[:, par, :],
                          start=(ch == 0), stop=(ch == KC - 1),
                          skip_group_check=True)
                ins = te.matmul(dotB_bank[0:16, :], hsum[:, ch, :],
                                prodB[:, par, :],
                                start=(ch == 0), stop=(ch == KC - 1),
                                skip_group_check=True)
                ins.then_inc(*lb.bump("mm"))
                dotm[(tch, ch)] = lb.mark("mm")
            lb.t(t_d)

        def v_si(ve, tch=tch, ts=ts, th=th):
            lb.wait(ve, "mm", dotm[(tch, KC - 1)])
            lb.ss(ve, "evv", ve.reciprocal(si[0:16, ts], dotA_bank[0:16, :]))
            lb.ss(ve, "evv", ve.reciprocal(eps1[0:16, 0:512],
                                           dotB_bank[0:16, :]))
            dot_guard[0] = lb.mark("evv")
            lb.ss(ve, "evv", ve.tensor_mul(son[0:16, ts], eps1[0:16, 0:512],
                                           nrm16[0:16, ts]))
            ins = ve.tensor_mul(sinb[0:16, ts], si[0:16, ts], nrm16[0:16, ts])
            ins.then_inc(*lb.bump("evv"))
        lb.v(v_si)
    P3_done = ("evv", lb.mark("evv"))
    if DBG_L1ATTN:
        def y_dbg3(sy):
            lb.wait(sy, *P3_done)
            sy.dma_start(dbg_k_h[:], k[:]).then_inc(*lb.bump("st", 16))
            sy.dma_start(dbg_si_h[:], si[0:16, :]).then_inc(*lb.bump("st", 16))
            sy.dma_start(dbg_son_h[:], son[0:16, :]).then_inc(*lb.bump("st", 16))
            sy.dma_start(dbg_carry_h[:], carry[:]).then_inc(*lb.bump("st", 16))
        lb.y(y_dbg3)

    def v_init_ls2(ve):
        lb.wait(ve, *P3_done)
        ve.memset(lsum2[:], 0.0).then_inc(*lb.bump("evv"))
    lb.v(v_init_ls2)

    for tch in range(NTCH):
        ts = slice(512 * tch, 512 * (tch + 1))
        for ch in range(KC):
            holders = {}

            def t_bc(te, tch=tch, ts=ts, ch=ch, holders=holders):
                i1, b1 = banks.acquire(te)
                te.matmul(b1[:], hbc[:, 128 * ch:128 * (ch + 1)],
                          son[0:16, ts], start=True, stop=True
                          ).then_inc(*lb.bump("mm"))
                holders["so"] = (i1, b1, lb.mark("mm"))
                i2, b2 = banks.acquire(te)
                te.matmul(b2[:], hbc[:, 128 * ch:128 * (ch + 1)],
                          sinb[0:16, ts], start=True, stop=True
                          ).then_inc(*lb.bump("mm"))
                holders["si"] = (i2, b2, lb.mark("mm"))
            lb.t(t_bc)

            def v_ls2(ve, tch=tch, ts=ts, ch=ch, holders=holders):
                i1, b1, mk1 = holders["so"]
                lb.wait(ve, "evv", ZR)
                lb.wait(ve, "mm", mk1)
                lb.ss(ve, "evv", ve.tensor_mul(ksoc[:], k[:, ch, ts], b1[:]))
                banks.set_guard(i1, "evv", lb.mark("evv"))
                lb.ss(ve, "evv", ve.reduce_sum(eps1[:, 0:1], ksoc[:],
                                               axis=AX.X))
                lb.ss(ve, "evv", ve.tensor_add(lsum2[:, ch:ch + 1],
                                               lsum2[:, ch:ch + 1],
                                               eps1[:, 0:1]))
                i2, b2, mk2 = holders["si"]
                lb.wait(ve, "mm", mk2)
                lb.ss(ve, "evv", ve.tensor_mul(qsic[:], q[:, ch, ts], b2[:]))
                banks.set_guard(i2, "evv", lb.mark("evv"))
                lb.ss(ve, "evv", ve.reduce_sum(eps1[:, 0:1], qsic[:],
                                               axis=AX.X))
                ins = ve.tensor_add(lsum2[:, 8 + ch:9 + ch],
                                    lsum2[:, 8 + ch:9 + ch], eps1[:, 0:1])
                ins.then_inc(*lb.bump("evv"))
            lb.v(v_ls2)
    ag2 = emit_ag_prefix(lb, "s2", lsum2, 128, 16, mpfx, carry2,
                         ("evv", lb.mark("evv")), agg2)

    dotm2 = {}
    sacs_mark = {}
    for tch in range(NTCH):
        ts = slice(512 * tch, 512 * (tch + 1))
        for ch in range(KC):
            par = ch % 2
            hh = {}

            def t_bc2(te, tch=tch, ts=ts, ch=ch, hh=hh):
                if tch == 0 and ch == 0:
                    lb.wait(te, *ag2)
                i1, b1 = banks.acquire(te)
                te.matmul(b1[:], hbc[:, 128 * ch:128 * (ch + 1)],
                          son[0:16, ts], start=True, stop=True
                          ).then_inc(*lb.bump("mm"))
                hh["so"] = (i1, b1, lb.mark("mm"))
                i2, b2 = banks.acquire(te)
                te.matmul(b2[:], hbc[:, 128 * ch:128 * (ch + 1)],
                          sinb[0:16, ts], start=True, stop=True
                          ).then_inc(*lb.bump("mm"))
                hh["si"] = (i2, b2, lb.mark("mm"))
            lb.t(t_bc2)

            def v_cd(ve, tch=tch, ts=ts, ch=ch, par=par, hh=hh):
                if DBG_L1ATTN:
                    lb.wait(ve, "st", lb.mark("st"))
                if ch == 0 and tch > 0:
                    lb.wait(ve, "evs", sacs_mark[tch - 1])
                prev = (tch, ch - 2) if ch >= 2 else (tch - 1, KC - 2 + ch)
                if prev in dotm2:
                    lb.wait(ve, "mm", dotm2[prev])
                i1, b1, mk1 = hh["so"]
                lb.wait(ve, "mm", mk1)
                lb.ss(ve, "evv", ve.tensor_mul(ksoc[:], k[:, ch, ts], b1[:]))
                banks.set_guard(i1, "evv", lb.mark("evv"))
                lb.ss(ve, "evv", ve.tensor_tensor_scan(
                    KCch[:], ksoc[:], zeros[:], carry2[:, ch:ch + 1],
                    ALU.add, ALU.add))
                lb.ss(ve, "evv", ve.tensor_copy(carry2[:, ch:ch + 1],
                                                KCch[:, 511:512]))
                lb.ss(ve, "evv", ve.tensor_scalar_add(eps1[:], q[:, ch, ts],
                                                      1e-6))
                lb.ss(ve, "evv", ve.scalar_tensor_tensor(
                    prodA[:, par, :], KCch[:], 1e-6, eps1[:],
                    ALU.add, ALU.mult))
                i2, b2, mk2 = hh["si"]
                lb.wait(ve, "mm", mk2)
                lb.ss(ve, "evv", ve.tensor_mul(qsic[:], q[:, ch, ts], b2[:]))
                banks.set_guard(i2, "evv", lb.mark("evv"))
                lb.ss(ve, "evv", ve.tensor_tensor_scan(
                    QCch[:], qsic[:], zeros[:], carry2[:, 8 + ch:9 + ch],
                    ALU.add, ALU.add))
                lb.ss(ve, "evv", ve.tensor_copy(carry2[:, 8 + ch:9 + ch],
                                                QCch[:, 511:512]))
                lb.ss(ve, "evv", ve.tensor_scalar_add(eps2[:], k[:, ch, ts],
                                                      1e-6))
                ins = ve.scalar_tensor_tensor(
                    prodB[:, par, :], QCch[:], 1e-6, eps2[:],
                    ALU.add, ALU.mult)
                ins.then_inc(*lb.bump("evv"))
                hh["pr"] = lb.mark("evv")
            lb.v(v_cd)

            def t_d2(te, tch=tch, ch=ch, par=par, hh=hh):
                if ch == 0 and dot_guard[0] is not None:
                    lb.wait(te, "evv", dot_guard[0])
                lb.wait(te, "evv", hh["pr"])
                te.matmul(dotA_bank[0:16, :], hsum[:, ch, :],
                          prodA[:, par, :],
                          start=(ch == 0), stop=(ch == KC - 1),
                          skip_group_check=True)
                ins = te.matmul(dotB_bank[0:16, :], hsum[:, ch, :],
                                prodB[:, par, :],
                                start=(ch == 0), stop=(ch == KC - 1),
                                skip_group_check=True)
                ins.then_inc(*lb.bump("mm"))
                dotm2[(tch, ch)] = lb.mark("mm")
            lb.t(t_d2)

        def v_cd_fin(ve, tch=tch, ts=ts):
            if DBG_L1ATTN:
                lb.wait(ve, "st", lb.mark("st"))
            lb.wait(ve, "mm", dotm2[(tch, KC - 1)])
            lb.wait(ve, "evv", ZR)
            lb.ss(ve, "evv", ve.tensor_mul(eps1[0:16, 0:512],
                                           dotA_bank[0:16, :],
                                           rn16[0:16, ts]))
            lb.ss(ve, "evv", ve.tensor_mul(KCch[0:16, 0:512],
                                           dotB_bank[0:16, :],
                                           rn16[0:16, ts]))
            dot_guard[0] = lb.mark("evv")
            ve.engine_nop().then_inc(*lb.bump("evv"))
        lb.v(v_cd_fin)
        CDM = lb.mark("evv")

        def s_sa_cs(se, tch=tch, ts=ts, CDM=CDM):
            lb.wait(se, "evv", CDM)
            se.activation(sa[0:16, ts], eps1[0:16, 0:512], AF.Sigmoid
                          ).then_inc(*lb.bump("evs"))
            ins = se.activation(csb[0:16, ts], KCch[0:16, 0:512], AF.Exp)
            ins.then_inc(*lb.bump("evs"))
        lb.s(s_sa_cs)
        sacs_mark[tch] = lb.mark("evs")

    P4_done = ("evs", lb.mark("evs"))
    if DBG_L1ATTN:
        def y_dbg4(sy):
            lb.wait(sy, *P4_done)
            sy.dma_start(dbg_sa_h[:], sa[0:16, :]).then_inc(*lb.bump("st", 16))
            sy.dma_start(dbg_cs_h[:], csb[0:16, :]).then_inc(*lb.bump("st", 16))
        lb.y(y_dbg4)

    def v_cs(ve):
        lb.wait(ve, *P4_done)
        st0 = zeros[0:16, 0:1]
        for tch in range(NTCH):
            ts = slice(512 * tch, 512 * (tch + 1))
            init = st0 if tch == 0 else CSf[0:16, 512 * tch - 1:512 * tch]
            lb.ss(ve, "evv", ve.tensor_tensor_scan(
                CSf[0:16, ts], csb[0:16, ts], zeros[0:16, 0:512],
                init, ALU.add, ALU.add))
        ins = ve.tensor_copy(cs_sum[:, :], CSf[0:16, TC - 1:TC])
        ins.then_inc(*lb.bump("evv"))
    lb.v(v_cs)
    ag3 = emit_ag_prefix(lb, "s3", cs_sum, 16, 1, mpfx, cs_carry,
                         ("evv", lb.mark("evv")), agg3)

    def v_sc(ve):
        lb.wait(ve, *ag3)
        for tch in range(NTCH):
            ts = slice(512 * tch, 512 * (tch + 1))
            lb.ss(ve, "evv", ve.tensor_scalar_add(CSf[0:16, ts],
                                                  CSf[0:16, ts],
                                                  cs_carry[0:16, 0:1]))
            lb.ss(ve, "evv", ve.reciprocal(CSf[0:16, ts], CSf[0:16, ts]))
            lb.ss(ve, "evv", ve.tensor_mul(eps1[0:16, 0:512],
                                           csb[0:16, ts], CSf[0:16, ts]))
            lb.ss(ve, "evv", ve.tensor_mul(scb[0:16, ts], eps1[0:16, 0:512],
                                           nrm16[0:16, ts]))
        for tch in range(NTCH):
            ts = slice(512 * tch, 512 * (tch + 1))
            lb.ss(ve, "evv", ve.tensor_mul(scal[0:16, ts], si[0:16, ts],
                                           sa[0:16, ts]))
        ve.engine_nop().then_inc(*lb.bump("evv"))
    lb.v(v_sc)
    P5_done = ("evv", lb.mark("evv"))
    if DBG_L1ATTN:
        def y_dbg5(sy):
            lb.wait(sy, *P5_done)
            sy.dma_start(dbg_sc_h[:], scb[0:16, :]).then_inc(*lb.bump("st", 16))
            sy.dma_start(dbg_scal_h[:], scal[0:16, :]).then_inc(*lb.bump("st", 16))
        lb.y(y_dbg5)

    for tch in range(NTCH):
        ts = slice(512 * tch, 512 * (tch + 1))
        for ch in range(KC):
            hh = {}

            def t_bc3(te, tch=tch, ts=ts, ch=ch, hh=hh):
                if tch == 0 and ch == 0:
                    lb.wait(te, *P5_done)
                i1, b1 = banks.acquire(te)
                te.matmul(b1[:], hbc[:, 128 * ch:128 * (ch + 1)],
                          scal[0:16, ts], start=True, stop=True
                          ).then_inc(*lb.bump("mm"))
                hh["q"] = (i1, b1, lb.mark("mm"))
                i2, b2 = banks.acquire(te)
                te.matmul(b2[:], hbc[:, 128 * ch:128 * (ch + 1)],
                          scb[0:16, ts], start=True, stop=True
                          ).then_inc(*lb.bump("mm"))
                hh["v"] = (i2, b2, lb.mark("mm"))
            lb.t(t_bc3)

            def v_scale(ve, tch=tch, ts=ts, ch=ch, hh=hh):
                i1, b1, mk1 = hh["q"]
                lb.wait(ve, "mm", mk1)
                lb.ss(ve, "evv", ve.tensor_mul(q[:, ch, ts], q[:, ch, ts],
                                               b1[:]))
                banks.set_guard(i1, "evv", lb.mark("evv"))
                i2, b2, mk2 = hh["v"]
                lb.wait(ve, "mm", mk2)
                ins = ve.tensor_mul(v[:, ch, ts], v[:, ch, ts], b2[:])
                ins.then_inc(*lb.bump("evv"))
                banks.set_guard(i2, "evv", lb.mark("evv"))
            lb.v(v_scale)
    P6_done = ("evv", lb.mark("evv"))
    if DBG_L1ATTN:
        def y_dbg6(sy):
            lb.wait(sy, *P6_done)
            sy.dma_start(dbg_q_h[:], q[:]).then_inc(*lb.bump("st", 16))
        lb.y(y_dbg6)

    for nm in ("qsic", "ksoc", "prodB", "prodA",
               "eps1", "KCch", "sinb", "son", "scal", "scb", "csb", "sa",
               "si", "CSf"):
        lb.free_sb(nm)
    ktmp = lb.sb_scoped("ktmp", [128, KC, 128], BF16)
    vtmp = lb.sb_scoped("vtmp", [128, KC, 128], BF16)
    Ss = lb.sb_scoped("Ss", [128, 2, 128], BF16)
    agg4 = lb.sb_scoped("agg4", [128, 8, 128], F32)

    def v_kvinit(ve):
        lb.wait(ve, *P6_done)
        lb.ss(ve, "evv", ve.memset(kv_loc[:], 0.0))
        ins = ve.memset(kv_bf[:], 0.0)
        ins.then_inc(*lb.bump("evv"))
    lb.v(v_kvinit)
    KVI = lb.mark("evv")


    for tc in range(NTC):
        tsl = slice(128 * tc, 128 * (tc + 1))
        hh = {}
        for ch in range(KC):
            def t_tp(te, tc=tc, ch=ch, hh=hh, tsl=tsl):
                if tc == 0 and ch == 0:
                    lb.wait(te, "evv", KVI)
                i1, b1 = tbanks.acquire(te)
                te.transpose(b1[:, 0:128], k[:, ch, tsl], idb[:]
                             ).then_inc(*lb.bump("mm"))
                hh[("tk", ch)] = (i1, b1, lb.mark("mm"))
                i2, b2 = tbanks.acquire(te)
                te.transpose(b2[:, 0:128], v[:, ch, tsl], idb[:]
                             ).then_inc(*lb.bump("mm"))
                hh[("tv", ch)] = (i2, b2, lb.mark("mm"))
            lb.t(t_tp)

            def v_tp(ve, ch=ch, hh=hh):
                i1, b1, mk1 = hh[("tk", ch)]
                lb.wait(ve, "mm", mk1)
                ins = ve.tensor_copy(ktmp[:, ch, :], b1[:, 0:128])
                ins.then_inc(*lb.bump("evv"))
                tbanks.set_guard(i1, "evv", lb.mark("evv"))
                i2, b2, mk2 = hh[("tv", ch)]
                lb.wait(ve, "mm", mk2)
                ins = ve.tensor_copy(vtmp[:, ch, :], b2[:, 0:128])
                ins.then_inc(*lb.bump("evv"))
                tbanks.set_guard(i2, "evv", lb.mark("evv"))
            lb.v(v_tp)
        tp_mark = lb.mark("evv")

        for ch in range(KC):
            for hp in range(2):
                r0 = 64 * hp
                eh = {}

                def t_S(te, tc=tc, ch=ch, hp=hp, r0=r0, eh=eh, tsl=tsl,
                        tp_mark=tp_mark):
                    iS, bS = banks.acquire(te)
                    te.matmul(bS[:, 0:128], k[r0:r0 + 64, ch, tsl],
                              q[r0:r0 + 64, ch, tsl], start=True, stop=True
                              ).then_inc(*lb.bump("mm"))
                    eh["S"] = (iS, bS, lb.mark("mm"))
                lb.t(t_S)

                def v_mask(ve, ch=ch, hp=hp, eh=eh):
                    iS, bS, mk = eh["S"]
                    lb.wait(ve, "mm", mk)
                    ins = ve.tensor_mul(Ss[:, hp, :], bS[:, 0:128], tri[:, :])
                    ins.then_inc(*lb.bump("evv"))
                    banks.set_guard(iS, "evv", lb.mark("evv"))
                    eh["Ss"] = lb.mark("evv")
                lb.v(v_mask)

                def t_O(te, tc=tc, ch=ch, hp=hp, r0=r0, eh=eh, tsl=tsl,
                        tp_mark=tp_mark):
                    lb.wait(te, "evv", eh["Ss"])
                    lb.wait(te, "evv", tp_mark)
                    iO, bO = banks.acquire(te)
                    te.matmul(bO[0:64, 0:128], vtmp[:, ch, r0:r0 + 64],
                              Ss[:, hp, :], start=True, stop=False,
                              skip_group_check=True)
                    te.matmul(bO[0:64, 0:128], kv_bf[r0:r0 + 64, ch, :],
                              q[r0:r0 + 64, ch, tsl], start=False, stop=True,
                              skip_group_check=True).then_inc(*lb.bump("mm"))
                    eh["O"] = (iO, bO, lb.mark("mm"))
                    iC, bC = banks.acquire(te)
                    te.matmul(bC[0:64, 0:64], ktmp[:, ch, r0:r0 + 64],
                              vtmp[:, ch, r0:r0 + 64], start=True, stop=True
                              ).then_inc(*lb.bump("mm"))
                    eh["C"] = (iC, bC, lb.mark("mm"))
                lb.t(t_O)

                def v_O(ve, tc=tc, ch=ch, hp=hp, r0=r0, eh=eh, tsl=tsl):
                    iO, bO, mkO = eh["O"]
                    lb.wait(ve, "mm", mkO)
                    ins = ve.tensor_copy(ofm[r0:r0 + 64, ch, tsl],
                                         bO[0:64, 0:128])
                    ins.then_inc(*lb.bump("evv"))
                    banks.set_guard(iO, "evv", lb.mark("evv"))
                    iC, bC, mkC = eh["C"]
                    lb.wait(ve, "mm", mkC)
                    lb.ss(ve, "evv", ve.tensor_add(kv_loc[r0:r0 + 64, ch, :],
                                                   kv_loc[r0:r0 + 64, ch, :],
                                                   bC[0:64, 0:64]))
                    banks.set_guard(iC, "evv", lb.mark("evv"))
                    ins = ve.tensor_copy(kv_bf[r0:r0 + 64, ch, :],
                                         kv_loc[r0:r0 + 64, ch, :])
                    ins.then_inc(*lb.bump("evv"))
                lb.v(v_O)
    P7_done = ("evv", lb.mark("evv"))
    if DBG_L1ATTN:
        def y_dbg7(sy):
            lb.wait(sy, *P7_done)
            sy.dma_start(dbg_kv_h[:], kv_loc[:]).then_inc(*lb.bump("st", 16))
        lb.y(y_dbg7)

    ag4 = emit_ag_prefix(lb, "s4", kv_loc[:].rearrange("p c x -> p (c x)"),
                         128, 512, mpfx,
                         kv_carry[:].rearrange("p c x -> p (c x)"), P7_done,
                         agg4)

    def v_kvc(ve):
        lb.wait(ve, *ag4)
        ins = ve.tensor_copy(kv_bf[:], kv_carry[:])
        ins.then_inc(*lb.bump("evv"))
    lb.v(v_kvc)
    KVC = lb.mark("evv")

    for tc in range(NTC):
        tsl = slice(128 * tc, 128 * (tc + 1))
        for ch in range(KC):
            for hp in range(2):
                r0 = 64 * hp
                eh = {}

                def t_oc(te, tc=tc, ch=ch, hp=hp, r0=r0, eh=eh, tsl=tsl):
                    if tc == 0 and ch == 0 and hp == 0:
                        lb.wait(te, "evv", KVC)
                    iO, bO = banks.acquire(te)
                    te.matmul(bO[0:64, 0:128], kv_bf[r0:r0 + 64, ch, :],
                              q[r0:r0 + 64, ch, tsl], start=True, stop=True
                              ).then_inc(*lb.bump("mm"))
                    eh["O"] = (iO, bO, lb.mark("mm"))
                lb.t(t_oc)

                def v_oc(ve, tc=tc, ch=ch, hp=hp, r0=r0, eh=eh, tsl=tsl):
                    iO, bO, mk = eh["O"]
                    lb.wait(ve, "mm", mk)
                    ins = ve.tensor_add(ofm[r0:r0 + 64, ch, tsl],
                                        ofm[r0:r0 + 64, ch, tsl],
                                        bO[0:64, 0:128])
                    ins.then_inc(*lb.bump("evv"))
                    banks.set_guard(iO, "evv", lb.mark("evv"))
                lb.v(v_oc)
    P8_done = ("evv", lb.mark("evv"))
    if DBG_L1ATTN:
        def y_dbg8(sy):
            lb.wait(sy, *P8_done)
            sy.dma_start(dbg_ofm_h[:], ofm[:]).then_inc(*lb.bump("st", 16))
        lb.y(y_dbg8)

    for nm in ("agg4", "Ss", "vtmp", "ktmp", "v", "k"):
        lb.free_sb(nm)
    NLN = 256
    wo_sb = lb.sb_scoped("wo_sb", [128, KC, 1024], BF16)
    resid = lb.sb_scoped("resid", [128, KC, NLN], F32)
    ytile = lb.sb_scoped("ytile", [128, KC, NLN], F32)
    yout = lb.sb_scoped("yout", [128, KC, NLN], F32)
    zsq = lb.sb_scoped("zsq", [128, KC, NLN], F32)
    statrow = lb.sb_scoped("statrow", [1, 3 * NLN], F32)
    aux = {"onescol": onescol, "onesrow": onesrow, "zsq": zsq,
           "statrow": statrow}

    def g_wo(g):
        g.dma_start(wo_sb[:], wo_h.rearrange("(c p) n -> p c n", p=128)
                    ).then_inc(*lb.bump("xld", 16))
    lb.g(g_wo)
    WOLD = lb.mark("xld")

    for sti in range(TC // NLN):
        ts = slice(NLN * sti, NLN * (sti + 1))

        def y_resid(sy, sti=sti):
            lb.wait(sy, "evv", lb.mark("evv"))
            if io["stream_dep"] is not None:
                lb.wait(sy, *io["stream_dep"])
            sy.dma_start(resid[:], rearr_pct(xin_h, NLN * sti, NLN)
                         ).then_inc(*lb.bump("rld", 16))
        lb.y(y_resid)
        rld_mark = lb.mark("rld")

        for j in range(KC):
            eh = {}

            def t_op(te, sti=sti, j=j, eh=eh, ts=ts):
                if sti == 0 and j == 0:
                    lb.wait(te, *P8_done)
                    lb.wait(te, "xld", WOLD)
                bi, bt = banks.acquire(te)
                for c in range(KC):
                    ins = te.matmul(bt[:, 0:NLN],
                                    wo_sb[:, c, 128 * j:128 * (j + 1)],
                                    ofm[:, c, ts],
                                    start=(c == 0), stop=(c == KC - 1))
                ins.then_inc(*lb.bump("mm"))
                eh["m"] = lb.mark("mm")
                eh["b"] = (bi, bt)
            lb.t(t_op)

            def v_y(ve, sti=sti, j=j, eh=eh, rld_mark=rld_mark):
                if j == 0:
                    lb.wait(ve, "rld", rld_mark)
                lb.wait(ve, "mm", eh["m"])
                bi, bt = eh["b"]
                lb.ss(ve, "evv", ve.tensor_add(ytile[:, j, :], bt[:, 0:NLN],
                                               resid[:, j, :]))
                banks.set_guard(bi, "evv", lb.mark("evv"))
                ins = ve.tensor_scalar_add(ytile[:, j, :], ytile[:, j, :],
                                           bos[:, j:j + 1])
                ins.then_inc(*lb.bump("evv"))
            lb.v(v_y)
        z_dep = ("evv", lb.mark("evv"))

        ln_dep = emit_ln(lb, banks, aux, ytile, yout, z_dep, nt=NLN,
                         out_extra_deps=[("st", lb.mark("st"))])

        def y_store(sy, sti=sti, ln_dep=ln_dep):
            lb.wait(sy, *ln_dep)
            sy.dma_start(rearr_pct(xout_h, NLN * sti, NLN), yout[:]
                         ).then_inc(*lb.bump("st", 16))
        lb.y(y_store)

    return (lb.qual("st"), lb.mark("st"))


def host_l1attn_inputs(inputs, xT_slices):
    def pc(b):
        return np.ascontiguousarray(b.reshape(8, 128).T).astype(np.float32)
    wq = np.ascontiguousarray(inputs["L1_attn_Wq"]).astype(np.float32)
    wkk = np.ascontiguousarray(inputs["L1_attn_Wk"]).astype(np.float32)
    wv = np.ascontiguousarray(inputs["L1_attn_Wv"]).astype(np.float32)
    wo = np.ascontiguousarray(inputs["L1_attn_Wo"]).astype(np.float32)
    import ml_dtypes as _md
    hsum = np.zeros((128, 8, 16), np.float32)
    for ch in range(8):
        for p in range(128):
            hsum[p, ch, (128 * ch + p) // 64] = 1.0
    hsum = hsum.astype(_md.bfloat16)
    import ml_dtypes
    hbc = np.zeros((16, 1024), np.float32)
    for ch in range(8):
        for m in range(128):
            hbc[2 * ch + (m >= 64), 128 * ch + m] = 1.0
    hbc = hbc.astype(ml_dtypes.bfloat16)
    tri = np.tril(np.ones((128, 128), np.float32)).T.copy().astype(
        ml_dtypes.bfloat16)
    idb = np.eye(128, dtype=np.float32).astype(ml_dtypes.bfloat16)
    common = dict(Wq=wq, Wk=wkk, Wv=wv, Wo=wo,
                  bqpc=pc(inputs["L1_attn_bq"]), bkpc=pc(inputs["L1_attn_bk"]),
                  bvpc=pc(inputs["L1_attn_bv"]), bopc=pc(inputs["L1_attn_bo"]),
                  hsum=hsum, hbcast=hbc, trimask=tri, id128b=idb,
                  onescol=np.ones((128, 1), np.float32),
                  onesrow=np.ones((1, 128), np.float32))
    maps = []
    for c in range(NCORES):
        nrm = (np.arange(TC * c, TC * (c + 1), dtype=np.float32) + 1.0)
        nrm16 = np.ascontiguousarray(np.broadcast_to(nrm[None, :], (16, TC))
                                     ).astype(np.float32)
        mp = np.zeros((128, 8), np.float32)
        mp[:, :c] = 1.0
        m = dict(common, nrm16=nrm16, maskpfx=mp)
        if xT_slices is not None:
            m["xinT"] = np.ascontiguousarray(xT_slices[c])
        maps.append(m)
    return maps




def emit_ag_full(lb, tag, src_ap, P, X, gath_sb, deps):
    ag_in = lb.dram(f"agf_{tag}_in", [P, X])
    ag_out = lb.dram(f"agf_{tag}_out", [8 * P, X])

    def g_ag(g):
        lb.wait(g, *deps)
        g.dma_start(ag_in[:], src_ap).then_inc(*lb.bump("agd", 16))
        lb.wait(g, "agd", lb.mark("agd"))
        g.collective_compute(
            "AllGather", ALU.bypass, RG,
            ins=[ag_in.opt()], outs=[ag_out.opt()]).then_inc(*lb.bump("cc"))
        lb.wait(g, "cc", lb.mark("cc"))
        g.dma_start(gath_sb[:, :, 0:X],
                    ag_out.rearrange("(r p) x -> p r x", p=P),
                    allow_slow_non_contiguous=X <= 4
                    ).then_inc(*lb.bump("agd", 16))
    lb.g(g_ag)
    return ("agd", lb.mark("agd"))


def build_l0attn():
    lb = LB()
    xq_h = lb.inp("xqT", [D, TC + 1])
    ocol_h = lb.inp("onescol", [128, 1])
    orow_h = lb.inp("onesrow", [1, 128])
    xout_h = lb.out("xoutT", [D, TC])
    lb.start_block(["shared"], allow_lp=True)
    onescol = lb.sb("sh_onescol", [128, 1], F32)
    onesrow = lb.sb("sh_onesrow", [1, 128], F32)

    def y_shc(sy):
        sy.dma_start(onescol[:, :], ocol_h[:]).then_inc(*lb.bump("@shared", 16))
        sy.dma_start(onesrow[:, :], orow_h[:]).then_inc(*lb.bump("@shared", 16))
    lb.y(y_shc)
    io = {"xq": xq_h, "xout": xout_h, "onescol": onescol,
          "onesrow": onesrow, "shc_dep": ("@shared", lb.mark("@shared"))}
    dep = emit_l0attn(lb, io)

    def y_final(sy):
        lb.wait(sy, *dep)
    lb.y(y_final)
    return lb.done()


def emit_l0attn(lb, io):
    xq_h = io["xq"]
    xout_h = io["xout"]
    onescol = io["onescol"]
    onesrow = io["onesrow"]
    wqT_h = lb.inp("WqT", [D, D])
    wvT_h = lb.inp("WvT", [D, D])
    wcT_h = lb.inp("WcT", [D, D])
    wp1_h = lb.inp("Wp1", [D, D])
    wp2_h = lb.inp("Wp2", [D, D])
    idn_h = lb.inp("identcol", [128, KC])
    fm_h = lb.inp("fmask", [128, 2])
    ohp_h = lb.inp("onehotprev", [128, 8])
    wsel_h = lb.inp("wsel", [128, 3, 16])
    if DBG_L0ATTN:
        dbg_p1_h = lb.out("dbg_p1", [128, KC, D], BF16)
        dbg_q_h = lb.out("dbg_q", [128, KC, TC + 1], BF16)
        dbg_h_h = lb.out("dbg_h", [128, KC, TC + 2], BF16)

    p1t = lb.sb("p1t", [128, KC, D], BF16)
    p2t = lb.sb("p2t", [128, KC, D], BF16)
    idn = lb.sb("idn_s", [128, KC], F32)
    fmk = lb.sb("fmk_s", [128, 2], F32)
    ohp = lb.sb("ohp_s", [128, 8], F32)
    wsel = lb.sb("wsel_s", [128, 3, 16], F32)
    wk_t = [lb.sb(f"wkt{i}", [128, KC, 128], BF16) for i in range(3)]
    rowx = lb.sb("rowx", [1, 2 * 512], F32)
    agrow = lb.sb("agrow", [128, 8, 16], F32)
    statrow = lb.sb("statrow", [1, 3 * 512], F32)

    q = lb.sb_scoped("q", [128, KC, TC + 1], BF16)

    pb = [lb.ps(f"pb{i}", [128, 512]) for i in range(8)]

    banks = Banks(lb, pb[0:6])
    statbank, bcbank = pb[6], pb[7]
    statguard = [None]
    bcguard = [None]
    wbuf = WBuf(lb, wk_t, "wkt")

    def y_const(sy):
        lb.wait(sy, *io["shc_dep"])
        for dst, srcc in [(idn[:, :], idn_h[:]), (fmk[:, :], fm_h[:]),
                          (ohp[:, :], ohp_h[:]), (wsel[:], wsel_h[:])]:
            sy.dma_start(dst, srcc).then_inc(*lb.bump("cld", 16))
    lb.y(y_const)
    CLD = lb.mark("cld")

    xbf = lb.sb_scoped("xbf", [128, KC, TC + 1], BF16)

    def g_x(g):
        g.dma_start(xbf[:], rearr_pct(xq_h, 0, TC + 1)
                    ).then_inc(*lb.bump("xld", 16))
    lb.g(g_x)

    plan = [(rearr_pcn(wqT_h, 128 * j, 128), KC) for j in range(KC)]
    pumped = [0]

    def pump(n):
        n = min(n, len(plan) - pumped[0])
        if n <= 0:
            return

        def g_loads(g):
            for _ in range(n):
                ap, kch = plan[pumped[0]]
                wbuf.emit_load(g, ap, kch)
                pumped[0] += 1
        lb.g(g_loads)
    pump(3)

    NQ = TC + 1
    for j in range(KC):
        hh = {}

        def t_q(te, j=j, hh=hh):
            if j == 0:
                lb.wait(te, "cld", CLD)
                lb.wait(te, "xld", 16)
            w = wbuf.consume(te, j)
            for f in range(5):
                f0 = 512 * f
                f1 = min(NQ, f0 + 512)
                bi, bt = banks.acquire(te)
                for c in range(KC):
                    ins = te.matmul(bt[:, 0:f1 - f0], w[:, c, :],
                                    xbf[:, c, f0:f1],
                                    start=(c == 0), stop=(c == KC - 1))
                ins.then_inc(*lb.bump("mm"))
                hh[f] = (bi, bt, lb.mark("mm"), f0, f1)
            wbuf.release_at(j, "mm", lb.mark("mm"))
        lb.t(t_q)

        def v_q(ve, j=j, hh=hh):
            for f in range(5):
                bi, bt, mk, f0, f1 = hh[f]
                lb.wait(ve, "mm", mk)
                ins = ve.tensor_copy(q[:, j, f0:f1], bt[:, 0:f1 - f0])
                ins.then_inc(*lb.bump("evv"))
                banks.set_guard(bi, "evv", lb.mark("evv"))
        lb.v(v_q)
        pump(1)

    def v_s0(ve):
        lb.wait(ve, "evv", lb.mark("evv"))
        lb.ss(ve, "evv", ve.tensor_scalar_mul(q[:, :, 0:1].rearrange(
            "p c x -> p (c x)"), q[:, :, 0:1].rearrange("p c x -> p (c x)"),
            fmk[:, 1:2]))
        ins = ve.scalar_tensor_tensor(
            q[:, :, 0:1].rearrange("p c x -> p (c x)"), idn[:, :],
            fmk[:, 0:1], q[:, :, 0:1].rearrange("p c x -> p (c x)"),
            ALU.mult, ALU.add)
        ins.then_inc(*lb.bump("evv"))
    lb.v(v_s0)
    Q_done = ("evv", lb.mark("evv"))
    lb.free_sb("xbf")

    KCD = KC * D
    PAY = KCD + 8
    NS_TILES = [t for t in range(16) if not (t % 2 == 0 and t // 2 >= 4)]
    NPAY = 512 * len(NS_TILES) + 8

    Gb = [lb.sb_scoped(f"nsG{m}", [128, PAY], F32R) for m in range(2)]
    stg = [lb.sb_scoped(f"nsStg{m}", [128, 2048], F32) for m in range(2)]
    nXr = [lb.sb_scoped(f"nsX{m}", [128, D], F32) for m in range(2)]
    nXb = [lb.sb_scoped(f"nsXb{m}", [128, D], F32R) for m in range(2)]
    nYr = [lb.sb_scoped(f"nsY{m}", [128, D], F32R) for m in range(2)]
    ntrT = [lb.sb_scoped(f"nsT{m}", [128, KC, 128], F32R) for m in range(2)]
    trTf = lb.sb_scoped("nsTf", [128, KC, 128], F32)
    nscl = [lb.sb_scoped(f"nsScl{m}", [128, 4], F32) for m in range(2)]
    nsqr = [lb.sb_scoped(f"nsSq{m}", [128, 8], F32) for m in range(2)]
    srow = lb.sb_scoped("srow", [1, 8], F32)
    idf = lb.sb_scoped("idf", [128, 128], F32)
    idfr = lb.sb_scoped("idfr", [128, 128], F32R)

    wr1_h = lb.inp("Wp1r", [128, D])
    wr2_h = lb.inp("Wp2r", [128, D])
    idf_h = lb.inp("ident128f", [128, 128])
    wr_h = [wr1_h, wr2_h]
    ar_in = [lb.dram(f"ns_ari{m}", [128, NPAY]) for m in range(2)]
    ar_out = [lb.dram(f"ns_aro{m}", [128, NPAY]) for m in range(2)]
    nag_in = [lb.dram(f"ns_agi{m}", [128, D]) for m in range(2)]
    nag_out = [lb.dram(f"ns_ago{m}", [8 * 128, D]) for m in range(2)]

    def y_nsld(sy):
        sy.dma_start(idf[:, :], idf_h[:]).then_inc(*lb.bump("cld", 16))
        for m in range(2):
            sy.dma_start(nXr[m][:, :], wr_h[m][:]).then_inc(*lb.bump("cld", 16))
    lb.y(y_nsld)

    def g_nsld(g):
        g.dma_start(idfr[:, :], idf_h[:]).then_inc(*lb.bump("cld", 16))
        for m in range(2):
            g.dma_start(nXb[m][:, :], wr_h[m][:]).then_inc(*lb.bump("cld", 16))
    lb.g(g_nsld)
    NSLD = lb.mark("cld")

    nst = [{"xdep": ("cld", NSLD), "cc_mk": None, "ari_busy": None,
            "aro_busy": None, "trT_busy": None, "stg_mk": [None, None],
            "scl_mk": None, "yr_free": None}
           for _ in range(2)]

    def emit_ns_h1(m, it):
        s = nst[m]
        if it == 0:
            def s_sq(se, s=s, m=m):
                lb.wait(se, *s["xdep"])
                for h2 in range(2):
                    sl = slice(512 * h2, 512 * (h2 + 1))
                    ins = se.activation(nYr[m][:, sl], nXr[m][:, sl],
                                        AF.Square)
                ins.then_inc(*lb.bump("sq"))
            lb.s(s_sq)
            SQ = lb.mark("sq")

            def v_sq(ve, m=m, SQ=SQ):
                lb.wait(ve, "sq", SQ)
                lb.ss(ve, "evv", ve.memset(nsqr[m][:, :], 0.0))
                ins = ve.reduce_sum(nsqr[m][:, 0:1], nYr[m][:, :], axis=AX.X)
                ins.then_inc(*lb.bump("evv"))
            lb.v(v_sq)
            SQV = lb.mark("evv")

            def y_sq(sy, m=m, SQV=SQV):
                lb.wait(sy, "evv", SQV)
                sy.dma_start(ar_in[m][:, NPAY - 8:NPAY], nsqr[m][:, :]
                             ).then_inc(*lb.bump(f"nstg{m}a", 16))
            lb.y(y_sq)

        for grp in range(6):
            bigslot = grp % 2
            hh = {}
            for k in range(2):
                t_i = NS_TILES[grp * 2 + k]
                cc, h2 = divmod(t_i, 2)
                first = (grp == 0 and k == 0)

                def t_cr(te, s=s, m=m, cc=cc, h2=h2, hh=hh, k=k,
                         first=first):
                    if first:
                        lb.wait(te, *s["xdep"])
                    bi, bt = banks.acquire(te)
                    te.matmul(bt[:], nXb[m][:, 128 * cc:128 * (cc + 1)],
                              nXb[m][:, 512 * h2:512 * (h2 + 1)],
                              start=True, stop=True).then_inc(*lb.bump("mm"))
                    hh[k] = (bi, bt, lb.mark("mm"))
                lb.t(t_cr)

                def v_cr(ve, s=s, m=m, bigslot=bigslot, k=k, hh=hh):
                    bi, bt, mk = hh[k]
                    lb.wait(ve, "mm", mk)
                    if k == 0 and s["stg_mk"][bigslot] is not None:
                        lb.wait(ve, f"nstg{m}" + "ab"[bigslot],
                                s["stg_mk"][bigslot])
                    ins = ve.tensor_copy(
                        stg[m][:, 1024 * bigslot + 512 * k:
                               1024 * bigslot + 512 * (k + 1)], bt[:])
                    ins.then_inc(*lb.bump("evv"))
                    banks.set_guard(bi, "evv", lb.mark("evv"))
                lb.v(v_cr)
            EVG = lb.mark("evv")

            def body_st(eng, s=s, m=m, grp=grp, bigslot=bigslot, EVG=EVG):
                lb.wait(eng, "evv", EVG)
                if grp in (0, 1) and s["ari_busy"] is not None:
                    lb.wait(eng, f"nscc{m}", s["ari_busy"])
                eng.dma_start(ar_in[m][:, 1024 * grp:1024 * (grp + 1)],
                              stg[m][:, 1024 * bigslot:1024 * (bigslot + 1)]
                              ).then_inc(*lb.bump(f"nstg{m}" + "ab"[bigslot],
                                                  16))
            if grp % 2 == 0:
                lb.y(body_st)
            else:
                lb.g(body_st)
            s["stg_mk"][bigslot] = lb.mark(f"nstg{m}" + "ab"[bigslot])
        ALLST = (lb.mark(f"nstg{m}a"), lb.mark(f"nstg{m}b"))

        def g_ar(g, s=s, m=m, it=it, ALLST=ALLST):
            lb.wait(g, f"nstg{m}a", ALLST[0])
            lb.wait(g, f"nstg{m}b", ALLST[1])
            if s["aro_busy"] is not None:
                lb.wait(g, f"nsgl{m}", s["aro_busy"])
            g.collective_compute(
                "AllReduce", ALU.add, RG,
                ins=[ar_in[m].opt()], outs=[ar_out[m].opt()]
            ).then_inc(*lb.bump(f"nscc{m}"))
        lb.g(g_ar)
        s["cc_mk"] = lb.mark(f"nscc{m}")
        s["ari_busy"] = s["cc_mk"]

    def emit_ns_h2(m, it, coef):
        a_c, b_c, c_c = coef
        s = nst[m]
        quintic = (c_c != 0.0)

        def g_gl(g, s=s, m=m):
            lb.wait(g, f"nscc{m}", s["cc_mk"])
            g.dma_start(Gb[m][:, 0:4096], ar_out[m][:, 0:4096]
                        ).then_inc(*lb.bump(f"nsgl{m}", 16))
            g.dma_start(Gb[m][:, 4096:8192]
                        .rearrange("p (c n) -> p c n", n=1024)[:, :, 512:1024],
                        ar_out[m][:, 4096:6144]
                        .rearrange("p (c n) -> p c n", n=512)
                        ).then_inc(*lb.bump(f"nsgl{m}", 16))
            g.dma_start(Gb[m][:, KCD:PAY], ar_out[m][:, NPAY - 8:NPAY]
                        ).then_inc(*lb.bump(f"nsgl{m}", 16))
        lb.g(g_gl)
        GMK = lb.mark(f"nsgl{m}")
        s["aro_busy"] = GMK

        hhr = {}
        for idx, (cc, bl) in enumerate((cc, bl) for cc in range(4, 8)
                                       for bl in range(4)):
            def t_rc(te, cc=cc, bl=bl, idx=idx, hhr=hhr, m=m, GMK=GMK):
                if idx == 0:
                    lb.wait(te, f"nsgl{m}", GMK)
                bi, bt = banks.acquire(te)
                te.matmul(bt[:, 0:128],
                          Gb[m][:, 1024 * bl + 128 * cc:
                                1024 * bl + 128 * cc + 128],
                          idfr[:, :], start=True, stop=True
                          ).then_inc(*lb.bump("mm"))
                hhr[idx] = (bi, bt, lb.mark("mm"))
            lb.t(t_rc)

            def v_rc(ve, idx=idx, cc=cc, bl=bl, hhr=hhr, m=m):
                bi, bt, mk = hhr[idx]
                lb.wait(ve, "mm", mk)
                ins = ve.tensor_copy(
                    Gb[m][:, 1024 * cc + 128 * bl:1024 * cc + 128 * bl + 128],
                    bt[:, 0:128])
                ins.then_inc(*lb.bump("evv"))
                banks.set_guard(bi, "evv", lb.mark("evv"))
            lb.v(v_rc)
        G_READY = lb.mark("evv")

        if it == 0:
            def v_tl(ve, m=m, GMK=GMK):
                lb.wait(ve, f"nsgl{m}", GMK)
                ins = ve.tensor_copy(nsqr[m][:, :], Gb[m][:, KCD:PAY])
                ins.then_inc(*lb.bump("evv"))
            lb.v(v_tl)
            VTL = lb.mark("evv")

            def t_s2(te, s=s, m=m, VTL=VTL):
                if statguard[0] is not None:
                    lb.wait(te, "evv", statguard[0])
                lb.wait(te, "evv", VTL)
                te.matmul(statbank[0:1, 0:8], onescol[:, :],
                          nsqr[m][:, :],
                          start=True, stop=True).then_inc(*lb.bump("mm"))
            lb.t(t_s2)
            TS2 = lb.mark("mm")

            def v_s2(ve, TS2=TS2):
                lb.wait(ve, "mm", TS2)
                lb.ss(ve, "evv", ve.tensor_copy(srow[0:1, 0:1],
                                                statbank[0:1, 0:1]))
                statguard[0] = lb.mark("evv")
                ins = ve.reciprocal(srow[0:1, 1:2], srow[0:1, 0:1])
                ins.then_inc(*lb.bump("evv"))
            lb.v(v_s2)
            VS2 = lb.mark("evv")

            def s_rt(se, VS2=VS2):
                lb.wait(se, "evv", VS2)
                ins = se.activation(srow[0:1, 2:3], srow[0:1, 0:1], AF.Sqrt)
                ins.then_inc(*lb.bump("sq"))
            lb.s(s_rt)
            SRT = lb.mark("sq")

            def v_s3(ve, SRT=SRT, a_c=a_c, b_c=b_c, c_c=c_c):
                lb.wait(ve, "sq", SRT)
                lb.ss(ve, "evv", ve.reciprocal(srow[0:1, 3:4], srow[0:1, 2:3]))
                lb.ss(ve, "evv", ve.tensor_mul(srow[0:1, 4:5], srow[0:1, 3:4],
                                               srow[0:1, 1:2]))
                lb.ss(ve, "evv", ve.tensor_mul(srow[0:1, 5:6], srow[0:1, 4:5],
                                               srow[0:1, 1:2]))
                lb.ss(ve, "evv", ve.tensor_scalar_mul(srow[0:1, 3:4],
                                                      srow[0:1, 3:4], a_c))
                lb.ss(ve, "evv", ve.tensor_scalar_mul(srow[0:1, 4:5],
                                                      srow[0:1, 4:5], b_c))
                ins = ve.tensor_scalar_mul(srow[0:1, 5:6], srow[0:1, 5:6], c_c)
                ins.then_inc(*lb.bump("evv"))
            lb.v(v_s3)
            VS3 = lb.mark("evv")

            def t_bc(te, VS3=VS3):
                if bcguard[0] is not None:
                    lb.wait(te, "evv", bcguard[0])
                lb.wait(te, "evv", VS3)
                te.matmul(bcbank[:, 0:3], onesrow[:, :], srow[0:1, 3:6],
                          start=True, stop=True).then_inc(*lb.bump("mm"))
            lb.t(t_bc)
            TBC = lb.mark("mm")

            def v_bc(ve, m=m, TBC=TBC):
                lb.wait(ve, "mm", TBC)
                ins = ve.tensor_copy(nscl[m][:, 0:3], bcbank[:, 0:3])
                ins.then_inc(*lb.bump("evv"))
                bcguard[0] = lb.mark("evv")
            lb.v(v_bc)
            s["scl_mk"] = lb.mark("evv")

        def emit_tr(src, src_dep=None, s=s, m=m):
            for cc in range(KC):
                hh = {}

                def t_tp(te, cc=cc, hh=hh, src=src, s=s, src_dep=src_dep):
                    if cc == 0 and src_dep is not None:
                        lb.wait(te, *src_dep)
                    if cc == 0 and s["trT_busy"] is not None:
                        lb.wait(te, "mm", s["trT_busy"])
                    bi, bt = banks.acquire(te)
                    te.matmul(bt[:, 0:128],
                              src[:, 128 * cc:128 * (cc + 1)],
                              idfr[:, :], start=True, stop=True
                              ).then_inc(*lb.bump("mm"))
                    hh["m"] = (bi, bt, lb.mark("mm"))
                lb.t(t_tp)

                def v_tp(ve, cc=cc, hh=hh, m=m):
                    bi, bt, mk = hh["m"]
                    lb.wait(ve, "mm", mk)
                    ins = ve.tensor_copy(ntrT[m][:, cc, :], bt[:, 0:128])
                    ins.then_inc(*lb.bump("evv"))
                    banks.set_guard(bi, "evv", lb.mark("evv"))
                lb.v(v_tp)
            return ("evv", lb.mark("evv"))

        def emit_prod(dep, need_g):
            out = {}
            for h2 in range(2):
                hh = {}

                def t_m(te, h2=h2, hh=hh, dep=dep, need_g=need_g, m=m):
                    if h2 == 0:
                        lb.wait(te, *dep)
                        if need_g:
                            lb.wait(te, "evv", G_READY)
                    bi, bt = banks.acquire(te)
                    for cc in range(KC):
                        ins = te.matmul(
                            bt[:], ntrT[m][:, cc, :],
                            Gb[m][:, 1024 * cc + 512 * h2:
                                  1024 * cc + 512 * (h2 + 1)],
                            start=(cc == 0), stop=(cc == KC - 1))
                    ins.then_inc(*lb.bump("mm"))
                    hh["m"] = (bi, bt, lb.mark("mm"))
                lb.t(t_m)
                out[h2] = hh
            return out

        dep = emit_tr(nXb[m])
        xr_snap_done = dep
        yh = emit_prod(dep, True)
        s["trT_busy"] = lb.mark("mm")

        if not quintic:
            for h2 in range(2):
                def v_up(ve, h2=h2, s=s, m=m, a_c=a_c, b_c=b_c, yh=yh,
                         xr_snap_done=xr_snap_done):
                    bi, bt, mk = yh[h2]["m"]
                    lb.wait(ve, "mm", mk)
                    if h2 == 0:
                        lb.wait(ve, *xr_snap_done)
                    sl = slice(512 * h2, 512 * (h2 + 1))
                    lb.ss(ve, "evv", ve.tensor_scalar_mul(
                        nXr[m][:, sl], nXr[m][:, sl], a_c))
                    lb.ss(ve, "evv", ve.scalar_tensor_tensor(
                        nXr[m][:, sl], bt[:], b_c, nXr[m][:, sl],
                        ALU.mult, ALU.add))
                    ins = ve.tensor_copy(nXb[m][:, sl], nXr[m][:, sl])
                    ins.then_inc(*lb.bump("evv"))
                    banks.set_guard(bi, "evv", lb.mark("evv"))
                lb.v(v_up)
            s["xdep"] = ("evv", lb.mark("evv"))
            return

        for h2 in range(2):
            def v_ev(ve, h2=h2, m=m, s=s, yh=yh):
                bi, bt, mk = yh[h2]["m"]
                lb.wait(ve, "mm", mk)
                if h2 == 0 and s["yr_free"] is not None:
                    lb.wait(ve, "evv", s["yr_free"])
                ins = ve.tensor_copy(nYr[m][:, 512 * h2:512 * (h2 + 1)],
                                     bt[:])
                ins.then_inc(*lb.bump("evv"))
                banks.set_guard(bi, "evv", lb.mark("evv"))
            lb.v(v_ev)
        ydep = ("evv", lb.mark("evv"))

        dep = emit_tr(nYr[m], src_dep=ydep)
        th = emit_prod(dep, False)
        s["trT_busy"] = lb.mark("mm")

        for h2 in range(2):
            def v_up(ve, h2=h2, s=s, m=m, it=it, a_c=a_c, b_c=b_c, c_c=c_c,
                     th=th, xr_snap_done=xr_snap_done):
                bi, bt, mk = th[h2]["m"]
                lb.wait(ve, "mm", mk)
                if h2 == 0:
                    lb.wait(ve, *xr_snap_done)
                    if it == 0:
                        lb.wait(ve, "evv", s["scl_mk"])
                sl = slice(512 * h2, 512 * (h2 + 1))
                if it == 0:
                    lb.ss(ve, "evv", ve.tensor_scalar_mul(
                        nXr[m][:, sl], nXr[m][:, sl], nscl[m][:, 0:1]))
                    lb.ss(ve, "evv", ve.tensor_scalar_mul(
                        nYr[m][:, sl], nYr[m][:, sl], nscl[m][:, 1:2]))
                    lb.ss(ve, "evv", ve.tensor_add(
                        nXr[m][:, sl], nXr[m][:, sl], nYr[m][:, sl]))
                    lb.ss(ve, "evv", ve.tensor_scalar_mul(
                        nYr[m][:, sl], bt[:], nscl[m][:, 2:3]))
                    lb.ss(ve, "evv", ve.tensor_add(
                        nXr[m][:, sl], nXr[m][:, sl], nYr[m][:, sl]))
                    ins = ve.tensor_copy(nXb[m][:, sl], nXr[m][:, sl])
                else:
                    lb.ss(ve, "evv", ve.tensor_scalar_mul(
                        nXr[m][:, sl], nXr[m][:, sl], a_c))
                    lb.ss(ve, "evv", ve.scalar_tensor_tensor(
                        nXr[m][:, sl], nYr[m][:, sl], b_c, nXr[m][:, sl],
                        ALU.mult, ALU.add))
                    lb.ss(ve, "evv", ve.scalar_tensor_tensor(
                        nXr[m][:, sl], bt[:], c_c, nXr[m][:, sl],
                        ALU.mult, ALU.add))
                ins = ve.tensor_copy(nXb[m][:, sl], nXr[m][:, sl])
                ins.then_inc(*lb.bump("evv"))
                banks.set_guard(bi, "evv", lb.mark("evv"))
            lb.v(v_up)
        s["xdep"] = ("evv", lb.mark("evv"))
        s["yr_free"] = lb.mark("evv")

    trtf_busy = [None]

    def emit_ns_final(m, out_pT):
        s = nst[m]

        def emit_tr_f(src, s=s, m=m):
            for cc in range(KC):
                hh = {}

                def t_tp(te, cc=cc, hh=hh, src=src, s=s):
                    if cc == 0:
                        lb.wait(te, *s["xdep"])
                        if s["trT_busy"] is not None:
                            lb.wait(te, "mm", s["trT_busy"])
                        if trtf_busy[0] is not None:
                            lb.wait(te, "agd", trtf_busy[0])
                    bi, bt = banks.acquire(te)
                    te.transpose(bt[:, 0:128],
                                 src[:, 128 * cc:128 * (cc + 1)],
                                 idf[:, :]).then_inc(*lb.bump("mm"))
                    hh["m"] = (bi, bt, lb.mark("mm"))
                lb.t(t_tp)

                def v_tp(ve, cc=cc, hh=hh, m=m):
                    bi, bt, mk = hh["m"]
                    lb.wait(ve, "mm", mk)
                    ins = ve.tensor_copy(trTf[:, cc, :], bt[:, 0:128])
                    ins.then_inc(*lb.bump("evv"))
                    banks.set_guard(bi, "evv", lb.mark("evv"))
                lb.v(v_tp)
            return ("evv", lb.mark("evv"))

        dep = emit_tr_f(nXr[m])

        def g_ag(g, m=m, dep=dep):
            lb.wait(g, *dep)
            g.dma_start(nag_in[m][:],
                        trTf[:].rearrange("p c n -> p (c n)")
                        ).then_inc(*lb.bump("agd", 16))
            trtf_busy[0] = lb.mark("agd")
            lb.wait(g, "agd", lb.mark("agd"))
            g.collective_compute(
                "AllGather", ALU.bypass, RG,
                ins=[nag_in[m].opt()], outs=[nag_out[m].opt()]
            ).then_inc(*lb.bump("cc"))
            lb.wait(g, "cc", lb.mark("cc"))
            for r in range(8):
                g.dma_start(out_pT[:, :, 128 * r:128 * (r + 1)],
                            nag_out[m][128 * r:128 * (r + 1), :]
                            .rearrange("p (c n) -> p c n", n=128)
                            ).then_inc(*lb.bump("agd", 16))
        lb.g(g_ag)
        return ("agd", lb.mark("agd"))

    if DBG_NS:
        dbg_nsg_h = lb.out("dbg_nsg", [128, PAY])
        dbg_srow_h = lb.out("dbg_srow", [1, 8])
        dbg_scl_h = lb.out("dbg_scl", [128, 4])
        dbg_nsx_h = lb.out("dbg_nsx", [128, len(NS_SCHED), D])
        dbg_nsg2_h = lb.out("dbg_nsg2", [128, PAY])
        dbg_nsy_h = lb.out("dbg_nsy", [128, D])
        dbg_nst_h = lb.out("dbg_nst", [128, KC * 128])
        dbg_ari_h = lb.out("dbg_ari", [128, PAY])

    bg = io.get("bg_pump")

    def bgp(n):
        if bg is None:
            return

        def g_bg(g, n=n):
            bg(g, n)
        lb.g(g_bg)

    with lb.nc.named_scope("l0a_ns"):
        for m in range(2):
            emit_ns_h1(m, 0)
            bgp(3)
        for it in range(len(NS_SCHED)):
            for m in range(2):
                emit_ns_h2(m, it, NS_SCHED[it])
                bgp(3)
                if DBG_NS and m == 0:
                    def y_dbg(sy, it=it):
                        lb.wait(sy, *nst[0]["xdep"])
                        sy.dma_start(dbg_nsx_h[:, it, :], nXr[0][:, :]
                                     ).then_inc(*lb.bump("agd", 16))
                        if it == DBG_NS_IT:
                            sy.dma_start(dbg_ari_h[:], ar_in[0][:, :]
                                         ).then_inc(*lb.bump("agd", 16))
                            sy.dma_start(dbg_nsg2_h[:], Gb[0][:, :]
                                         ).then_inc(*lb.bump("agd", 16))
                            sy.dma_start(dbg_nsy_h[:], nYr[0][:, :]
                                         ).then_inc(*lb.bump("agd", 16))
                            sy.dma_start(dbg_nst_h[:],
                                         ntrT[0][:].rearrange("p c n -> p (c n)")
                                         ).then_inc(*lb.bump("agd", 16))
                        if it == 0:
                            sy.dma_start(dbg_nsg_h[:], Gb[0][:, :]
                                         ).then_inc(*lb.bump("agd", 16))
                            sy.dma_start(dbg_srow_h[:], srow[0:1, :]
                                         ).then_inc(*lb.bump("agd", 16))
                            sy.dma_start(dbg_scl_h[:], nscl[0][:, :]
                                         ).then_inc(*lb.bump("agd", 16))
                    lb.y(y_dbg)
                if it + 1 < len(NS_SCHED):
                    emit_ns_h1(m, it + 1)
                    bgp(3)
        p1_dep = emit_ns_final(0, p1t)
        p2_dep = emit_ns_final(1, p2t)

    for nm in ("idfr", "idf", "srow", "nsSq1", "nsSq0", "nsScl1", "nsScl0",
               "nsTf", "nsT1", "nsT0", "nsY1", "nsY0", "nsXb1", "nsXb0",
               "nsX1", "nsX0", "nsStg1", "nsStg0", "nsG1", "nsG0"):
        lb.free_sb(nm)

    zsq = lb.sb_scoped("zsq", [128, KC, 512], F32)
    resB = lb.sb_scoped("resB", [128, KC, 2050], BF16)
    elems = lb.sb_scoped("elems", [128, KC, 2047], BF16)
    resA = lb.sb_scoped("resA", [128, KC, 1026], BF16)
    e16g = lb.sb_scoped("e16g", [128, KC, 16], BF16)
    sc16 = lb.sb_scoped("sc16", [128, KC, 64], BF16)
    agx = lb.sb_scoped("agx", [128, 8, 8], F32)
    agx16 = lb.sb_scoped("agx16", [128, 8, 16], F32)
    slotrow = lb.sb_scoped("slotrow", [128, 8, 1], F32)
    slotrow2 = lb.sb_scoped("slotrow2", [128, 8, 1], F32)

    LVL_OFF = {}
    off = 0
    npc = 1024
    while npc >= 1:
        LVL_OFF[npc] = off
        off += npc
        npc //= 2

    def lvl_ap(npc):
        o = LVL_OFF[npc]
        return elems[:, :, o:o + npc]

    def emit_combine(ncols, aT_fn, bT_fn, out_fn, dep, sdep=None):
        NB = ceil_div(ncols, 512)
        last = None
        for cb in range(NB):
            c0 = 512 * cb
            c1 = min(ncols, c0 + 512)
            w = c1 - c0
            jm = {}
            for j in range(KC):
                hh = {}

                def t_c(te, j=j, hh=hh, c0=c0, c1=c1, w=w, cb=cb, dep=dep,
                        sdep=sdep):
                    if j == 0 and cb == 0:
                        lb.wait(te, *dep)
                        if sdep is not None:
                            lb.wait(te, *sdep)
                    bi, bt = banks.acquire(te)
                    for cc in range(KC):
                        te.matmul(bt[:, 0:w], p1t[:, cc, 128 * j:128 * (j + 1)],
                                  aT_fn(cc, c0, c1),
                                  start=(cc == 0), stop=False,
                                  skip_group_check=True)
                        ins = te.matmul(bt[:, 0:w],
                                        p2t[:, cc, 128 * j:128 * (j + 1)],
                                        bT_fn(cc, c0, c1),
                                        start=False, stop=(cc == KC - 1),
                                        skip_group_check=True)
                    ins.then_inc(*lb.bump("mm"))
                    hh["m"] = (bi, bt, lb.mark("mm"))
                lb.t(t_c)

                def s_c(se, j=j, hh=hh, w=w):
                    bi, bt, mk = hh["m"]
                    lb.wait(se, "mm", mk)
                    ins = se.activation(zsq[:, j, 0:w], bt[:, 0:w], AF.Square)
                    ins.then_inc(*lb.bump("sq"))
                    hh["sq"] = lb.mark("sq")
                lb.s(s_c)

                def v_c(ve, j=j, hh=hh, w=w, c0=c0, c1=c1):
                    bi, bt, mk = hh["m"]
                    lb.wait(ve, "mm", mk)
                    lb.wait(ve, "sq", hh["sq"])
                    ins = ve.tensor_copy(out_fn(j, c0, c1), bt[:, 0:w])
                    ins.then_inc(*lb.bump("evv"))
                    banks.set_guard(bi, "evv", lb.mark("evv"))
                lb.v(v_c)
                jm[j] = hh

            def t_st(te, jm=jm, w=w):
                if statguard[0] is not None:
                    lb.wait(te, "evv", statguard[0])
                for j in range(KC):
                    lb.wait(te, "sq", jm[j]["sq"])
                for j in range(KC):
                    ins = te.matmul(statbank[0:1, 0:w], onescol[:, :],
                                    zsq[:, j, 0:w],
                                    start=(j == 0), stop=(j == KC - 1))
                ins.then_inc(*lb.bump("mm"))
            lb.t(t_st)
            STM = lb.mark("mm")

            def v_st(ve, w=w):
                lb.wait(ve, "mm", STM)
                lb.ss(ve, "evv", ve.tensor_scalar_mul(
                    statrow[0:1, 0:w], statbank[0:1, 0:w], 1.0 / D))
                ins = ve.tensor_scalar_add(statrow[0:1, 0:w],
                                           statrow[0:1, 0:w], 1e-8)
                ins.then_inc(*lb.bump("evv"))
            lb.v(v_st)
            VST = lb.mark("evv")
            statguard[0] = VST

            def s_st(se, w=w):
                lb.wait(se, "evv", VST)
                ins = se.activation(statrow[0:1, 0:w], statrow[0:1, 0:w],
                                    AF.Sqrt)
                ins.then_inc(*lb.bump("sq"))
            lb.s(s_st)
            SST = lb.mark("sq")

            def v_rc(ve, w=w):
                lb.wait(ve, "sq", SST)
                ins = ve.reciprocal(statrow[0:1, 512:512 + w],
                                    statrow[0:1, 0:w])
                ins.then_inc(*lb.bump("evv"))
            lb.v(v_rc)
            VRC = lb.mark("evv")

            def t_bc(te, w=w):
                if bcguard[0] is not None:
                    lb.wait(te, "evv", bcguard[0])
                lb.wait(te, "evv", VRC)
                te.matmul(bcbank[:, 0:w], onesrow[:, :],
                          statrow[0:1, 512:512 + w],
                          start=True, stop=True).then_inc(*lb.bump("mm"))
            lb.t(t_bc)
            BCM = lb.mark("mm")

            def v_out(ve, c0=c0, c1=c1, w=w):
                lb.wait(ve, "mm", BCM)
                for j in range(KC):
                    o = out_fn(j, c0, c1)
                    ins = ve.tensor_mul(o, o, bcbank[:, 0:w])
                ins.then_inc(*lb.bump("evv"))
            lb.v(v_out)
            last = ("evv", lb.mark("evv"))
            bcguard[0] = last[1]
        return last

    dep = p2_dep

    dep = emit_combine(
        1024,
        lambda cc, c0, c1: q[:, cc, 2 * c0:2 * (c1 - 1) + 1:2],
        lambda cc, c0, c1: q[:, cc, 2 * c0 + 1:2 * (c1 - 1) + 2:2],
        lambda j, c0, c1: lvl_ap(1024)[:, j, c0:c1],
        dep)
    npc = 1024
    while npc >= 4:
        src_l = lvl_ap(npc)
        dst = lvl_ap(npc // 2)
        dep = emit_combine(
            npc // 2,
            lambda cc, c0, c1, src_l=src_l: src_l[:, cc, 2 * c0:2 * (c1 - 1) + 1:2],
            lambda cc, c0, c1, src_l=src_l: src_l[:, cc, 2 * c0 + 1:2 * (c1 - 1) + 2:2],
            lambda j, c0, c1, dst=dst: dst[:, j, c0:c1],
            dep)
        npc //= 2

    ag16 = emit_ag_full(lb, "e16", lvl_ap(2), 128, 16, agx16, dep)

    def v_e16(ve):
        lb.wait(ve, *ag16)
        for r in range(8):
            ins = ve.tensor_copy(
                e16g[:, :, 2 * r:2 * r + 2],
                agx16[:, r, :].rearrange("p (c l) -> p c l", l=2))
        ins.then_inc(*lb.bump("evv"))
    lb.v(v_e16)
    dep = ("evv", lb.mark("evv"))

    def sc(o, n):
        return sc16[:, :, o:o + n]

    def cp(dst_ap, src_ap, dep):
        def v_cp(ve, dep=dep):
            lb.wait(ve, *dep)
            ins = ve.tensor_copy(dst_ap, src_ap)
            ins.then_inc(*lb.bump("evv"))
        lb.v(v_cp)
        return ("evv", lb.mark("evv"))

    E16, L8, L4, L2, L1 = e16g, sc(0, 8), sc(8, 4), sc(12, 2), sc(14, 1)
    R2, R4, R8, R16 = sc(16, 2), sc(18, 4), sc(22, 8), sc(30, 16)
    dep = emit_combine(8, lambda cc, c0, c1: E16[:, cc, 2 * c0:2 * c1:2],
                       lambda cc, c0, c1: E16[:, cc, 2 * c0 + 1:2 * c1:2],
                       lambda j, c0, c1: L8[:, j, c0:c1], dep)
    dep = emit_combine(4, lambda cc, c0, c1: L8[:, cc, 2 * c0:2 * c1:2],
                       lambda cc, c0, c1: L8[:, cc, 2 * c0 + 1:2 * c1:2],
                       lambda j, c0, c1: L4[:, j, c0:c1], dep)
    dep = emit_combine(2, lambda cc, c0, c1: L4[:, cc, 2 * c0:2 * c1:2],
                       lambda cc, c0, c1: L4[:, cc, 2 * c0 + 1:2 * c1:2],
                       lambda j, c0, c1: L2[:, j, c0:c1], dep)
    dep = emit_combine(1, lambda cc, c0, c1: L2[:, cc, 0:1],
                       lambda cc, c0, c1: L2[:, cc, 1:2],
                       lambda j, c0, c1: L1[:, j, c0:c1], dep)
    dep = cp(R2[:, :, 0:1], L2[:, :, 0:1], dep)
    dep = cp(R2[:, :, 1:2], L1[:, :, 0:1], dep)
    dep = emit_combine(1, lambda cc, c0, c1: R2[:, cc, 0:1],
                       lambda cc, c0, c1: L4[:, cc, 2:3],
                       lambda j, c0, c1: R4[:, j, 2:3], dep)
    dep = cp(R4[:, :, 0:1], L4[:, :, 0:1], dep)
    dep = cp(R4[:, :, 1:2], R2[:, :, 0:1], dep)
    dep = cp(R4[:, :, 3:4], R2[:, :, 1:2], dep)
    dep = emit_combine(3, lambda cc, c0, c1: R4[:, cc, c0:c1],
                       lambda cc, c0, c1: L8[:, cc, 2 + 2 * c0:2 + 2 * (c1 - 1) + 1:2],
                       lambda j, c0, c1: R8[:, j, 2 + 2 * c0:2 + 2 * (c1 - 1) + 1:2],
                       dep)
    dep = cp(R8[:, :, 0:1], L8[:, :, 0:1], dep)
    dep = cp(R8[:, :, 1::2], R4[:, :, 0:4], dep)
    dep = emit_combine(7, lambda cc, c0, c1: R8[:, cc, c0:c1],
                       lambda cc, c0, c1: E16[:, cc, 2 + 2 * c0:2 + 2 * (c1 - 1) + 1:2],
                       lambda j, c0, c1: R16[:, j, 2 + 2 * c0:2 + 2 * (c1 - 1) + 1:2],
                       dep)
    dep = cp(R16[:, :, 0:1], E16[:, :, 0:1], dep)
    dep = cp(R16[:, :, 1::2], R8[:, :, 0:8], dep)

    def v_sel(ve, dep=dep):
        lb.wait(ve, *dep)
        for l in range(3):
            for cc in range(KC):
                lb.ss(ve, "evv", ve.tensor_mul(
                    agx[:, 0, 0:8].rearrange("p x -> p x") if False else
                    zsq[:, 0, 0:16], R16[:, cc, :], wsel[:, l, :]))
                lb.ss(ve, "evv", ve.reduce_sum(zsq[:, 1, 0:1],
                                               zsq[:, 0, 0:16], axis=AX.X))
                lb.ss(ve, "evv", ve.tensor_copy(resB[:, cc, l:l + 1],
                                                zsq[:, 1, 0:1]))
        ve.engine_nop().then_inc(*lb.bump("evv"))
    lb.v(v_sel)
    dep = ("evv", lb.mark("evv"))

    def emit_down(Ltot, elems_ap, ncols, half_buf, nc_half, out_buf, dep):
        if Ltot >= 64:
            def v_last(ve, dep=dep):
                lb.wait(ve, *dep)
                ins = ve.tensor_copy(slotrow[:],
                                     half_buf[:, :, nc_half:nc_half + 1])
                ins.then_inc(*lb.bump("evv"))
            lb.v(v_last)
            agdep = emit_ag_prefix(
                lb, f"dn{Ltot}", slotrow[:, :, 0], 128, 8, ohp,
                slotrow2[:, :, 0], ("evv", lb.mark("evv")), agx)

            def v_slot(ve, agdep=agdep):
                lb.wait(ve, *agdep)
                ins = ve.tensor_copy(half_buf[:, :, 0:1], slotrow2[:])
                ins.then_inc(*lb.bump("evv"))
            lb.v(v_slot)
            dep = ("evv", lb.mark("evv"))
        nE = (ncols + 1) // 2
        dep = emit_combine(
            nE,
            lambda cc, c0, c1: half_buf[:, cc, c0:c1],
            lambda cc, c0, c1: elems_ap[:, cc, 2 * c0:2 * (c1 - 1) + 1:2],
            lambda j, c0, c1: out_buf[:, j, 1 + 2 * c0:1 + 2 * (c1 - 1) + 1:2],
            dep)

        def v_fm(ve, dep=dep):
            lb.wait(ve, *dep)
            lb.ss(ve, "evv", ve.tensor_scalar_mul(
                out_buf[:, :, 1:2].rearrange("p c x -> p (c x)"),
                out_buf[:, :, 1:2].rearrange("p c x -> p (c x)"),
                fmk[:, 1:2]))
            lb.ss(ve, "evv", ve.scalar_tensor_tensor(
                out_buf[:, :, 1:2].rearrange("p c x -> p (c x)"),
                elems_ap[:, :, 0:1].rearrange("p c x -> p (c x)"),
                fmk[:, 0:1],
                out_buf[:, :, 1:2].rearrange("p c x -> p (c x)"),
                ALU.mult, ALU.add))
            ins = ve.tensor_copy(out_buf[:, :, 2:2 + 2 * nc_half - 1:2],
                                 half_buf[:, :, 1:1 + nc_half])
            ins.then_inc(*lb.bump("evv"))
        lb.v(v_fm)
        return ("evv", lb.mark("evv"))

    half, hb = resB, 1
    dep = emit_down(32, lvl_ap(4), 4, resB, 2, resA, dep)
    dep = emit_down(64, lvl_ap(8), 8, resA, 4, resB, dep)
    dep = emit_down(128, lvl_ap(16), 16, resB, 8, resA, dep)
    dep = emit_down(256, lvl_ap(32), 32, resA, 16, resB, dep)
    dep = emit_down(512, lvl_ap(64), 64, resB, 32, resA, dep)
    dep = emit_down(1024, lvl_ap(128), 128, resA, 64, resB, dep)
    dep = emit_down(2048, lvl_ap(256), 256, resB, 128, resA, dep)
    dep = emit_down(4096, lvl_ap(512), 512, resA, 256, resB, dep)
    dep = emit_down(8192, lvl_ap(1024), 1024, resB, 512, resA, dep)
    dep = emit_down(16385, q, 2049, resA, 1024, resB, dep)
    if DBG_L0ATTN:
        def y_dbgh(sy, dep=dep):
            lb.wait(sy, *dep)
            sy.dma_start(dbg_h_h[:], resB[:]).then_inc(*lb.bump("st", 16))
            sy.dma_start(dbg_q_h[:], q[:]).then_inc(*lb.bump("st", 16))
            sy.dma_start(dbg_p1_h[:], p1t[:]).then_inc(*lb.bump("st", 16))
        lb.y(y_dbgh)
        def t_dbg_guard(te, dep=dep):
            lb.wait(te, "st", lb.mark("st"))
        lb.t(t_dbg_guard)

    for nm in ("slotrow2", "slotrow", "agx16", "agx", "sc16", "e16g",
               "resA", "elems"):
        lb.free_sb(nm)
    vq = lb.sb_scoped("vq", [128, KC, TC], BF16)
    xch = lb.sb_scoped("xch", [128, KC, 512], BF16)
    NLN = 256
    resid = lb.sb_scoped("resid", [128, KC, NLN], F32)
    ytile = lb.sb_scoped("ytile", [128, KC, NLN], F32)
    yout = lb.sb_scoped("yout", [128, KC, NLN], F32)

    plan.extend([(rearr_pcn(wvT_h, 128 * j, 128), KC) for j in range(KC)]
                * 4)
    plan.extend([(rearr_pcn(wcT_h, 128 * j, 128), KC) for j in range(KC)]
                * (TC // NLN))
    wv_base = KC

    vdep = dep
    xch_free = {}
    for f in range(4):
        def g_xch(g, f=f, vdep=vdep):
            if f > 0:
                lb.wait(g, "mm", xch_free[f - 1])
            else:
                lb.wait(g, *vdep)
            g.dma_start(xch[:], rearr_pct(xq_h, 1 + 512 * f, 512)
                        ).then_inc(*lb.bump("xld", 16))
        lb.g(g_xch)
        xld_m = lb.mark("xld")
        pump(3 if f == 0 else 0)
        for j in range(KC):
            hh = {}

            def t_v(te, f=f, j=j, hh=hh, xld_m=xld_m):
                if j == 0:
                    lb.wait(te, "xld", xld_m)
                w = wbuf.consume(te, wv_base + 8 * f + j)
                bi, bt = banks.acquire(te)
                for cc in range(KC):
                    ins = te.matmul(bt[:], w[:, cc, :], xch[:, cc, :],
                                    start=(cc == 0), stop=(cc == KC - 1))
                ins.then_inc(*lb.bump("mm"))
                hh["m"] = (bi, bt, lb.mark("mm"))
                wbuf.release_at(wv_base + 8 * f + j, "mm", lb.mark("mm"))
                if j == KC - 1:
                    xch_free[f] = lb.mark("mm")
            lb.t(t_v)

            def v_v(ve, f=f, j=j, hh=hh):
                bi, bt, mk = hh["m"]
                lb.wait(ve, "mm", mk)
                ins = ve.tensor_mul(vq[:, j, 512 * f:512 * (f + 1)],
                                    bt[:], resB[:, j, 2 + 512 * f:2 + 512 * (f + 1)])
                ins.then_inc(*lb.bump("evv"))
                banks.set_guard(bi, "evv", lb.mark("evv"))
            lb.v(v_v)
            pump(1)
    YV = ("evv", lb.mark("evv"))

    aux = {"onescol": onescol, "onesrow": onesrow, "zsq": zsq[:, :, 0:NLN],
           "statrow": statrow}
    wc_base = wv_base + 32
    for sti in range(TC // NLN):
        ts = slice(NLN * sti, NLN * (sti + 1))

        def y_resid(sy, sti=sti):
            lb.wait(sy, "evv", lb.mark("evv"))
            sy.dma_start(resid[:], rearr_pct(xq_h, 1 + NLN * sti, NLN)
                         ).then_inc(*lb.bump("rld", 16))
        lb.y(y_resid)
        rld_mark = lb.mark("rld")

        for j in range(KC):
            eh = {}

            def t_y(te, sti=sti, j=j, eh=eh, ts=ts):
                if sti == 0 and j == 0:
                    lb.wait(te, *YV)
                w = wbuf.consume(te, wc_base + 8 * sti + j)
                bi, bt = banks.acquire(te)
                for cc in range(KC):
                    ins = te.matmul(bt[:, 0:NLN], w[:, cc, :], vq[:, cc, ts],
                                    start=(cc == 0), stop=(cc == KC - 1))
                ins.then_inc(*lb.bump("mm"))
                eh["m"] = lb.mark("mm")
                eh["b"] = (bi, bt)
                wbuf.release_at(wc_base + 8 * sti + j, "mm", lb.mark("mm"))
            lb.t(t_y)

            def v_y(ve, sti=sti, j=j, eh=eh, rld_mark=rld_mark):
                if j == 0:
                    lb.wait(ve, "rld", rld_mark)
                lb.wait(ve, "mm", eh["m"])
                bi, bt = eh["b"]
                ins = ve.tensor_add(ytile[:, j, :], bt[:, 0:NLN],
                                    resid[:, j, :])
                ins.then_inc(*lb.bump("evv"))
                banks.set_guard(bi, "evv", lb.mark("evv"))
            lb.v(v_y)
            pump(1)
        z_dep = ("evv", lb.mark("evv"))

        ln_dep = emit_ln(lb, banks, aux, ytile, yout, z_dep, nt=NLN,
                         out_extra_deps=[("st", lb.mark("st"))])

        def y_store(sy, sti=sti, ln_dep=ln_dep):
            lb.wait(sy, *ln_dep)
            sy.dma_start(rearr_pct(xout_h, NLN * sti, NLN), yout[:]
                         ).then_inc(*lb.bump("st", 16))
        lb.y(y_store)

    return (lb.qual("st"), lb.mark("st"))


def host_l0attn_inputs(inputs, xT_full):
    wqT = np.ascontiguousarray(inputs["L0_attn_Wq"]).astype(np.float32)
    wvT = np.ascontiguousarray(inputs["L0_attn_Wv"]).astype(np.float32)
    wcT = np.ascontiguousarray(inputs["L0_attn_Wc"]).astype(np.float32)
    wqT = np.ascontiguousarray(inputs["L0_attn_Wq"].T).astype(np.float32)
    wvT = np.ascontiguousarray(inputs["L0_attn_Wv"].T).astype(np.float32)
    wcT = np.ascontiguousarray(inputs["L0_attn_Wc"].T).astype(np.float32)
    idn = inputs["L0_attn_identity"].astype(np.float32)
    idncol = np.ascontiguousarray(idn.reshape(8, 128).T)
    wp1 = inputs["L0_attn_Wp1"].astype(np.float32)
    wp2 = inputs["L0_attn_Wp2"].astype(np.float32)
    common = dict(WqT=wqT, WvT=wvT, WcT=wcT, identcol=idncol,
                  Wp1=wp1, Wp2=wp2,
                  ident128f=np.eye(128, dtype=np.float32),
                  onescol=np.ones((128, 1), np.float32),
                  onesrow=np.ones((1, 128), np.float32))
    maps = []
    for c in range(NCORES):
        xq = np.zeros((D, TC + 1), np.float32)
        lo = TC * c - 1
        if c == 0:
            xq[:, 1:] = xT_full[:, 0:TC]
        else:
            xq[:, :] = xT_full[:, lo:lo + TC + 1]
        fm = np.zeros((128, 2), np.float32)
        fm[:, 0] = 1.0 if c == 0 else 0.0
        fm[:, 1] = 0.0 if c == 0 else 1.0
        ohp = np.zeros((128, 8), np.float32)
        if c > 0:
            ohp[:, c - 1] = 1.0
        wsel = np.zeros((128, 3, 16), np.float32)
        for l in range(3):
            g = 2 * c - 1 + l
            if 0 <= g < 16:
                wsel[:, l, g] = 1.0
        maps.append(dict(common, xqT=np.ascontiguousarray(xq),
                         fmask=fm, onehotprev=ohp, wsel=wsel,
                         Wp1r=np.ascontiguousarray(wp1[128 * c:128 * (c + 1)]),
                         Wp2r=np.ascontiguousarray(wp2[128 * c:128 * (c + 1)])))
    return maps


def build_fused():
    lb = LB()
    xq_h = lb.inp("xqT", [D, TC + 1])
    ocol_h = lb.inp("onescol", [128, 1])
    orow_h = lb.inp("onesrow", [1, 128])
    xout_h = lb.out("xoutT", [D, TC])
    x1T = lb.dram("x1T", [D, TC])
    x2T = lb.dram("x2T", [D, TC])
    x3T = lb.dram("x3T", [D, TC])

    SEMS = ["mm", "evv", "evs", "sq", "xld", "xfld", "cld", "rld", "st",
            "agd", "cc", "wk_s0", "wk_s1", "wk_s2", "wk_s3",
            "wkt_s0", "wkt_s1", "wkt_s2"]
    lb.sem("shared")
    for pfx in ("b0_", "b1_", "b2_", "b3_"):
        for nm in SEMS:
            lb.sem("@" + pfx + nm)
    lb.start_block([], allow_lp=True)
    onescol = lb.sb("sh_onescol", [128, 1], F32)
    onesrow = lb.sb("sh_onesrow", [1, 128], F32)
    id128sh = lb.sb("sh_id128", [128, 128], F32)

    def y_shc(sy):
        sy.dma_start(onescol[:, :], ocol_h[:]).then_inc(*lb.bump("@shared", 16))
        sy.dma_start(onesrow[:, :], orow_h[:]).then_inc(*lb.bump("@shared", 16))
    lb.y(y_shc)
    shc = ("@shared", lb.mark("@shared"))

    def gate(dep):
        for emit in (lb.t, lb.v, lb.s, lb.g, lb.y):
            def g_(e, dep=dep):
                lb.wait(e, *dep)
            emit(g_)

    wc1_d = lb.dram("wc1cache", [136, 128, 4096], BF16)
    wc3_d = lb.dram("wc3cache", [128, 128, 4096], BF16)

    lb.sem_prefix = "b0_"
    lb.push_zone()
    io0 = {"xq": xq_h, "xout": x1T, "onescol": onescol, "onesrow": onesrow,
           "shc_dep": shc}
    with lb.nc.named_scope("b0_l0attn"):
        dep = emit_l0attn(lb, io0)
    lb.pop_zone()

    lb.sem_prefix = "b1_"
    gate(dep)
    lb.push_zone()
    io1 = {"xin": x1T, "xout": x2T, "onescol": onescol, "onesrow": onesrow,
           "shc_dep": dep, "stream_dep": None}
    if WCACHE1:
        io1["wc1"] = wc1_d
    with lb.nc.named_scope("b1_l0ffn"):
        dep = emit_l0ffn(lb, io1)
    lb.pop_zone()

    lb.sem_prefix = "b2_"
    gate(dep)
    lb.push_zone()
    io2 = {"xin": x2T, "xout": x3T, "onescol": onescol, "onesrow": onesrow,
           "shc_dep": dep, "stream_dep": None}
    with lb.nc.named_scope("b2_l1attn"):
        dep = emit_l1attn(lb, io2)
    lb.pop_zone()

    lb.sem_prefix = "b3_"
    gate(dep)

    def g_id(g):
        g.dma_start(id128sh[:, :],
                    lb.nc.dram_tensor("ident128", [128, 128], F32,
                                      kind="ExternalInput").ap()[:]
                    ).then_inc(*lb.bump("@shared", 16))
    lb.g(g_id)
    shc3 = ("@shared", lb.mark("@shared"))
    lb.push_zone()
    io3 = {"xin": x3T, "xout": xout_h, "onescol": onescol,
           "onesrow": onesrow, "id128": id128sh, "shc_dep": shc3,
           "stream_dep": dep}
    if WCACHE3:
        io3["wc3"] = wc3_d
    with lb.nc.named_scope("b3_l1ffn"):
        dep = emit_l1ffn(lb, io3)
    lb.pop_zone()
    lb.sem_prefix = ""

    def y_final(sy):
        lb.wait(sy, *dep)
    lb.y(y_final)
    return lb.done()


def host_fused_inputs(inputs, xT_full):
    maps0 = host_l0attn_inputs(inputs, xT_full)
    maps1 = host_l0ffn_inputs(inputs, None)
    maps2 = host_l1attn_inputs(inputs, None)
    maps3 = host_l1ffn_inputs(inputs, None)
    out = []
    for c in range(NCORES):
        m = {}
        m.update(maps0[c])
        m.update(maps1[c])
        m.update(maps2[c])
        m.update(maps3[c])
        out.append(m)
    return out


FUSED = True

_BUILT = {}


def _get(name, builder):
    if name not in _BUILT:
        _BUILT[name] = builder()
    return _BUILT[name]


def kernel(**inputs):
    x = np.ascontiguousarray(inputs["L0_attn_x"][0]).astype(np.float32)
    xT = np.ascontiguousarray(x.T)

    if FUSED:
        ncf = _get("fused", build_fused)
        resf = run_bass_kernel_spmd(ncf, host_fused_inputs(inputs, xT),
                                    core_ids=list(range(NCORES)))
        out = np.concatenate([resf.results[c]["xoutT"]
                              for c in range(NCORES)], axis=1).T
        return np.ascontiguousarray(out)[None].astype(np.float32)

    nc0 = _get("l0attn", build_l0attn)
    res0 = run_bass_kernel_spmd(nc0, host_l0attn_inputs(inputs, xT),
                                core_ids=list(range(NCORES)))
    s1 = [np.ascontiguousarray(res0.results[c]["xoutT"])
          for c in range(NCORES)]

    nc1 = _get("l0ffn", build_l0ffn)
    res1 = run_bass_kernel_spmd(nc1, host_l0ffn_inputs(inputs, s1),
                                core_ids=list(range(NCORES)))
    s2 = [np.ascontiguousarray(res1.results[c]["xoutT"])
          for c in range(NCORES)]

    nc2 = _get("l1attn", build_l1attn)
    res2 = run_bass_kernel_spmd(nc2, host_l1attn_inputs(inputs, s2),
                                core_ids=list(range(NCORES)))
    s3 = [np.ascontiguousarray(res2.results[c]["xoutT"])
          for c in range(NCORES)]

    nc3 = _get("l1ffn", build_l1ffn)
    res3 = run_bass_kernel_spmd(nc3, host_l1ffn_inputs(inputs, s3),
                                core_ids=list(range(NCORES)))
    out = np.concatenate([res3.results[c]["xoutT"]
                          for c in range(NCORES)], axis=1).T
    return np.ascontiguousarray(out)[None].astype(np.float32)
